# version 09
# baseline (speedup 1.0000x reference)
import math
from contextlib import ExitStack

import numpy as np
import ml_dtypes

import concourse.bass as bass
import concourse.mybir as mybir
from concourse.bass_utils import run_bass_kernel_spmd

F32 = mybir.dt.float32
BF16 = mybir.dt.bfloat16
AF = mybir.ActivationFunctionType
ALU = mybir.AluOpType
AX = mybir.AxisListType
ds = bass.ds

NCORES = 8
B, T, D = 4, 4096, 1024
TH = T // 2
NT = TH // 128
H, HD = 16, 64
HPC = 8
FEAT = 16
NQUAD = 136

BF = ml_dtypes.bfloat16


def _bf(x):
    return np.ascontiguousarray(np.asarray(x).astype(BF))


def _f32(x):
    return np.ascontiguousarray(np.asarray(x).astype(np.float32))


ENGINES = ("sync", "gpsimd", "tensor", "scalar", "vector")


class Plan:
    def __init__(self):
        self.items = {e: [] for e in ENGINES}
        self.cnt = {e: 0 for e in ENGINES}

    def op(self, eng, fn, dep=None, inc=None, amount=None):
        if amount is None:
            amount = 16 if eng in ("sync",) else 1
        if dep is None:
            dep = eng not in ("tensor",)
        if amount == 16 and eng == "gpsimd":
            dep = True
        if inc is None:
            inc = eng not in ("tensor",)
        if not inc:
            self.items[eng].append(("op", fn, dep, False, 0))
            return (eng, self.cnt[eng])
        self.cnt[eng] += amount
        self.items[eng].append(("op", fn, dep, True, amount))
        return (eng, self.cnt[eng])

    def dma(self, fn, eng="sync", dep=False):
        return self.op(eng, fn, dep=True, inc=True, amount=16)

    def wait(self, eng, token):
        if token is None:
            return
        teng, tval = token
        if tval <= 0:
            return
        self.items[eng].append(("wait", teng, tval))

    def tok(self, eng):
        return (eng, self.cnt[eng])

    def custom(self, eng, fn, sem):
        self.items[eng].append(("cop", fn, sem))
        return (sem, 1)

    def barrier(self):
        toks = {e: self.tok(e) for e in ENGINES}
        for e in ENGINES:
            for o in ENGINES:
                if o != e and toks[o][1] > 0:
                    self.wait(e, toks[o])

    def emit(self, nc, ctx):
        sems = {e: ctx.enter_context(nc.semaphore("S" + e)) for e in ENGINES}
        csems = {}
        for e in ENGINES:
            for it in self.items[e]:
                if it[0] == "cop" and it[2] not in csems:
                    csems[it[2]] = ctx.enter_context(nc.semaphore(it[2]))
        self.csems = csems
        block = ctx.enter_context(nc.Block())

        def mk_body(ename):
            items = self.items[ename]

            def body(eng):
                done = 0
                for it in items:
                    if it[0] == "wait":
                        _, teng, tval = it
                        eng.wait_ge(sems.get(teng) or csems[teng], tval)
                    elif it[0] == "cop":
                        _, fn, semn = it
                        inst = fn(eng)
                        inst.then_inc(csems[semn], 1)
                    else:
                        _, fn, dep, inc, amount = it
                        if dep and done > 0:
                            eng.wait_ge(sems[ename], done)
                        inst = fn(eng)
                        if inc:
                            inst.then_inc(sems[ename], amount)
                            done += amount
            return body

        for ename in ENGINES:
            if not self.items[ename]:
                continue
            getattr(block, ename)(mk_body(ename))


def emit_a1(nc, p, ctx, io):
    xt = io["x"].rearrange("(n p) d -> n p d", p=128)
    lnt = io["ln0"].rearrange("(n p) d -> n p d", p=128)
    hfr = io["hf"].rearrange("(kc p) t -> p kc t", p=128)
    ident = io["ident"]
    with ExitStack() as c2:
        sb = lambda n, s, dt=F32: c2.enter_context(nc.sbuf_tensor(n, s, dt))
        sx = sb("sxA1", [128, D])
        sqd = sb("sqdA1", [128, D])
        scr = sb("scrA1", [128, 16])
        sln = sb("slnA1", [128, D])
        sh = sb("shA1", [128, D], BF16)
        s_id = sb("s_idA1", [128, 128], BF16)
        hf = sb("hfA1", [128, 8 * TH], BF16)
        PT = c2.enter_context(nc.psum_tensor("PTa1", [128, 128], BF16))
        tid = p.dma(lambda e: e.dma_start(out=s_id[:], in_=ident))
        p.wait("tensor", tid)
        t_sh = None
        t_st = None
        for i in range(NT):
            p.wait("sync", t_sh)
            tl = p.dma(lambda e, i=i: e.dma_start(out=sx[:], in_=xt[i]), dep=True)
            tb = plan_ln_scalars(p, sx[:], D, scr, sqd[:], 1e-5, dep_tok=tl)
            p.op("vector", lambda e: e.tensor_tensor(scr[:, 9:10], scr[:, 4:5], scr[:, 5:6], ALU.mult))
            p.op("vector", lambda e: e.tensor_scalar_add(scr[:, 9:10], scr[:, 9:10], 1e-6))
            tv = p.op("vector", lambda e: e.reciprocal(scr[:, 9:10], scr[:, 9:10]))
            p.wait("scalar", tv)
            ts = p.op("scalar", lambda e: e.activation(scr[:, 9:10], scr[:, 9:10], AF.Sqrt))
            p.wait("vector", ts)
            p.op("vector", lambda e: e.tensor_tensor(scr[:, 10:11], scr[:, 6:7], scr[:, 9:10], ALU.mult))
            tg = p.op("vector", lambda e: e.scalar_tensor_tensor(
                scr[:, 11:12], scr[:, 2:3], -1.0, scr[:, 10:11], ALU.mult, ALU.mult))
            p.wait("scalar", tb)
            p.wait("scalar", t_st)
            ta = p.op("scalar", lambda e: e.activation(
                sln[:], sx[:], AF.Identity, bias=scr[:, 7:8], scale=scr[:, 6:7]), dep=True)
            p.wait("scalar", tg)
            t_sh = p.op("scalar", lambda e: e.activation(
                sh[:], sx[:], AF.Identity, bias=scr[:, 11:12], scale=scr[:, 10:11]), dep=True)
            p.wait("sync", ta)
            if t_st is not None:
                p.wait("sync", t_st)
            t_st = p.dma(lambda e, i=i: e.dma_start(out=lnt[i], in_=sln[:]), dep=True)
            p.wait("tensor", t_sh)
            for kc in range(8):
                tt = p.op("tensor", lambda e, kc=kc: e.transpose(
                    PT[:], sh[:, kc * 128:(kc + 1) * 128], s_id[:]), inc=True)
                p.wait("scalar", tt)
                tcp = p.op("scalar", lambda e, kc=kc, i=i: e.copy(
                    hf[:, kc * TH + i * 128:kc * TH + (i + 1) * 128], PT[:]), dep=True)
                p.wait("tensor", tcp)
            t_sh = tcp
        p.wait("sync", t_sh)
        t_h = None
        for kc in range(8):
            t_h = p.dma(lambda e, kc=kc: e.dma_start(
                out=hfr[:, kc, :], in_=hf[:, kc * TH:(kc + 1) * TH]), dep=False)
    return t_h


def build_a1():
    nc = bass.Bass("TRN2", target_bir_lowering=False, debug=False)
    io = dict(
        x=nc.dram_tensor("x", [TH, D], F32, kind="ExternalInput").ap(),
        ident=nc.dram_tensor("ident", [128, 128], BF16, kind="ExternalInput").ap(),
        ln0=nc.dram_tensor("o_ln", [TH, D], F32, kind="ExternalOutput").ap(),
        hf=nc.dram_tensor("o_hf", [D, TH], BF16, kind="ExternalOutput").ap(),
    )
    with ExitStack() as ctx:
        p = Plan()
        emit_a1(nc, p, ctx, io)
        p.wait("sync", p.tok("sync"))
        p.emit(nc, ctx)
    return nc


_CACHE = {}


def _get(name, builder):
    if name not in _CACHE:
        _CACHE[name] = builder()
    return _CACHE[name]


def stage_a1(x_shards):
    nc = _get("a1", build_a1)
    idb = np.eye(128, dtype=np.float32)
    res = run_bass_kernel_spmd(nc, [{"x": s, "ident": _bf(idb)} for s in x_shards],
                               core_ids=list(range(NCORES)))
    return ([r["o_ln"] for r in res.results], [r["o_hf"] for r in res.results])


def shard_tokens(x):
    return [np.ascontiguousarray(x[c // 2, (c % 2) * TH:(c % 2 + 1) * TH])
            for c in range(NCORES)]


def unshard_tokens(shards):
    out = np.empty((B, T, shards[0].shape[-1]), dtype=np.asarray(shards[0]).dtype)
    for c in range(NCORES):
        out[c // 2, (c % 2) * TH:(c % 2 + 1) * TH] = shards[c]
    return out



A2_NHEADS = 8
A2_PARTS = ("scan", "win")
A2_PHASES = ("fold", "proj", "cb")
A2_TT = 8
A2_QK = True
A2_V = True


def emit_a2(nc, p, ctx, io):
    wq_d = io["wq"]; wk_d = io["wk"]; wv_d = io["wv"]; nw_d = io["nw"]
    wqf_d = io["wqf2"]; wkf_d = io["wkf2"]; siu_d = io["s_iu"]; sju_d = io["s_ju"]
    tri_d = io["tri01"]; winm_d = io["winm"]; winm0_d = io["winm0"]
    h_ld = io["h_ld"]; lin_dst = io["lin_dst"]; win_dst = io["win_dst"]

    NC2 = T // 256
    if True:
        sb = lambda n, s, dt=BF16: ctx.enter_context(nc.sbuf_tensor(n, s, dt))
        ps = lambda n, s: ctx.enter_context(nc.psum_tensor(n, s, F32))
        s_wqf = sb("s_wqf", [128, FEAT]); s_wkf = sb("s_wkf", [128, FEAT])
        s_iu = sb("ss_iu", [64, NQUAD]); s_ju = sb("ss_ju", [64, NQUAD])
        s_tri = sb("s_tri", [128, 128]); s_wm = sb("s_wm", [128, 256]); s_wm0 = sb("s_wm0", [128, 256])
        qfm = sb("qfm", [128, 4 * T]); kfm = sb("kfm", [128, 4 * T])
        vsb = sb("vsb", [128, 32 * 520])
        vstg = sb("vstg", [128, 512])
        ones_b = sb("ones_b", [128, 128])
        onesf = ctx.enter_context(nc.sbuf_tensor("onesf", [128, 128], F32))
        md = ctx.enter_context(nc.sbuf_tensor("md", [128, 16], F32))
        cbuf = ctx.enter_context(nc.sbuf_tensor("cbuf", [1, 32], F32))
        negc = ctx.enter_context(nc.sbuf_tensor("negc", [128, 8], F32))
        rz = ctx.enter_context(nc.sbuf_tensor("rz", [128, 4], F32))

        ctx2 = ExitStack()
        sb2 = lambda n, s, dt=BF16: ctx2.enter_context(nc.sbuf_tensor(n, s, dt))
        s_h = sb2("s_h", [128, 8 * T])
        s_wq = sb2("s_wq", [128, 8 * 512]); s_wk = sb2("s_wk", [128, 8 * 512]); s_wv = sb2("s_wv", [128, 8 * 512])
        s_nw = ctx2.enter_context(nc.sbuf_tensor("s_nw", [128, 8], F32))

        PX = ps("PX", [128, 512]); PS = ps("PS", [128, 512])
        PG = ps("PG", [128, 512]); PH = ps("PH", [128, 512])
        PY = ps("PY", [128, 512]); PKV = ps("PKV", [128, 260])
        PB2 = ps("PB2", [128, 512])

        for kc in range(8):
            for r in range(2):
                t_h = p.dma(lambda e, kc=kc, r=r: e.dma_start(
                    out=s_h[:, kc * T + r * TH:kc * T + (r + 1) * TH], in_=h_ld(kc, r)))
        t_w = []
        for w_sb, w_d in ((s_wq, wq_d), (s_wk, wk_d), (s_wv, wv_d)):
            w_r = w_d.rearrange("(kc p) c -> p kc c", p=128)
            for kc in range(8):
                t_w.append(p.op("gpsimd", lambda e, a=w_sb, w_r=w_r, kc=kc: e.dma_start(
                    out=a[:, kc * 512:(kc + 1) * 512], in_=w_r[:, kc, :]), dep=False, amount=16))
        smalls = [(s_nw, nw_d), (s_wqf, wqf_d), (s_wkf, wkf_d), (s_iu, siu_d),
                  (s_ju, sju_d), (s_tri, tri_d), (s_wm, winm_d), (s_wm0, winm0_d)]
        t_small = None
        for a, b in smalls:
            t_small = p.dma(lambda e, a=a, b=b: e.dma_start(out=a[:], in_=b))
        t_ones = p.op("vector", lambda e: e.memset(ones_b[:], 1.0), dep=False)
        t_onesf = p.op("vector", lambda e: e.memset(onesf[:], 1.0), dep=False)
        t_vinit = p.op("vector", lambda e: e.memset(vsb[:], 1.0), dep=False)
        p.wait("vector", t_w[-1])
        p.wait("vector", t_small)
        t_fold = None
        for w_sb in ((s_wq, s_wk, s_wv) if "fold" in A2_PHASES else ()):
            for kc in range(8):
                t_fold = p.op("vector", lambda e, w=w_sb, kc=kc: e.tensor_scalar_mul(
                    w[:, kc * 512:(kc + 1) * 512], w[:, kc * 512:(kc + 1) * 512],
                    s_nw[:, kc:kc + 1]), dep=False)
        p.wait("tensor", t_h)
        p.wait("tensor", t_fold)
        p.wait("tensor", t_vinit)
        pq = [PX, PS, PG, PH]
        ev_tok = [None, None, None, None]
        pi = 0
        for tt in range(A2_TT if "proj" in A2_PHASES else 0):
            for hp in range(4 if A2_QK else 0):
                for which, wsb, dst in (("q", s_wq, qfm), ("k", s_wk, kfm)):
                    pb = pq[pi % 4]
                    if ev_tok[pi % 4] is not None:
                        p.wait("tensor", ev_tok[pi % 4])
                    for kc in range(8):
                        tmm = p.op("tensor", lambda e, pb=pb, wsb=wsb, kc=kc, hp=hp, tt=tt: e.matmul(
                            pb[:], wsb[:, kc * 512 + hp * 128:kc * 512 + (hp + 1) * 128],
                            s_h[:, kc * T + tt * 512:kc * T + (tt + 1) * 512],
                            start=(kc == 0), stop=(kc == 7)), inc=(kc == 7))
                    p.wait("scalar", tmm)
                    ev_tok[pi % 4] = p.op("scalar", lambda e, pb=pb, dst=dst, hp=hp, tt=tt: e.copy(
                        dst[:, hp * T + tt * 512:hp * T + (tt + 1) * 512], pb[:]), dep=False)
                    pi += 1
            for t4 in range(4 if A2_V else 0):
                ck = tt * 4 + t4
                pb = pq[pi % 4]
                if ev_tok[pi % 4] is not None:
                    p.wait("tensor", ev_tok[pi % 4])
                for kc in range(8):
                    tmm = p.op("tensor", lambda e, pb=pb, kc=kc, ck=ck: e.matmul(
                        pb[:], s_h[:, kc * T + ck * 128:kc * T + (ck + 1) * 128],
                        s_wv[:, kc * 512:(kc + 1) * 512],
                        start=(kc == 0), stop=(kc == 7)), inc=(kc == 7))
                p.wait("scalar", tmm)
                tstg = p.op("scalar", lambda e, pb=pb: e.copy(vstg[:], pb[:]), dep=True)
                p.wait("vector", tstg)
                vview = vsb[:].rearrange("p (c h ee) -> p c h ee", c=32, h=8)
                tvs = p.op("vector", lambda e, ck=ck, vv=vview: e.tensor_copy(
                    vv[:, ck, :, 0:64], vstg[:].rearrange("p (h ee) -> p h ee", h=8)), dep=False)
                ev_tok[pi % 4] = tstg
                p.wait("scalar", tvs)
                pi += 1
        t_proj_pe = p.tok("tensor")
        t_proj_sc = p.tok("scalar")
        t_proj_ve = p.tok("vector")
        ctx2.close()
        qf_sb = sb("qf_sb", [16, 2 * T]); kf_sb = sb("kf_sb", [16, 2 * T])
        phik = sb("phik", [128, 32 * NQUAD]); kp_sb = sb("kp_sb", [128, 32 * 16])
        Asb = sb("Asb", [128, 384])
        Q2qa = sb("Q2qa", [128, 256]); Q2qb = sb("Q2qb", [8, 256])
        Gsb = sb("Gsb", [128, 256])
        KVsb = sb("KVsb", [128, 260])
        ptw = sb("ptw", [128, 256])
        lin_sbs = [sb("lin_sb0", [64, T]), sb("lin_sb1", [64, T])]
        win_sbs = [sb("win_sb0", [64, T]), sb("win_sb1", [64, T])]
        rzf = ctx.enter_context(nc.sbuf_tensor("rzfA", [128, 512], F32))
        bsbA = sb("bsbA", [65, 512])
        p.wait("vector", t_proj_sc)
        t_md = None
        for cg in range(4 if "cb" in A2_PHASES else 0):
            t_md = p.op("vector", lambda e, cg=cg: e.reduce_max(
                md[:, cg:cg + 1], qfm[:, cg * T:(cg + 1) * T], axis=AX.X,
                apply_absolute_value=True), dep=False)
            t_md = p.op("vector", lambda e, cg=cg: e.reduce_max(
                md[:, 4 + cg:5 + cg], kfm[:, cg * T:(cg + 1) * T], axis=AX.X,
                apply_absolute_value=True), dep=False)
        p.wait("scalar", t_md)
        if "cb" not in A2_PHASES:
            p.wait("sync", p.tok("sync"))
            p.emit(nc, ctx)
            return nc
        t_md2 = p.op("scalar", lambda e: e.activation(md[:, 8:16], md[:, 0:8], AF.Square))
        p.wait("tensor", t_md2)
        p.wait("tensor", t_onesf)
        for h in range(8):
            cg, r = h // 2, 64 * (h % 2)
            p.op("tensor", lambda e, cg=cg, r=r, h=h: e.matmul(
                PY[0:1, h:h + 1], md[r:r + 64, 8 + cg:9 + cg], onesf[r:r + 64, 0:1],
                start=True, stop=True))
            t_cmm = p.op("tensor", lambda e, cg=cg, r=r, h=h: e.matmul(
                PY[0:1, 8 + h:9 + h], md[r:r + 64, 12 + cg:13 + cg], onesf[r:r + 64, 0:1],
                start=True, stop=True), inc=(h == 7))
        p.wait("vector", t_cmm)
        p.op("vector", lambda e: e.tensor_copy(cbuf[0:1, 0:16], PY[0:1, 0:16]))
        p.wait("scalar", p.tok("vector"))
        t_sq = p.op("scalar", lambda e: e.activation(cbuf[0:1, 16:32], cbuf[0:1, 0:16], AF.Sqrt))
        p.wait("vector", t_sq)
        p.op("vector", lambda e: e.tensor_tensor(
            cbuf[0:1, 0:8], cbuf[0:1, 16:24], cbuf[0:1, 24:32], ALU.mult))
        tnb = p.op("vector", lambda e: e.tensor_scalar(
            cbuf[0:1, 8:16], cbuf[0:1, 0:8], -0.125, 60.0, ALU.mult, ALU.add))
        p.wait("tensor", tnb)
        t_nc = p.op("tensor", lambda e: e.matmul(PY[:, 16:24], onesf[0:1, 0:128],
                                          cbuf[0:1, 8:16], start=True, stop=True), inc=True)
        p.wait("vector", t_nc)
        t_negc = p.op("vector", lambda e: e.tensor_copy(negc[:], PY[:, 16:24]))
        t_out_prev = None
        t_y_done = None
        t_lin_dma = None
        t_win_dma = None

        def emit_head(hq, t_y_done, t_lin_dma, t_win_dma):
            lin_sb = lin_sbs[hq % 2]; win_sb = win_sbs[hq % 2]
            hp, par = hq // 2, hq % 2
            r = 64 * par
            qcol = hp * T
            fcol = par * T
            vview = vsb[:].rearrange("p (c h ee) -> p c h ee", c=32, h=8)

            def vrow(s, hq=hq, vview=vview):
                return vview[:, s, hq, :]

            t_qf = None
            for t8 in range(8):
                p.op("tensor", lambda e, t8=t8: e.matmul(
                    PX[0:16, 0:512], s_wqf[r:r + 64, 0:FEAT],
                    qfm[r:r + 64, qcol + t8 * 512: qcol + (t8 + 1) * 512],
                    start=True, stop=True), dep=False)
                tq = p.op("tensor", lambda e, t8=t8: e.matmul(
                    PS[0:16, 0:512], s_wkf[r:r + 64, 0:FEAT],
                    kfm[r:r + 64, qcol + t8 * 512: qcol + (t8 + 1) * 512],
                    start=True, stop=True), dep=False, inc=True)
                p.wait("scalar", tq)
                t_qf = p.op("scalar", lambda e, t8=t8: e.copy(
                    qf_sb[0:16, fcol + t8 * 512: fcol + (t8 + 1) * 512], PX[0:16, 0:512]), dep=False)
                t_qf = p.op("scalar", lambda e, t8=t8: e.copy(
                    kf_sb[0:16, fcol + t8 * 512: fcol + (t8 + 1) * 512], PS[0:16, 0:512]), dep=False)
                p.wait("tensor", t_qf)
            t_phik = None
            for ck in range(32):
                kfs = kf_sb[0:16, fcol + ck * 128: fcol + (ck + 1) * 128]
                p.op("tensor", lambda e, kfs=kfs: e.matmul(
                    PG[:, 0:NQUAD], kfs, s_iu[0:16, :], start=True, stop=True), dep=False)
                p.op("tensor", lambda e, kfs=kfs: e.matmul(
                    PH[:, 0:NQUAD], kfs, s_ju[0:16, :], start=True, stop=True), dep=False)
                tmm = p.op("tensor", lambda e, ck=ck: e.matmul(
                    PG[:, 256:272], kfm[r:r + 64, qcol + ck * 128: qcol + (ck + 1) * 128],
                    s_wkf[r:r + 64, 0:FEAT], start=True, stop=True), dep=False, inc=True)
                p.wait("scalar", tmm)
                tg = p.op("scalar", lambda e: e.copy(Gsb[:, 0:NQUAD], PG[:, 0:NQUAD]), dep=True)
                tg2 = p.op("scalar", lambda e, ck=ck: e.copy(
                    kp_sb[:, ck * 16:(ck + 1) * 16], PG[:, 256:272]), dep=False)
                p.wait("vector", tg)
                t_phik = p.op("vector", lambda e, ck=ck: e.tensor_tensor(
                    phik[:, ck * NQUAD:(ck + 1) * NQUAD], Gsb[:, 0:NQUAD],
                    PH[:, 0:NQUAD], ALU.mult), dep=True)
                p.wait("tensor", t_phik)
                p.wait("tensor", tg2)
            t_kv_evac = p.op("vector", lambda e: e.memset(KVsb[:], 0.0), dep=True)
            t_y_done = None
            prev_tg2 = None
            prev_tq2 = None
            for c2 in range(NC2 if "scan" in A2_PARTS else 0):
                s0, s1 = 2 * c2, 2 * c2 + 1
                p.wait("tensor", prev_tg2)
                p.wait("tensor", prev_tq2)
                for (cols, ks, qs) in ((0, s0, s0), (128, s0, s1), (256, s1, s1)):
                    p.op("tensor", lambda e, cols=cols, ks=ks, qs=qs: e.matmul(
                        PS[:, cols:cols + 128],
                        kf_sb[0:16, fcol + ks * 128: fcol + (ks + 1) * 128],
                        qf_sb[0:16, fcol + qs * 128: fcol + (qs + 1) * 128],
                        start=True, stop=True), dep=False)
                qfs = qf_sb[0:16, fcol + c2 * 256: fcol + (c2 + 1) * 256]
                p.op("tensor", lambda e, qfs=qfs: e.matmul(
                    PG[:, 0:256], s_iu[0:16, 0:128], qfs, start=True, stop=True), dep=False)
                p.op("tensor", lambda e, qfs=qfs: e.matmul(
                    PG[0:8, 256:512], s_iu[0:16, 128:136], qfs, start=True, stop=True), dep=False)
                p.op("tensor", lambda e, qfs=qfs: e.matmul(
                    PH[:, 0:256], s_ju[0:16, 0:128], qfs, start=True, stop=True), dep=False)
                tb = p.op("tensor", lambda e, qfs=qfs: e.matmul(
                    PH[0:8, 256:512], s_ju[0:16, 128:136], qfs, start=True, stop=True),
                    dep=False, inc=True)
                p.wait("scalar", tb)
                ta1 = p.op("scalar", lambda e: e.activation(
                    Asb[:], PS[:, 0:384], AF.Square, bias=1.0, scale=0.5), dep=True)
                tg1 = p.op("scalar", lambda e: e.copy(Gsb[:, 0:256], PG[:, 0:256]), dep=False)
                tg2 = p.op("scalar", lambda e: e.copy(Q2qb[0:8, 0:256], PG[0:8, 256:512]), dep=False)
                p.wait("vector", ta1)
                tm1 = p.op("vector", lambda e: e.tensor_tensor(
                    Asb[:, 0:128], Asb[:, 0:128], s_tri[:], ALU.mult), dep=True)
                tm2 = p.op("vector", lambda e: e.tensor_tensor(
                    Asb[:, 256:384], Asb[:, 256:384], s_tri[:], ALU.mult), dep=False)
                p.wait("vector", tg1)
                tq1 = p.op("vector", lambda e: e.tensor_tensor(
                    Q2qa[:], Gsb[:, 0:256], PH[:, 0:256], ALU.mult), dep=True)
                p.wait("vector", tg2)
                tq2 = p.op("vector", lambda e: e.tensor_tensor(
                    Q2qb[0:8, 0:256], Q2qb[0:8, 0:256], PH[0:8, 256:512], ALU.mult), dep=True)
                prev_tg2 = tg2
                prev_tq2 = tq2
                for si, s in enumerate((s0, s1)):
                    p.wait("tensor", tm1 if si == 0 else tm2)
                    p.wait("tensor", tq1)
                    p.wait("tensor", tq2)
                    if si == 0:
                        p.wait("tensor", t_kv_evac)
                    if si == 0 and t_y_done is not None:
                        p.wait("tensor", t_y_done)
                    cs = si * 128
                    first = True
                    if si == 1:
                        p.op("tensor", lambda e, s0=s0, cs=cs: e.matmul(
                            PY[0:65, cs:cs + 128], vrow(s0), Asb[:, 128:256],
                            start=True, stop=False), dep=False)
                        first = False
                    diag = 0 if si == 0 else 256
                    p.op("tensor", lambda e, diag=diag, s=s, first=first, cs=cs: e.matmul(
                        PY[0:65, cs:cs + 128], vrow(s), Asb[:, diag:diag + 128],
                        start=first, stop=False), dep=False)
                    p.op("tensor", lambda e, si=si, cs=cs: e.matmul(
                        PY[0:65, cs:cs + 128], KVsb[:, 0:65], Q2qa[:, si * 128:(si + 1) * 128],
                        start=False, stop=False), dep=False)
                    p.op("tensor", lambda e, si=si, cs=cs: e.matmul(
                        PY[0:65, cs:cs + 128], KVsb[0:8, 65:130], Q2qb[0:8, si * 128:(si + 1) * 128],
                        start=False, stop=False), dep=False)
                    p.op("tensor", lambda e, s=s, cs=cs: e.matmul(
                        PY[0:65, cs:cs + 128], KVsb[0:16, 130:195],
                        qf_sb[0:16, fcol + s * 128: fcol + (s + 1) * 128],
                        start=False, stop=False), dep=False)
                    tylast = p.op("tensor", lambda e, cs=cs: e.matmul(
                        PY[0:65, cs:cs + 128], KVsb[0:1, 195:260], ones_b[0:1, 0:128],
                        start=False, stop=True), dep=False, inc=True)
                p.wait("vector", tylast)
                tz = p.op("vector", lambda e: e.tensor_scalar_add(
                    rzf[64:65, 0:256], PY[64:65, 0:256], 1e-6), dep=True)
                tz2 = p.op("vector", lambda e: e.reciprocal(
                    rzf[64:65, 0:256], rzf[64:65, 0:256]), dep=True)
                p.wait("tensor", tz2)
                tbm = p.op("tensor", lambda e: e.matmul(
                    PB2[0:65, 0:256], onesf[64:65, 0:65], rzf[64:65, 0:256],
                    start=True, stop=True), dep=False, inc=True)
                p.wait("scalar", tbm)
                tbe = p.op("scalar", lambda e: e.copy(bsbA[0:65, 0:256], PB2[0:65, 0:256]), dep=True)
                p.wait("vector", tbe)
                t_y_done = p.op("vector", lambda e, c2=c2: e.tensor_tensor(
                    lin_sb[0:64, c2 * 256:(c2 + 1) * 256], bsbA[0:64, 0:256],
                    PY[0:64, 0:256], ALU.mult), dep=True)
                p.wait("tensor", t_y_done)
                for si, s in enumerate((s0, s1)):
                    stf = (c2 == 0 and si == 0)
                    last = (c2 == NC2 - 1 and si == 1)
                    p.op("tensor", lambda e, s=s, stf=stf: e.matmul(
                        PKV[:, 0:65], phik[:, s * NQUAD: s * NQUAD + 128], vrow(s),
                        start=stf, stop=False), dep=False)
                    p.op("tensor", lambda e, s=s: e.matmul(
                        PKV[0:8, 65:130], phik[:, s * NQUAD + 128: s * NQUAD + 136], vrow(s),
                        start=False, stop=False), dep=False)
                    p.op("tensor", lambda e, s=s: e.matmul(
                        PKV[0:16, 130:195], kp_sb[:, s * 16:(s + 1) * 16], vrow(s),
                        start=False, stop=False), dep=False)
                    tkv = p.op("tensor", lambda e, s=s, last=last: e.matmul(
                        PKV[0:1, 195:260], ones_b[:, 0:1], vrow(s),
                        start=False, stop=last), dep=False, inc=(si == 1))
                p.wait("scalar", tkv)
                t_kv_evac = p.op("scalar", lambda e: e.copy(KVsb[:], PKV[:]), dep=True)
            t_w_done = t_y_done
            t_pt = None
            for qb in range(32 if "win" in A2_PARTS else 0):
                kprev = max(qb - 1, 0)
                if t_pt is not None:
                    p.wait("tensor", t_pt)
                p.op("tensor", lambda e, qb=qb, kprev=kprev: e.matmul(
                    PS[:, 0:128], kfm[r:r + 64, qcol + kprev * 128: qcol + (kprev + 1) * 128],
                    qfm[r:r + 64, qcol + qb * 128: qcol + (qb + 1) * 128],
                    start=True, stop=True), dep=False)
                ts2 = p.op("tensor", lambda e, qb=qb: e.matmul(
                    PS[:, 128:256], kfm[r:r + 64, qcol + qb * 128: qcol + (qb + 1) * 128],
                    qfm[r:r + 64, qcol + qb * 128: qcol + (qb + 1) * 128],
                    start=True, stop=True), dep=False, inc=True)
                p.wait("scalar", ts2)
                te = p.op("scalar", lambda e, hq=hq: e.activation(
                    ptw[:], PS[:, 0:256], AF.Exp, bias=negc[:, hq:hq + 1], scale=0.125), dep=True)
                p.wait("vector", te)
                tmsk = p.op("vector", lambda e, qb=qb: e.tensor_tensor(
                    ptw[:], ptw[:], s_wm0[:] if qb == 0 else s_wm[:], ALU.mult), dep=True)
                t_pt = tmsk
                p.wait("tensor", tmsk)
                cs = (qb % 4) * 128
                if qb % 4 == 0 and t_w_done is not None:
                    p.wait("tensor", t_w_done)
                p.op("tensor", lambda e, kprev=kprev, cs=cs: e.matmul(
                    PY[0:65, cs:cs + 128], vrow(kprev), ptw[:, 0:128],
                    start=True, stop=False), dep=False)
                tav = p.op("tensor", lambda e, qb=qb, cs=cs: e.matmul(
                    PY[0:65, cs:cs + 128], vrow(qb), ptw[:, 128:256],
                    start=False, stop=True), dep=False, inc=True)
                if qb % 4 == 3:
                    qg0 = (qb // 4) * 512
                    p.wait("vector", tav)
                    tz2 = p.op("vector", lambda e: e.reciprocal(
                        rzf[64:65, 0:512], PY[64:65, 0:512]), dep=True)
                    p.wait("tensor", tz2)
                    tbm = p.op("tensor", lambda e: e.matmul(
                        PB2[0:65, 0:512], onesf[64:65, 0:65], rzf[64:65, 0:512],
                        start=True, stop=True), dep=False, inc=True)
                    p.wait("scalar", tbm)
                    tbe = p.op("scalar", lambda e: e.copy(bsbA[0:65, 0:512], PB2[0:65, 0:512]), dep=True)
                    p.wait("vector", tbe)
                    t_w_done = p.op("vector", lambda e, qg0=qg0: e.tensor_tensor(
                        win_sb[0:64, qg0:qg0 + 512], bsbA[0:64, 0:512],
                        PY[0:64, 0:512], ALU.mult), dep=True)
                    p.wait("tensor", t_w_done)
            t_y_done = t_w_done
            p.wait("sync", t_y_done)
            p.wait("sync", t_w_done)
            t_lin_dma = p.dma(lambda e, hq=hq: e.dma_start(
                out=lin_dst(hq), in_=lin_sb[0:64, :]))
            t_win_dma = p.dma(lambda e, hq=hq: e.dma_start(
                out=win_dst(hq), in_=win_sb[0:64, :]))
            p.wait("scalar", t_lin_dma)
            p.wait("scalar", t_win_dma)
            p.wait("vector", t_win_dma)
            return t_y_done, t_lin_dma, t_win_dma

        for hq in range(A2_NHEADS):
            t_y_done, t_lin_dma, t_win_dma = emit_head(hq, t_y_done, t_lin_dma, t_win_dma)
    return p.tok("sync")


def build_a2():
    nc = bass.Bass("TRN2", target_bir_lowering=False, debug=False)
    hfm = nc.dram_tensor("hfm", [D, T], BF16, kind="ExternalInput").ap()
    hfm_r = hfm.rearrange("(kc p) t -> p kc t", p=128)
    o_lin = nc.dram_tensor("o_lin", [HPC, HD, T], BF16, kind="ExternalOutput").ap()
    o_win = nc.dram_tensor("o_win", [HPC, HD, T], BF16, kind="ExternalOutput").ap()
    io = {}
    for nmm, shp, dt in (("wq", [D, 512], F32), ("wk", [D, 512], F32), ("wv", [D, 512], F32),
                         ("nw", [128, 8], F32), ("wqf2", [128, FEAT], BF16),
                         ("wkf2", [128, FEAT], BF16), ("s_iu", [64, NQUAD], BF16),
                         ("s_ju", [64, NQUAD], BF16), ("tri01", [128, 128], BF16),
                         ("winm", [128, 256], BF16), ("winm0", [128, 256], BF16)):
        io[nmm] = nc.dram_tensor(nmm, shp, dt, kind="ExternalInput").ap()
    io["h_ld"] = lambda kc, r: hfm_r[:, kc, r * TH:(r + 1) * TH]
    io["lin_dst"] = lambda hq: o_lin[hq]
    io["win_dst"] = lambda hq: o_win[hq]
    with ExitStack() as ctx:
        p = Plan()
        emit_a2(nc, p, ctx, io)
        p.wait("sync", p.tok("sync"))
        p.emit(nc, ctx)
    return nc


def _a2_inputs():
    pass


def stage_a2(inz, h_shards):
    nc = _get("a2", build_a2)
    iu, ju = np.triu_indices(FEAT)
    scl = np.where(iu == ju, 0.5, 1.0 / math.sqrt(2.0)).astype(np.float32)
    s_iu_m = np.zeros((64, NQUAD), np.float32)
    s_ju_m = np.zeros((64, NQUAD), np.float32)
    for m in range(NQUAD):
        s_iu_m[iu[m], m] = scl[m]
        s_ju_m[ju[m], m] = 1.0
    tri = (np.arange(128)[:, None] <= np.arange(128)[None, :]).astype(np.float32)
    ii = np.arange(128)[:, None]
    jj = np.arange(128)[None, :]
    wm_prev = (ii >= jj + 64).astype(np.float32)
    wm_self = ((ii <= jj) & (ii >= jj - 64)).astype(np.float32)
    winm = np.concatenate([wm_prev, wm_self], axis=1)
    winm0 = np.concatenate([np.zeros_like(wm_prev), wm_self], axis=1)
    wq, wk, wv = inz["L0_attn_Wq"], inz["L0_attn_Wk"], inz["L0_attn_Wv"]
    nw = np.asarray(inz["L0_attn_norm_w"]).reshape(8, 128).T
    wqf2 = np.concatenate([inz["L0_attn_Wqf"], inz["L0_attn_Wqf"]], axis=0)
    wkf2 = np.concatenate([inz["L0_attn_Wkf"], inz["L0_attn_Wkf"]], axis=0)
    in_maps = []
    hfull = [np.concatenate([np.asarray(h_shards[2 * b]), np.asarray(h_shards[2 * b + 1])], axis=1)
             for b in range(B)]
    for c in range(NCORES):
        b, ho = c // 2, (c % 2) * 8
        cols = slice(ho * HD, (ho + 8) * HD)
        in_maps.append(dict(
            hfm=_bf(hfull[b]), wq=_f32(wq[:, cols]), wk=_f32(wk[:, cols]),
            wv=_f32(wv[:, cols]), nw=_f32(nw), wqf2=_bf(wqf2), wkf2=_bf(wkf2),
            s_iu=_bf(s_iu_m), s_ju=_bf(s_ju_m), tri01=_bf(tri),
            winm=_bf(winm), winm0=_bf(winm0)))
    res = run_bass_kernel_spmd(nc, in_maps, core_ids=list(range(NCORES)))
    return ([r["o_lin"] for r in res.results], [r["o_win"] for r in res.results])



def plan_ln_scalars(p, xtile, Dd, scr, sqd, eps, dep_tok=None):
    if dep_tok is not None:
        p.wait("vector", dep_tok)
        p.wait("scalar", dep_tok)
    t1 = p.op("vector", lambda e: e.reduce_sum(scr[:, 0:1], xtile, axis=AX.X), dep=True)
    t2 = p.op("scalar", lambda e: e.activation(sqd, xtile, AF.Square,
                                               accum_out=scr[:, 1:2]), dep=True)
    p.wait("vector", t2)
    p.op("vector", lambda e: e.tensor_scalar_mul(scr[:, 2:3], scr[:, 0:1], 1.0 / Dd))
    p.op("vector", lambda e: e.tensor_scalar_mul(scr[:, 3:4], scr[:, 1:2], 1.0 / Dd))
    p.op("vector", lambda e: e.tensor_tensor(scr[:, 4:5], scr[:, 2:3], scr[:, 2:3], ALU.mult))
    p.op("vector", lambda e: e.tensor_sub(scr[:, 4:5], scr[:, 3:4], scr[:, 4:5]))
    p.op("vector", lambda e: e.tensor_scalar_add(scr[:, 5:6], scr[:, 4:5], eps))
    tv = p.op("vector", lambda e: e.reciprocal(scr[:, 5:6], scr[:, 5:6]))
    p.wait("scalar", tv)
    ts = p.op("scalar", lambda e: e.activation(scr[:, 6:7], scr[:, 5:6], AF.Sqrt))
    p.wait("vector", ts)
    tb = p.op("vector", lambda e: e.scalar_tensor_tensor(
        scr[:, 7:8], scr[:, 2:3], -1.0, scr[:, 6:7], ALU.mult, ALU.mult))
    return tb


def emit_a3(nc, p, ctx, io):
    wout = io["wout"]; ln0 = io["ln0"]; x0 = io["x0"]; o_x1 = io["x1"]
    cat_ld = io["cat_ld"]
    if True:
        sb = lambda n, s, dt=BF16: ctx.enter_context(nc.sbuf_tensor(n, s, dt))
        s_cat = sb("s_cat", [128, 16 * TH])
        s_w = sb("s_w", [128, 16 * D])
        s_ln = ctx.enter_context(nc.sbuf_tensor("s_ln", [128, D], F32))
        s_x0 = ctx.enter_context(nc.sbuf_tensor("s_x0", [128, D], F32))
        s_a0 = ctx.enter_context(nc.sbuf_tensor("s_a0", [128, D], F32))
        s_x1 = ctx.enter_context(nc.sbuf_tensor("s_x1", [128, D], F32))
        sqd = ctx.enter_context(nc.sbuf_tensor("sqd", [128, D], F32))
        scr = ctx.enter_context(nc.sbuf_tensor("scr", [128, 8], F32))
        P0 = ctx.enter_context(nc.psum_tensor("P0", [128, 512], F32))
        P1 = ctx.enter_context(nc.psum_tensor("P1", [128, 512], F32))
        woutr = wout.rearrange("(kc p) d -> p kc d", p=128)
        lnr = ln0.rearrange("(n p) d -> n p d", p=128)
        x0r = x0.rearrange("(n p) d -> n p d", p=128)
        x1r = o_x1.rearrange("(n p) d -> n p d", p=128)

        tld = None
        for kc in range(16):
            tld = p.dma(lambda e, kc=kc: e.dma_start(
                out=s_cat[:, kc * TH:(kc + 1) * TH], in_=cat_ld(kc)))
            tld = p.op("gpsimd", lambda e, kc=kc: e.dma_start(
                out=s_w[:, kc * D:(kc + 1) * D], in_=woutr[:, kc, :]),
                dep=False, amount=16)
        t_w_ld = p.tok("gpsimd")
        t_c_ld = p.tok("sync")
        p.wait("tensor", t_w_ld)
        p.wait("tensor", t_c_ld)
        t_ev = None
        t_st = None
        tx = None
        for i in range(NT):
            p.wait("sync", tx)
            tl1 = p.dma(lambda e, i=i: e.dma_start(out=s_ln[:], in_=lnr[i]), dep=True)
            tl2 = p.dma(lambda e, i=i: e.dma_start(out=s_x0[:], in_=x0r[i]), dep=True)
            if t_st is not None:
                p.wait("sync", t_st)
            for half in range(2):
                pb = P0 if half == 0 else P1
                if t_ev is not None:
                    p.wait("tensor", t_ev)
                for kc in range(16):
                    tm = p.op("tensor", lambda e, pb=pb, kc=kc, i=i, half=half: e.matmul(
                        pb[:], s_cat[:, kc * TH + i * 128:kc * TH + (i + 1) * 128],
                        s_w[:, kc * D + half * 512:kc * D + (half + 1) * 512],
                        start=(kc == 0), stop=(kc == 15)), inc=(kc == 15))
                p.wait("vector", tm)
                p.wait("vector", tl1)
                t_ev = p.op("vector", lambda e, pb=pb, half=half: e.tensor_add(
                    s_a0[:, half * 512:(half + 1) * 512],
                    s_ln[:, half * 512:(half + 1) * 512], pb[:]), dep=True)
            tb = plan_ln_scalars(p, s_a0[:], D, scr, sqd[:], 1e-5, dep_tok=t_ev)
            p.wait("scalar", tb)
            p.wait("scalar", t_st)
            ta = p.op("scalar", lambda e: e.activation(
                s_x1[:], s_a0[:], AF.Identity, bias=scr[:, 7:8], scale=scr[:, 6:7]), dep=True)
            p.wait("vector", ta)
            p.wait("vector", tl2)
            tx = p.op("vector", lambda e: e.tensor_add(s_x1[:], s_x1[:], s_x0[:]), dep=True)
            p.wait("sync", tx)
            t_st = p.dma(lambda e, i=i: e.dma_start(out=x1r[i], in_=s_x1[:]), dep=True)
    return t_st


def build_a3():
    nc = bass.Bass("TRN2", target_bir_lowering=False, debug=False)
    catf = nc.dram_tensor("catf", [2 * D, TH], BF16, kind="ExternalInput").ap()
    catr = catf.rearrange("(kc p) t -> p kc t", p=128)
    io = dict(
        wout=nc.dram_tensor("wout", [2 * D, D], F32, kind="ExternalInput").ap(),
        ln0=nc.dram_tensor("ln0", [TH, D], F32, kind="ExternalInput").ap(),
        x0=nc.dram_tensor("x0", [TH, D], F32, kind="ExternalInput").ap(),
        x1=nc.dram_tensor("o_x1", [TH, D], F32, kind="ExternalOutput").ap(),
        cat_ld=lambda kc: catr[:, kc, :],
    )
    with ExitStack() as ctx:
        p = Plan()
        emit_a3(nc, p, ctx, io)
        p.wait("sync", p.tok("sync"))
        p.emit(nc, ctx)
    return nc


def stage_a3(inz, lin_s, win_s, ln_s, x_shards):
    nc = _get("a3", build_a3)
    in_maps = []
    for c in range(NCORES):
        b, half = c // 2, c % 2
        linf = np.concatenate([np.asarray(lin_s[2 * b]), np.asarray(lin_s[2 * b + 1])],
                              axis=0).reshape(D, T)
        winf = np.concatenate([np.asarray(win_s[2 * b]), np.asarray(win_s[2 * b + 1])],
                              axis=0).reshape(D, T)
        catf = np.concatenate([linf, winf], axis=0)[:, half * TH:(half + 1) * TH]
        in_maps.append(dict(catf=_bf(catf), wout=_f32(inz["L0_attn_Wout"]),
                            ln0=_f32(ln_s[c]), x0=_f32(x_shards[c])))
    res = run_bass_kernel_spmd(nc, in_maps, core_ids=list(range(NCORES)))
    return [r["o_x1"] for r in res.results]


def emit_b(nc, p, ctx, io):
    x1 = io["x1"]; w1t = io["w1t"]; b1 = io["b1"]; w2t = io["w2t"]; b2 = io["b2"]
    ident = io["ident"]; identf = io["identf"]; o_x2 = io["x2"]; ln2f_d = io["ln2f"]
    HN = 4096
    CMAGIC = 1.5 * (2 ** 23)
    if True:
        sb = lambda n, s, dt=BF16: ctx.enter_context(nc.sbuf_tensor(n, s, dt))
        f32t = lambda n, s: ctx.enter_context(nc.sbuf_tensor(n, s, F32))
        wq1 = sb("wq1", [128, 8 * HN])
        wq2 = sb("wq2", [128, 32 * D])
        s_id = sb("s_id", [128, 128]); s_idf = f32t("s_idf", [128, 128])
        b1s = sb("b1s", [1, HN]); b2s = sb("b2s", [1, D])
        s_x1 = f32t("s_x1", [128, D])
        scr = f32t("scrB", [128, 16])
        gs = f32t("gsB", [1, 8])
        onesf = f32t("onesfB", [128, 1])
        swcol = f32t("swcolB", [128, 2])
        srow = sb("srowB", [1, 128])
        xq = sb("xqB", [128, D]); xqf = sb("xqfB", [128, D])
        f0 = f32t("f0B", [128, D]); x2t = f32t("x2tB", [128, D])
        PT = ctx.enter_context(nc.psum_tensor("PTb", [128, 128], BF16))
        PM0 = ctx.enter_context(nc.psum_tensor("PM0", [128, 512], F32))
        PM1 = ctx.enter_context(nc.psum_tensor("PM1", [128, 512], F32))
        PR = ctx.enter_context(nc.psum_tensor("PRb", [128, 128], F32))
        x1r = x1.rearrange("(n p) d -> n p d", p=128)
        x2r = o_x2.rearrange("(n p) d -> n p d", p=128)
        ln2fr = ln2f_d.rearrange("(kc p) t -> p kc t", p=128)
        w1r = w1t.rearrange("(kc p) h -> p kc h", p=128)
        w2r = w2t.rearrange("(kc p) d -> p kc d", p=128)

        t0 = p.dma(lambda e: e.dma_start(out=s_id[:], in_=ident))
        t0 = p.dma(lambda e: e.dma_start(out=s_idf[:], in_=identf))
        p.op("vector", lambda e: e.memset(onesf[:], 1.0), dep=False)

        onesrow = f32t("onesrowB", [1, 128])
        ctx2 = ExitStack()
        wstg = ctx2.enter_context(nc.sbuf_tensor("wstgB", [128, 4096], F32))
        s_b1 = ctx2.enter_context(nc.sbuf_tensor("s_b1", [1, HN], F32))
        s_b2 = ctx2.enter_context(nc.sbuf_tensor("s_b2", [1, D], F32))

        t0 = p.dma(lambda e: e.dma_start(out=s_b1[:], in_=b1))
        t0 = p.dma(lambda e: e.dma_start(out=s_b2[:], in_=b2))

        def quant(wr, nkc, cols, gcol, wq_sb, bsrc, bdst):
            p.op("vector", lambda e: e.memset(scr[:, 8:9], 0.0), dep=True)
            tq = p.tok("vector")
            for kc in range(nkc):
                p.wait("sync", tq)
                tld = p.dma(lambda e, kc=kc: e.dma_start(
                    out=wstg[:, 0:cols], in_=wr[:, kc, :]), dep=True)
                p.wait("vector", tld)
                p.op("vector", lambda e: e.tensor_reduce(
                    scr[:, 9:10], wstg[:, 0:cols], AX.X, ALU.add,
                    apply_absolute_value=True), dep=True)
                tq = p.op("vector", lambda e: e.tensor_add(
                    scr[:, 8:9], scr[:, 8:9], scr[:, 9:10]), dep=True)
            p.wait("tensor", tq)
            tps = p.op("tensor", lambda e: e.matmul(
                PR[0:1, 0:1], scr[:, 8:9], onesf[:], start=True, stop=True), inc=True)
            p.wait("vector", tps)
            p.op("vector", lambda e: e.tensor_copy(gs[0:1, gcol:gcol + 1], PR[0:1, 0:1]), dep=True)
            nn = float(nkc * 128 * cols)
            p.op("vector", lambda e: e.tensor_scalar_mul(
                gs[0:1, gcol:gcol + 1], gs[0:1, gcol:gcol + 1], 1.0 / nn), dep=True)
            p.op("vector", lambda e: e.tensor_scalar_max(
                gs[0:1, gcol:gcol + 1], gs[0:1, gcol:gcol + 1], 1e-5), dep=True)
            trc = p.op("vector", lambda e: e.reciprocal(
                gs[0:1, 2 + gcol:3 + gcol], gs[0:1, gcol:gcol + 1]), dep=True)
            tbs = p.op("vector", lambda e: e.tensor_copy(bdst[0:1, :], bsrc[0:1, :]), dep=True)
            return tbs

        p.op("vector", lambda e: e.memset(onesrow[:], 1.0), dep=False)

        def bcast_col(src11, dstcol, tdep):
            p.wait("tensor", tdep)
            t = p.op("tensor", lambda e: e.matmul(
                PR[:, 2:3], onesrow[0:1, :], src11, start=True, stop=True), inc=True)
            p.wait("vector", t)
            return p.op("vector", lambda e: e.tensor_copy(dstcol, PR[:, 2:3]), dep=True)

        def quant2(wr, nkc, cols, gcol, wq_sb):
            tq = p.tok("vector")
            for kc in range(nkc):
                p.wait("sync", tq)
                tld = p.dma(lambda e, kc=kc: e.dma_start(
                    out=wstg[:, 0:cols], in_=wr[:, kc, :]), dep=True)
                p.wait("vector", tld)
                p.op("vector", lambda e: e.tensor_scalar(
                    wstg[:, 0:cols], wstg[:, 0:cols], swcol[:, gcol:gcol + 1],
                    CMAGIC, ALU.mult, ALU.add), dep=True)
                p.op("vector", lambda e: e.tensor_scalar(
                    wstg[:, 0:cols], wstg[:, 0:cols], CMAGIC, 1.0,
                    ALU.subtract, ALU.min), dep=True)
                tq = p.op("vector", lambda e, kc=kc: e.tensor_scalar_max(
                    wq_sb[:, kc * cols:(kc + 1) * cols], wstg[:, 0:cols], -1.0), dep=True)
            return tq

        tq1t = quant(w1r, 8, HN, 0, wq1, s_b1, b1s)
        bcast_col(gs[0:1, 2:3], swcol[:, 0:1], tq1t)
        tq2t = quant(w2r, 32, D, 1, wq2, s_b2, b2s)
        bcast_col(gs[0:1, 3:4], swcol[:, 1:2], tq2t)
        quant2(w1r, 8, HN, 0, wq1)
        tqw = quant2(w2r, 32, D, 1, wq2)
        if "dbg_gs" in io:
            p.wait("sync", tqw)
            p.dma(lambda e: e.dma_start(out=io["dbg_gs"], in_=gs[:]))
        ctx2.close()
        sqd = f32t("sqdB", [128, 2048])
        hbuf = sb("hbufB", [128, HN])
        xq2 = sb("xq2B", [128, HN]); xq2f = sb("xq2fB", [128, HN])
        ln2f = sb("ln2fB", [128, 1024])
        s_x1b = f32t("s_x1bB", [128, D])
        scrB2 = f32t("scrB2B", [128, 16]); scrM = f32t("scrMB", [128, 16])
        sqdp0 = f32t("sqdp0B", [128, D]); sqdp1 = f32t("sqdp1B", [128, D])
        xqb = sb("xqbB", [128, D])
        PT2 = ctx.enter_context(nc.psum_tensor("PT2b", [128, 128], BF16))
        s_x1P = [s_x1, s_x1b]; scrP = [scr, scrB2]
        sqdP = [sqdp0, sqdp1]; xqP = [xq, xqb]
        p.wait("tensor", tqw)
        p.wait("scalar", tqw)
        p.wait("vector", tqw)
        p.wait("sync", tqw)

        def act_quant_scalars(src_tile, Dd, has_mean, scr):
            if has_mean:
                p.op("vector", lambda e: e.tensor_tensor(scr[:, 9:10], scr[:, 4:5], scr[:, 5:6], ALU.mult))
                p.op("vector", lambda e: e.tensor_scalar_add(scr[:, 9:10], scr[:, 9:10], 1e-6))
                tv = p.op("vector", lambda e: e.reciprocal(scr[:, 9:10], scr[:, 9:10]))
                p.wait("scalar", tv)
                ts = p.op("scalar", lambda e: e.activation(scr[:, 9:10], scr[:, 9:10], AF.Sqrt))
                p.wait("vector", ts)
                p.op("vector", lambda e: e.tensor_tensor(scr[:, 10:11], scr[:, 6:7], scr[:, 9:10], ALU.mult))
                t1 = p.op("vector", lambda e: e.reduce_max(scr[:, 11:12], src_tile, axis=AX.X), dep=True)
                t1 = p.op("vector", lambda e: e.tensor_reduce(scr[:, 9:10], src_tile, AX.X, ALU.min), dep=True)
                p.op("vector", lambda e: e.tensor_sub(scr[:, 11:12], scr[:, 11:12], scr[:, 2:3]))
                p.op("vector", lambda e: e.tensor_sub(scr[:, 9:10], scr[:, 2:3], scr[:, 9:10]))
                p.op("vector", lambda e: e.tensor_max(scr[:, 11:12], scr[:, 11:12], scr[:, 9:10]))
            else:
                p.op("vector", lambda e, Dd=Dd: e.tensor_scalar_mul(scr[:, 9:10], scr[:, 1:2], 1.0 / Dd))
                p.op("vector", lambda e: e.tensor_scalar_add(scr[:, 9:10], scr[:, 9:10], 1e-6))
                tv = p.op("vector", lambda e: e.reciprocal(scr[:, 9:10], scr[:, 9:10]))
                p.wait("scalar", tv)
                ts = p.op("scalar", lambda e: e.activation(scr[:, 10:11], scr[:, 9:10], AF.Sqrt))
                p.wait("vector", ts)
                t1 = p.op("vector", lambda e: e.reduce_max(scr[:, 11:12], src_tile, axis=AX.X,
                                                           apply_absolute_value=True), dep=True)
            p.op("vector", lambda e: e.tensor_tensor(scr[:, 11:12], scr[:, 11:12], scr[:, 10:11], ALU.mult))
            p.op("vector", lambda e: e.tensor_scalar_max(scr[:, 11:12], scr[:, 11:12], 1e-5))
            p.op("vector", lambda e: e.reciprocal(scr[:, 12:13], scr[:, 11:12]))
            p.op("vector", lambda e: e.tensor_scalar_mul(scr[:, 12:13], scr[:, 12:13], 127.0))
            p.op("vector", lambda e: e.tensor_tensor(scr[:, 13:14], scr[:, 10:11], scr[:, 12:13], ALU.mult))
            if has_mean:
                tg = p.op("vector", lambda e: e.scalar_tensor_tensor(
                    scr[:, 14:15], scr[:, 2:3], -1.0, scr[:, 13:14], ALU.mult, ALU.mult))
            else:
                tg = p.op("vector", lambda e: e.memset(scr[:, 14:15], 0.0), dep=True)
            return tg

        def scale_rows_np(scr, gcol):
            p.op("vector", lambda e, scr=scr, gcol=gcol: e.tensor_tensor(
                scr[:, 15:16], scr[:, 12:13], swcol[:, gcol:gcol + 1], ALU.mult))
            tr = p.op("vector", lambda e, scr=scr: e.reciprocal(scr[:, 8:9], scr[:, 15:16]))
            return tr

        t_srt = [None]

        def scale_rows_pe(scr, tr):
            p.wait("tensor", tr)
            p.wait("tensor", t_srt[0])
            tt = p.op("tensor", lambda e, scr=scr: e.transpose(
                PR[0:1, 0:128], scr[:, 15:16], s_idf[:]), inc=True)
            p.wait("scalar", tt)
            tsr = p.op("scalar", lambda e: e.copy(srow[0:1, 0:128], PR[0:1, 0:128]), dep=True)
            t_srt[0] = tsr
            return tt, tsr

        tok_pt = [None, None]

        def transpose_to_fm(src_bf, dst_bf, nkq, tdep):
            p.wait("tensor", tdep)
            tt = None; tcp = None
            for kq in range(nkq):
                pb = PT if kq % 2 == 0 else PT2
                if tok_pt[kq % 2] is not None:
                    p.wait("tensor", tok_pt[kq % 2])
                tt = p.op("tensor", lambda e, kq=kq, pb=pb, src_bf=src_bf: e.transpose(
                    pb[:], src_bf[:, kq * 128:(kq + 1) * 128], s_id[:]), inc=True)
                p.wait("scalar", tt)
                tcp = p.op("scalar", lambda e, kq=kq, pb=pb, dst_bf=dst_bf: e.copy(
                    dst_bf[:, kq * 128:(kq + 1) * 128], pb[:]), dep=True)
                tok_pt[kq % 2] = tcp
            return tt, tcp

        t_st = [p.tok("sync")]
        tx2_tok = [None, None]
        gelu_tok = [None, None]
        xpt_tok = [None, None]
        tsub_par = [None, None]
        scale_tt_tok = [None, None]
        f0ev_prev = [None, None]
        tsub2_prev = [None]
        l2t_prev = [None]
        t_l2dma = [None]

        def pre_np(i):
            par = i % 2
            scr = scrP[par]; src = s_x1P[par][:]
            p.wait("sync", tx2_tok[par])
            tl = p.dma(lambda e, i=i, par=par: e.dma_start(
                out=s_x1P[par][:], in_=x1r[i]), dep=True)
            if scale_tt_tok[par] is not None:
                p.wait("vector", scale_tt_tok[par])
            gp = lambda fn: p.op("vector", fn, dep=True)
            p.wait("vector", tl)
            tvr = p.op("vector", lambda e, scr=scr, src=src: e.tensor_reduce(
                scr[:, 11:12], src, AX.X, ALU.max), dep=True)
            tvr = p.op("vector", lambda e, scr=scr, src=src: e.tensor_reduce(
                scr[:, 7:8], src, AX.X, ALU.min), dep=True)
            p.wait("scalar", tl)
            if gelu_tok[par] is not None:
                p.wait("scalar", gelu_tok[par])
            if tsub_par[par] is not None:
                p.wait("scalar", tsub_par[par])
            tS = p.op("scalar", lambda e, scr=scr, src=src, par=par: e.activation(
                sqdP[par][:], src, AF.Identity, accum_out=scr[:, 0:1]), dep=True)
            t2 = p.op("scalar", lambda e, scr=scr, src=src, par=par: e.activation(
                sqdP[par][:], src, AF.Square, accum_out=scr[:, 1:2]), dep=True)
            p.wait("vector", t2)
            gp(lambda e, scr=scr: e.tensor_scalar_mul(scr[:, 2:3], scr[:, 0:1], 1.0 / D))
            gp(lambda e, scr=scr: e.tensor_scalar_mul(scr[:, 3:4], scr[:, 1:2], 1.0 / D))
            gp(lambda e, scr=scr: e.tensor_tensor(scr[:, 4:5], scr[:, 2:3], scr[:, 2:3], ALU.mult))
            gp(lambda e, scr=scr: e.tensor_sub(scr[:, 4:5], scr[:, 3:4], scr[:, 4:5]))
            tga = gp(lambda e, scr=scr: e.tensor_scalar_add(scr[:, 5:6], scr[:, 4:5], 1e-5))
            p.wait("vector", tga)
            tvx = p.op("vector", lambda e, scr=scr: e.reciprocal(
                scr[:, 5:6], scr[:, 5:6]), dep=True)
            p.wait("scalar", tvx)
            tsa = p.op("scalar", lambda e, scr=scr: e.activation(
                scr[:, 6:7], scr[:, 5:6], AF.Sqrt), dep=True)
            p.wait("vector", tsa)
            gp(lambda e, scr=scr: e.tensor_tensor(scr[:, 9:10], scr[:, 4:5], scr[:, 5:6], ALU.mult))
            tga = gp(lambda e, scr=scr: e.tensor_scalar_add(scr[:, 9:10], scr[:, 9:10], 1e-6))
            p.wait("vector", tga)
            tvx = p.op("vector", lambda e, scr=scr: e.reciprocal(
                scr[:, 9:10], scr[:, 9:10]), dep=True)
            p.wait("scalar", tvx)
            tsa = p.op("scalar", lambda e, scr=scr: e.activation(
                scr[:, 9:10], scr[:, 9:10], AF.Sqrt), dep=True)
            p.wait("vector", tsa)
            gp(lambda e, scr=scr: e.tensor_tensor(scr[:, 10:11], scr[:, 6:7], scr[:, 9:10], ALU.mult))
            
            gp(lambda e, scr=scr: e.tensor_sub(scr[:, 11:12], scr[:, 11:12], scr[:, 2:3]))
            gp(lambda e, scr=scr: e.tensor_sub(scr[:, 7:8], scr[:, 2:3], scr[:, 7:8]))
            gp(lambda e, scr=scr: e.tensor_max(scr[:, 11:12], scr[:, 11:12], scr[:, 7:8]))
            tga = gp(lambda e, scr=scr: e.tensor_scalar(
                scr[:, 11:12], scr[:, 11:12], scr[:, 10:11], 1e-5, ALU.mult, ALU.max))
            p.wait("vector", tga)
            tvx = p.op("vector", lambda e, scr=scr: e.reciprocal(
                scr[:, 12:13], scr[:, 11:12]), dep=True)
            
            gp(lambda e, scr=scr: e.tensor_scalar(
                scr[:, 13:14], scr[:, 12:13], scr[:, 10:11], 127.0, ALU.mult, ALU.mult))
            gp(lambda e, scr=scr: e.scalar_tensor_tensor(
                scr[:, 14:15], scr[:, 2:3], -1.0, scr[:, 13:14], ALU.mult, ALU.mult))
            tg15 = gp(lambda e, scr=scr: e.tensor_scalar(
                scr[:, 15:16], scr[:, 12:13], swcol[:, 0:1], 127.0, ALU.mult, ALU.mult))
            p.wait("vector", tg15)
            tr8 = p.op("vector", lambda e, scr=scr: e.reciprocal(
                scr[:, 8:9], scr[:, 15:16]), dep=True)
            p.wait("scalar", tg15)
            ta = p.op("scalar", lambda e, par=par, scr=scr: e.activation(
                sqdP[par][:], s_x1P[par][:], AF.Identity,
                bias=scr[:, 14:15], scale=scr[:, 13:14]), dep=True)
            p.wait("vector", ta)
            if xpt_tok[par] is not None:
                p.wait("vector", xpt_tok[par])
            tsub = p.op("vector", lambda e, par=par: e.tensor_scalar(
                xqP[par][:], sqdP[par][:], CMAGIC, -CMAGIC, ALU.add, ALU.add), dep=True)
            tsub_par[par] = tsub
            return tsub, tr8

        def body(i, tsub, tr):
            par = i % 2
            tt0, tsr = scale_rows_pe(scrP[par], tr)
            scale_tt_tok[par] = tt0
            ttx, tcp = transpose_to_fm(xqP[par], xqf, 8, tsub)
            xpt_tok[par] = ttx
            p.wait("tensor", tsr)
            p.wait("tensor", f0ev_prev[0])
            p.wait("tensor", f0ev_prev[1])
            t_ev = None
            for g8 in range(8):
                pb = PM0 if g8 % 2 == 0 else PM1
                if g8 >= 2:
                    p.wait("tensor", t_ev2 if g8 % 2 == 0 else t_ev)
                for kc in range(8):
                    p.op("tensor", lambda e, pb=pb, kc=kc, g8=g8: e.matmul(
                        pb[:], xqf[:, kc * 128:(kc + 1) * 128],
                        wq1[:, kc * HN + g8 * 512:kc * HN + (g8 + 1) * 512],
                        start=(kc == 0), stop=False))
                tm = p.op("tensor", lambda e, pb=pb, g8=g8: e.matmul(
                    pb[:], srow[0:1, 0:128], b1s[0:1, g8 * 512:(g8 + 1) * 512],
                    start=False, stop=True), inc=True)
                p.wait("scalar", tm)
                tev = p.op("scalar", lambda e, pb=pb, g8=g8, par=par: e.activation(
                    hbuf[:, g8 * 512:(g8 + 1) * 512], pb[:], AF.Gelu,
                    scale=scrP[par][:, 8:9]), dep=True)
                if g8 % 2 == 0:
                    t_ev2 = tev
                else:
                    t_ev = tev
            t_ev = tev
            gelu_tok[par] = tev
            p.wait("scalar", t_ev)
            p.wait("scalar", tsub2_prev[0])
            t2s = p.op("scalar", lambda e: e.activation(
                sqd[:], hbuf[:, 0:2048], AF.Square, accum_out=scrM[:, 1:2]), dep=True)
            t2sb = p.op("scalar", lambda e: e.activation(
                sqd[:], hbuf[:, 2048:HN], AF.Square, accum_out=scrM[:, 3:4]), dep=True)
            p.wait("vector", t2sb)
            p.op("vector", lambda e: e.tensor_add(
                scrM[:, 1:2], scrM[:, 1:2], scrM[:, 3:4]), dep=True)
            tg2 = act_quant_scalars(hbuf[:], HN, False, scrM)
            p.wait("scalar", tg2)
            ta2 = p.op("scalar", lambda e: e.activation(
                sqd[:], hbuf[:, 0:2048], AF.Identity, bias=scrM[:, 14:15],
                scale=scrM[:, 13:14]), dep=True)
            p.wait("vector", ta2)
            p.wait("vector", l2t_prev[0])
            tsub2 = p.op("vector", lambda e: e.tensor_scalar(
                xq2[:, 0:2048], sqd[:], CMAGIC, -CMAGIC, ALU.add, ALU.add), dep=True)
            p.wait("scalar", tsub2)
            ta2b = p.op("scalar", lambda e: e.activation(
                sqd[:], hbuf[:, 2048:HN], AF.Identity, bias=scrM[:, 14:15],
                scale=scrM[:, 13:14]), dep=True)
            p.wait("vector", ta2b)
            tsub2 = p.op("vector", lambda e: e.tensor_scalar(
                xq2[:, 2048:HN], sqd[:], CMAGIC, -CMAGIC, ALU.add, ALU.add), dep=True)
            tsub2_prev[0] = tsub2
            tr2 = scale_rows_np(scrM, 1)
            _, tsr2 = scale_rows_pe(scrM, tr2)
            _, tcp2 = transpose_to_fm(xq2, xq2f, 32, tsub2)
            p.wait("tensor", tsr2)
            f0evs = [None, None]
            for half in range(2):
                pb = PM0 if half == 0 else PM1
                for kc in range(32):
                    p.op("tensor", lambda e, pb=pb, kc=kc, half=half: e.matmul(
                        pb[:], xq2f[:, kc * 128:(kc + 1) * 128],
                        wq2[:, kc * D + half * 512:kc * D + (half + 1) * 512],
                        start=(kc == 0), stop=False))
                tm = p.op("tensor", lambda e, pb=pb, half=half: e.matmul(
                    pb[:], srow[0:1, 0:128], b2s[0:1, half * 512:(half + 1) * 512],
                    start=False, stop=True), inc=True)
                p.wait("scalar", tm)
                t_ev = p.op("scalar", lambda e, pb=pb, half=half: e.activation(
                    f0[:, half * 512:(half + 1) * 512], pb[:], AF.Identity,
                    scale=scrM[:, 8:9]), dep=True)
                f0evs[half] = t_ev
            f0ev_prev[0] = f0evs[0]
            f0ev_prev[1] = f0evs[1]
            if i == 0 and "dbg_gs" in io:
                p.wait("sync", t_ev)
                p.dma(lambda e: e.dma_start(out=io["dbg_wq"], in_=wq1[:, 0:4096]))
                p.dma(lambda e: e.dma_start(out=io["dbg_xq"], in_=xqP[0][:]))
                p.dma(lambda e: e.dma_start(out=io["dbg_h"], in_=hbuf[:]))
                p.dma(lambda e: e.dma_start(out=io["dbg_sc"], in_=scrM[:]))
            tb2 = plan_ln_scalars(p, f0[:], D, scrM, sqd[:, 0:D], 1e-5, dep_tok=t_ev)
            p.wait("scalar", tb2)
            p.wait("scalar", t_st[0])
            ta3 = p.op("scalar", lambda e: e.activation(
                x2t[:], f0[:], AF.Identity, bias=scrM[:, 7:8], scale=scrM[:, 6:7]), dep=True)
            p.wait("vector", ta3)
            tx2 = p.op("vector", lambda e, par=par: e.tensor_add(
                x2t[:], x2t[:], s_x1P[par][:]), dep=True)
            tx2_tok[par] = tx2
            p.wait("sync", tx2)
            p.wait("sync", t_st[0])
            t_st[0] = p.dma(lambda e, i=i: e.dma_start(out=x2r[i], in_=x2t[:]), dep=True)
            tb3 = plan_ln_scalars(p, x2t[:], D, scrM, sqd[:, 0:D], 1e-5, dep_tok=tx2)
            p.wait("scalar", tb3)
            tl2 = p.op("scalar", lambda e: e.activation(
                xq2[:, 0:D], x2t[:], AF.Identity, bias=scrM[:, 7:8],
                scale=scrM[:, 6:7]), dep=True)
            p.wait("scalar", t_l2dma[0])
            ttl, tcpl = transpose_to_fm(xq2, ln2f, 8, tl2)
            l2t_prev[0] = ttl
            p.wait("sync", tcpl)
            t_l2dma[0] = p.dma(lambda e, i=i: e.dma_start(
                out=ln2fr[:, :, i * 128:(i + 1) * 128],
                in_=ln2f[:].rearrange("p (kc t) -> p kc t", kc=8)), dep=False)

        pre = pre_np(0)
        for i in range(NT):
            nxt = pre_np(i + 1) if i + 1 < NT else None
            body(i, *pre)
            pre = nxt
    return t_l2dma[0]


def build_b():
    nc = bass.Bass("TRN2", target_bir_lowering=False, debug=False)
    io = dict(
        x1=nc.dram_tensor("x1", [TH, D], F32, kind="ExternalInput").ap(),
        w1t=nc.dram_tensor("w1t", [D, 4096], F32, kind="ExternalInput").ap(),
        b1=nc.dram_tensor("b1", [1, 4096], F32, kind="ExternalInput").ap(),
        w2t=nc.dram_tensor("w2t", [4096, D], F32, kind="ExternalInput").ap(),
        b2=nc.dram_tensor("b2", [1, D], F32, kind="ExternalInput").ap(),
        ident=nc.dram_tensor("ident", [128, 128], BF16, kind="ExternalInput").ap(),
        identf=nc.dram_tensor("identf", [128, 128], F32, kind="ExternalInput").ap(),
        x2=nc.dram_tensor("o_x2", [TH, D], F32, kind="ExternalOutput").ap(),
        ln2f=nc.dram_tensor("o_ln2f", [D, TH], BF16, kind="ExternalOutput").ap(),
    )
    with ExitStack() as ctx:
        p = Plan()
        emit_b(nc, p, ctx, io)
        p.wait("sync", p.tok("sync"))
        p.emit(nc, ctx)
    return nc


def stage_b(inz, x1_shards):
    nc = _get("b", build_b)
    idb = np.eye(128, dtype=np.float32)
    in_maps = []
    for c in range(NCORES):
        in_maps.append(dict(
            x1=_f32(x1_shards[c]), w1t=_f32(np.asarray(inz["L0_ffn_w1"]).T),
            b1=_f32(np.asarray(inz["L0_ffn_b1"]).reshape(1, -1)),
            w2t=_f32(np.asarray(inz["L0_ffn_w2"]).T),
            b2=_f32(np.asarray(inz["L0_ffn_b2"]).reshape(1, -1)),
            ident=_bf(idb), identf=_f32(idb)))
    res = run_bass_kernel_spmd(nc, in_maps, core_ids=list(range(NCORES)))
    return ([r["o_x2"] for r in res.results], [r["o_ln2f"] for r in res.results])



def emit_c2(nc, p, ctx, io):
    wq_d = io["wq"]; wqs_d = io["wqs"]; wk_d = io["wk"]; wks_d = io["wks"]
    wv_d = io["wv"]; bq_d = io["bq"]; bqs_d = io["bqs"]; bk_d = io["bk"]
    bks_d = io["bks"]; bv_d = io["bv"]; cosf_d = io["cosf"]; sins_d = io["sins"]
    msk_d = io["msk"]; ln_ld = io["ln_ld"]; o_dst = io["o_dst"]
    if True:
        sb = lambda n, s, dt=BF16: ctx.enter_context(nc.sbuf_tensor(n, s, dt))
        f32t = lambda n, s: ctx.enter_context(nc.sbuf_tensor(n, s, F32))
        qfm = sb("qfmC", [128, 4 * T]); kfm = sb("kfmC", [128, 4 * T])
        vsb = sb("vsbC", [128, 32 * 520])
        s_cos = sb("s_cosC", [128, T]); s_sin = sb("s_sinC", [128, T])
        s_msk = sb("s_mskC", [128, 4 * 512])
        ptw = sb("ptwC", [128, 512])
        ptw2 = sb("ptw2C", [128, 512])
        bsb = sb("bsbC", [128, 512])
        rsc1 = sb("rsc1C", [128, 512]); rsc2 = sb("rsc2C", [128, 512])
        ones_b = sb("ones_bC", [128, 128])
        onesf = f32t("onesfC", [128, 128])
        md = f32t("mdC", [128, 16])
        cbuf = f32t("cbufC", [1, 32])
        negc = f32t("negcC", [128, 8])
        rzb = f32t("rzbC", [128, 512])
        osb = sb("osbC", [64, T])
        ones_row_b = sb("ones_row_bC", [1, 512])
        PX = ctx.enter_context(nc.psum_tensor("PXc", [128, 512], F32))
        PS = ctx.enter_context(nc.psum_tensor("PSc", [128, 512], F32))
        PO = ctx.enter_context(nc.psum_tensor("POc", [128, 512], F32))
        PB = ctx.enter_context(nc.psum_tensor("PBc", [128, 512], F32))
        PY = ctx.enter_context(nc.psum_tensor("PYc", [128, 72], F32))

        ctx2 = ExitStack()
        sb2 = lambda n, s, dt: ctx2.enter_context(nc.sbuf_tensor(n, s, dt))
        s_ht = [sb2("s_ht0C", [128, 8 * 512], BF16), sb2("s_ht1C", [128, 8 * 512], BF16)]
        s_wq = sb2("s_wqC", [128, 8 * 512], BF16)
        s_wqs = sb2("s_wqsC", [128, 8 * 512], BF16)
        s_wk = sb2("s_wkC", [128, 8 * 512], BF16)
        s_wks = sb2("s_wksC", [128, 8 * 512], BF16)
        s_wv = sb2("s_wvC", [128, 8 * 512], BF16)
        s_bias = sb2("s_biasC", [1, 5 * 512], BF16)

        t_h = None
        t_w = None
        for w_sb, w_d in ((s_wq, wq_d), (s_wqs, wqs_d), (s_wk, wk_d),
                          (s_wks, wks_d), (s_wv, wv_d)):
            w_r = w_d.rearrange("(kc p) c -> p kc c", p=128)
            for kc in range(8):
                t_w = p.op("gpsimd", lambda e, a=w_sb, w_r=w_r, kc=kc: e.dma_start(
                    out=a[:, kc * 512:(kc + 1) * 512], in_=w_r[:, kc, :]),
                    dep=False, amount=16)
        for j, b_d in enumerate((bq_d, bqs_d, bk_d, bks_d, bv_d)):
            t_h = p.dma(lambda e, j=j, b_d=b_d: e.dma_start(
                out=s_bias[0:1, j * 512:(j + 1) * 512], in_=b_d))
        t_h = p.dma(lambda e: e.dma_start(out=s_cos[:], in_=cosf_d))
        t_h = p.dma(lambda e: e.dma_start(out=s_sin[:], in_=sins_d))
        t_h = p.dma(lambda e: e.dma_start(out=s_msk[:], in_=msk_d))
        t_ones = p.op("vector", lambda e: e.memset(ones_b[:], 1.0), dep=False)
        p.op("vector", lambda e: e.memset(ones_row_b[:], 1.0), dep=False)
        t_onesf = p.op("vector", lambda e: e.memset(onesf[:], 1.0), dep=False)
        t_vinit = p.op("vector", lambda e: e.memset(vsb[:], 1.0), dep=False)

        p.wait("tensor", t_h)
        p.wait("tensor", t_w)
        p.wait("tensor", t_ones)
        p.wait("tensor", t_vinit)

        def proj(w_sb, bj, dst, tt, cg, t_evprev):
            sh = s_ht[tt % 2]
            if t_evprev is not None:
                p.wait("tensor", t_evprev)
            pb = PX if (tt + cg) % 2 == 0 else PS
            for kc in range(8):
                p.op("tensor", lambda e, pb=pb, w_sb=w_sb, kc=kc, cg=cg, sh=sh: e.matmul(
                    pb[:], w_sb[:, kc * 512 + cg * 128:kc * 512 + (cg + 1) * 128],
                    sh[:, kc * 512:(kc + 1) * 512],
                    start=(kc == 0), stop=False))
            tm = p.op("tensor", lambda e, pb=pb, bj=bj, cg=cg: e.matmul(
                pb[:], s_bias[0:1, bj * 512 + cg * 128:bj * 512 + (cg + 1) * 128],
                ones_b[0:1, 0:128].rearrange("a b -> a b") if False else ones_row_b[0:1, 0:512],
                start=False, stop=True), inc=True)
            p.wait("scalar", tm)
            tev = p.op("scalar", lambda e, pb=pb, dst=dst, cg=cg, tt=tt: e.copy(
                dst[:, cg * T + tt * 512:cg * T + (tt + 1) * 512], pb[:]), dep=False)
            return tev

        t_ev = None
        t_htld = [None, None]
        for tt in range(8):
            sh = s_ht[tt % 2]
            if tt >= 2:
                p.wait("sync", t_ev)
            for kc in range(8):
                t_htld[tt % 2] = p.dma(lambda e, kc=kc, sh=sh, tt=tt: e.dma_start(
                    out=sh[:, kc * 512:(kc + 1) * 512],
                    in_=ln_ld(kc, tt)), dep=False)
            p.wait("tensor", t_htld[tt % 2])
            for cg in range(4):
                t_ev = proj(s_wq, 0, qfm, tt, cg, t_ev)
                t_ev = proj(s_wk, 2, kfm, tt, cg, t_ev)
                for wsw, bj, dst in ((s_wqs, 1, qfm), (s_wks, 3, kfm)):
                    p.wait("tensor", t_ev)
                    pb = PX if (tt + cg) % 2 == 0 else PS
                    for kc in range(8):
                        p.op("tensor", lambda e, pb=pb, wsw=wsw, kc=kc, cg=cg, sh=sh: e.matmul(
                            pb[:], wsw[:, kc * 512 + cg * 128:kc * 512 + (cg + 1) * 128],
                            sh[:, kc * 512:(kc + 1) * 512],
                            start=(kc == 0), stop=False))
                    tm = p.op("tensor", lambda e, pb=pb, bj=bj, cg=cg: e.matmul(
                        pb[:], s_bias[0:1, bj * 512 + cg * 128:bj * 512 + (cg + 1) * 128],
                        ones_row_b[0:1, 0:512], start=False, stop=True), inc=True)
                    p.wait("scalar", tm)
                    tsw = p.op("scalar", lambda e, pb=pb: e.copy(rsc1[:], pb[:]), dep=True)
                    p.wait("vector", tsw)
                    p.op("vector", lambda e, dst=dst, cg=cg, tt=tt: e.tensor_tensor(
                        rsc2[:], dst[:, cg * T + tt * 512:cg * T + (tt + 1) * 512],
                        s_cos[:, tt * 512:(tt + 1) * 512], ALU.mult), dep=True)
                    p.op("vector", lambda e, tt=tt: e.tensor_tensor(
                        rsc1[:], rsc1[:], s_sin[:, tt * 512:(tt + 1) * 512], ALU.mult), dep=True)
                    t_ev = p.op("vector", lambda e, dst=dst, cg=cg, tt=tt: e.tensor_add(
                        dst[:, cg * T + tt * 512:cg * T + (tt + 1) * 512],
                        rsc2[:], rsc1[:]), dep=True)
            vview = vsb[:].rearrange("p (c h ee) -> p c h ee", c=32, h=8)
            for t4 in range(4):
                ck = tt * 4 + t4
                p.wait("tensor", t_ev)
                pb = PX if t4 % 2 == 0 else PS
                for kc in range(8):
                    p.op("tensor", lambda e, pb=pb, kc=kc, t4=t4, sh=sh: e.matmul(
                        pb[:], sh[:, kc * 512 + t4 * 128:kc * 512 + t4 * 128 + 128],
                        s_wv[:, kc * 512:(kc + 1) * 512],
                        start=(kc == 0), stop=False))
                tm = p.op("tensor", lambda e, pb=pb: e.matmul(
                    pb[:], ones_row_b[0:1, 0:128], s_bias[0:1, 4 * 512:4 * 512 + 512],
                    start=False, stop=True), inc=True)
                p.wait("scalar", tm)
                tstg = p.op("scalar", lambda e, pb=pb: e.copy(rsc1[:], pb[:]), dep=True)
                p.wait("vector", tstg)
                t_ev = p.op("vector", lambda e, ck=ck, vv=vview: e.tensor_copy(
                    vv[:, ck, :, 0:64], rsc1[:].rearrange("p (h ee) -> p h ee", h=8)), dep=True)
        ctx2.close()
        sqa = sb("sqaC", [64, 512]); sqb = sb("sqbC", [64, 512])
        mrow = f32t("mrowC", [1, 256])
        mred = f32t("mredC", [1, 32])

        p.wait("scalar", t_ev)
        p.wait("tensor", t_onesf)
        t_mm2 = [None, None]
        t_rd = [None, None]
        i = 0
        for side, src in ((0, qfm), (1, kfm)):
            for hq in range(8):
                hp, par = hq // 2, hq % 2
                r = 64 * par
                for qg in range(8):
                    parb = i % 2
                    sq = sqa if parb == 0 else sqb
                    pbb = PB if parb == 0 else PO
                    p.wait("scalar", t_mm2[parb])
                    tsq = p.op("scalar", lambda e, sq=sq, src=src, r=r, hp=hp, qg=qg: e.activation(
                        sq[0:64, 0:512],
                        src[r:r + 64, hp * T + qg * 512:hp * T + (qg + 1) * 512],
                        AF.Square), dep=True)
                    p.wait("tensor", tsq)
                    p.wait("tensor", t_rd[parb])
                    tmm = p.op("tensor", lambda e, sq=sq, pbb=pbb: e.matmul(
                        pbb[0:1, 0:512], ones_b[0:64, 0:1], sq[0:64, 0:512],
                        start=True, stop=True), inc=True)
                    t_mm2[parb] = tmm
                    p.wait("vector", tmm)
                    trd = p.op("vector", lambda e, pbb=pbb, side=side, hq=hq, qg=qg: e.reduce_max(
                        mrow[0:1, side * 128 + hq * 8 + qg:side * 128 + hq * 8 + qg + 1],
                        pbb[0:1, 0:512], axis=AX.X), dep=True)
                    t_rd[parb] = trd
                    i += 1
        for side in range(2):
            for hq in range(8):
                p.op("vector", lambda e, side=side, hq=hq: e.reduce_max(
                    mred[0:1, side * 8 + hq:side * 8 + hq + 1],
                    mrow[0:1, side * 128 + hq * 8:side * 128 + hq * 8 + 8],
                    axis=AX.X), dep=True)
        tm2 = p.op("vector", lambda e: e.tensor_tensor(
            mred[0:1, 16:24], mred[0:1, 0:8], mred[0:1, 8:16], ALU.mult), dep=True)
        p.wait("scalar", tm2)
        tsr = p.op("scalar", lambda e: e.activation(
            mred[0:1, 16:24], mred[0:1, 16:24], AF.Sqrt), dep=True)
        p.wait("vector", tsr)
        tnb = p.op("vector", lambda e: e.tensor_scalar(
            mred[0:1, 24:32], mred[0:1, 16:24], -0.125, 60.0, ALU.mult, ALU.add), dep=True)
        tnb = p.op("vector", lambda e: e.tensor_scalar(
            mred[0:1, 24:32], mred[0:1, 24:32], 0.0, None, ALU.min), dep=True)
        p.wait("tensor", tnb)
        t_nc = p.op("tensor", lambda e: e.matmul(PY[:, 16:24], onesf[0:1, 0:128],
                                                 mred[0:1, 24:32], start=True, stop=True), inc=True)
        p.wait("vector", t_nc)
        t_negc = p.op("vector", lambda e: e.tensor_copy(negc[:], PY[:, 16:24]), dep=True)
        p.wait("scalar", t_negc)
        p.wait("tensor", t_rd[0])
        p.wait("tensor", t_rd[1])

        t_scal0 = p.tok("scalar")
        pt_tok = [None, None]
        ps_free = [t_scal0, t_scal0]
        av_tok = [None, None]

        def emit_head_c2(hq, t_o_dma):
            hp, par = hq // 2, hq % 2
            r = 64 * par
            qcol = hp * T
            vview = vsb[:].rearrange("p (c h ee) -> p c h ee", c=32, h=8)

            def vrow(s):
                return vview[:, s, hq, :]

            t_po_done = None
            t_bsb = None
            for qg in range(8):
                nkb = 4 * qg + 4

                def emit_S(kb):
                    par = kb % 2
                    psb = PS if par == 0 else PX
                    p.wait("tensor", ps_free[par])
                    ts = p.op("tensor", lambda e, kb=kb, qg=qg, psb=psb: e.matmul(
                        psb[:], kfm[r:r + 64, qcol + kb * 128:qcol + (kb + 1) * 128],
                        qfm[r:r + 64, qcol + qg * 512:qcol + (qg + 1) * 512],
                        start=True, stop=True), inc=True)
                    p.wait("scalar", ts)
                    p.wait("scalar", av_tok[par])
                    ptwb = ptw if par == 0 else ptw2
                    te = p.op("scalar", lambda e, psb=psb, ptwb=ptwb, hq=hq: e.activation(
                        ptwb[:], psb[:], AF.Exp, bias=negc[:, hq:hq + 1], scale=0.125), dep=True)
                    ps_free[par] = te
                    tp = te
                    j = kb - 4 * qg
                    if j >= 0:
                        p.wait("vector", te)
                        tp = p.op("vector", lambda e, j=j, ptwb=ptwb: e.tensor_tensor(
                            ptwb[:], ptwb[:], s_msk[:, j * 512:(j + 1) * 512], ALU.mult), dep=True)
                    pt_tok[par] = tp

                def emit_AV(kb):
                    par = kb % 2
                    ptwb = ptw if par == 0 else ptw2
                    p.wait("tensor", pt_tok[par])
                    if kb == 0 and t_po_done is not None:
                        p.wait("tensor", t_po_done)
                    tavx = p.op("tensor", lambda e, kb=kb, ptwb=ptwb: e.matmul(
                        PO[0:65, 0:512], vrow(kb), ptwb[:],
                        start=(kb == 0), stop=(kb == nkb - 1)), inc=True)
                    av_tok[par] = tavx
                    return tavx

                emit_S(0)
                for kb in range(1, nkb):
                    emit_S(kb)
                    tav = emit_AV(kb - 1)
                tav = emit_AV(nkb - 1)
                p.wait("vector", tav)
                trz = p.op("vector", lambda e: e.reciprocal(
                    rzb[64:65, 0:512], PO[64:65, 0:512]), dep=True)
                p.wait("tensor", trz)
                p.wait("tensor", t_bsb)
                tbm = p.op("tensor", lambda e: e.matmul(
                    PB[0:65, 0:512], onesf[64:65, 0:65], rzb[64:65, 0:512],
                    start=True, stop=True), inc=True)
                p.wait("scalar", tbm)
                tbe = p.op("scalar", lambda e: e.copy(bsb[0:65, 0:512], PB[0:65, 0:512]), dep=True)
                t_bsb = tbe
                p.wait("vector", tbe)
                t_po_done = p.op("vector", lambda e, qg=qg: e.tensor_tensor(
                    osb[0:64, qg * 512:(qg + 1) * 512], bsb[0:64, 0:512],
                    PO[0:64, 0:512], ALU.mult), dep=True)
            p.wait("sync", t_po_done)
            t_o_dma = p.dma(lambda e, hq=hq: e.dma_start(out=o_dst(hq), in_=osb[0:64, :]))
            p.wait("vector", t_o_dma)
            p.wait("tensor", t_o_dma)
            return t_o_dma

        t_o_dma = None
        for hq in range(HPC):
            t_o_dma = emit_head_c2(hq, t_o_dma)
    return t_o_dma


def build_c2():
    nc = bass.Bass("TRN2", target_bir_lowering=False, debug=False)
    lnf = nc.dram_tensor("lnf", [D, T], BF16, kind="ExternalInput").ap()
    lnr = lnf.rearrange("(kc p) t -> p kc t", p=128)
    o_att = nc.dram_tensor("o_att", [HPC, HD, T], BF16, kind="ExternalOutput").ap()
    io = {}
    for nmm, shp, dt in (("wq", [D, 512], F32), ("wqs", [D, 512], F32), ("wk", [D, 512], F32),
                         ("wks", [D, 512], F32), ("wv", [D, 512], F32),
                         ("bq", [1, 512], BF16), ("bqs", [1, 512], BF16), ("bk", [1, 512], BF16),
                         ("bks", [1, 512], BF16), ("bv", [1, 512], BF16),
                         ("cosf", [128, T], BF16), ("sins", [128, T], BF16),
                         ("msk", [128, 4 * 512], BF16)):
        io[nmm] = nc.dram_tensor(nmm, shp, dt, kind="ExternalInput").ap()
    io["ln_ld"] = lambda kc, tt: lnr[:, kc, tt * 512:(tt + 1) * 512]
    io["o_dst"] = lambda hq: o_att[hq]
    with ExitStack() as ctx:
        p = Plan()
        emit_c2(nc, p, ctx, io)
        p.wait("sync", p.tok("sync"))
        p.emit(nc, ctx)
    return nc


def _rope_tables():
    inv_freq = 1.0 / 10000.0 ** (np.arange(0, HD, 2, dtype=np.float64) / HD)
    t = np.arange(T, dtype=np.float64)
    fr = t[:, None] * inv_freq[None, :]
    rope = np.concatenate([np.sin(fr), np.cos(fr)], axis=1)
    sin = rope[:, 0::2].T
    cos = rope[:, 1::2].T
    cosf = np.concatenate([cos, cos, cos, cos], axis=0)
    sins = np.concatenate([-sin, sin, -sin, sin], axis=0)
    return cosf.astype(np.float32), sins.astype(np.float32)


def _c2_perm():
    ev = np.arange(0, 64, 2); od = np.arange(1, 64, 2)
    base = np.concatenate([ev, od, 64 + ev, 64 + od])
    swap = np.concatenate([od, ev, 64 + od, 64 + ev])
    return base, swap


def stage_c2(inz, ln2_shards):
    nc = _get("c2", build_c2)
    qkv_w = np.asarray(inz["L1_attn_qkv_w"])
    qkv_b = np.asarray(inz["L1_attn_qkv_b"])
    base, swap = _c2_perm()
    cosf, sins = _rope_tables()
    kk = np.arange(128)[:, None]
    msk = np.zeros((128, 4 * 512), np.float32)
    for j in range(4):
        for s4 in range(4):
            qq = np.arange(128)[None, :]
            if j < s4:
                blk = np.ones((128, 128), np.float32)
            elif j == s4:
                blk = (kk <= qq).astype(np.float32)
            else:
                blk = np.zeros((128, 128), np.float32)
            msk[:, j * 512 + s4 * 128:j * 512 + (s4 + 1) * 128] = blk
    in_maps = []
    lnfull = [np.concatenate([np.asarray(ln2_shards[2 * b]), np.asarray(ln2_shards[2 * b + 1])],
                             axis=0) for b in range(B)]
    for c in range(NCORES):
        b, ho = c // 2, (c % 2) * 8
        qs, ks, vs = [], [], []
        qbs, kbs, vbs = [], [], []
        qss, kss = [], []
        qsb, ksb = [], []
        for g in range(4):
            cols0 = (ho + 2 * g) * HD
            idx = cols0 + base
            idxs = cols0 + swap
            qs.append(qkv_w[:, idx]); qss.append(qkv_w[:, idxs])
            ks.append(qkv_w[:, D + idx]); kss.append(qkv_w[:, D + idxs])
            vs.append(qkv_w[:, 2 * D + cols0:2 * D + cols0 + 128])
            qbs.append(qkv_b[idx]); qsb.append(qkv_b[idxs])
            kbs.append(qkv_b[D + idx]); ksb.append(qkv_b[D + idxs])
            vbs.append(qkv_b[2 * D + cols0:2 * D + cols0 + 128])
        in_maps.append(dict(
            lnf=_bf(lnfull[b].astype(np.float32).T),
            wq=_f32(np.concatenate(qs, 1)), wqs=_f32(np.concatenate(qss, 1)),
            wk=_f32(np.concatenate(ks, 1)), wks=_f32(np.concatenate(kss, 1)),
            wv=_f32(np.concatenate(vs, 1)),
            bq=_bf(np.concatenate(qbs).reshape(1, -1)), bqs=_bf(np.concatenate(qsb).reshape(1, -1)),
            bk=_bf(np.concatenate(kbs).reshape(1, -1)), bks=_bf(np.concatenate(ksb).reshape(1, -1)),
            bv=_bf(np.concatenate(vbs).reshape(1, -1)),
            cosf=_bf(cosf), sins=_bf(sins), msk=_bf(msk)))
    res = run_bass_kernel_spmd(nc, in_maps, core_ids=list(range(NCORES)))
    return [r["o_att"] for r in res.results]


def emit_c3(nc, p, ctx, io):
    outw = io["outw"]; outb = io["outb"]; x2 = io["x2"]; o_x3 = io["x3"]
    ln3f_d = io["ln3f"]; o_ld = io["o_ld"]; ident = io["ident"]
    if True:
        s_o = ctx.enter_context(nc.sbuf_tensor("s_oC3", [128, 8 * TH], BF16))
        s_w = ctx.enter_context(nc.sbuf_tensor("s_wC3", [128, 8 * D], BF16))
        s_b = ctx.enter_context(nc.sbuf_tensor("s_bC3", [1, D], BF16))
        ones_r = ctx.enter_context(nc.sbuf_tensor("ones_rC3", [1, 128], BF16))
        s_x2 = ctx.enter_context(nc.sbuf_tensor("s_x2C3", [128, D], F32))
        s_o1 = ctx.enter_context(nc.sbuf_tensor("s_o1C3", [128, D], F32))
        s_x3 = ctx.enter_context(nc.sbuf_tensor("s_x3C3", [128, D], F32))
        s_l3 = ctx.enter_context(nc.sbuf_tensor("s_l3C3", [128, D], BF16))
        sqd = ctx.enter_context(nc.sbuf_tensor("sqdC3", [128, D], F32))
        scr = ctx.enter_context(nc.sbuf_tensor("scrC3", [128, 8], F32))
        P0 = ctx.enter_context(nc.psum_tensor("P0c3", [128, 512], F32))
        P1 = ctx.enter_context(nc.psum_tensor("P1c3", [128, 512], F32))
        wrr = outw.rearrange("(kc p) d -> p kc d", p=128)
        x2r = x2.rearrange("(n p) d -> n p d", p=128)
        x3r = o_x3.rearrange("(n p) d -> n p d", p=128)
        l3fr = ln3f_d.rearrange("(kc p) t -> p kc t", p=128)
        s_idc = ctx.enter_context(nc.sbuf_tensor("s_idC3", [128, 128], BF16))
        l3stg = ctx.enter_context(nc.sbuf_tensor("l3stgC3", [128, 1024], BF16))
        PTc = ctx.enter_context(nc.psum_tensor("PTc3", [128, 128], BF16))

        for kc in range(8):
            tld = p.dma(lambda e, kc=kc: e.dma_start(
                out=s_o[:, kc * TH:(kc + 1) * TH], in_=o_ld(kc)))
            tld = p.op("gpsimd", lambda e, kc=kc: e.dma_start(
                out=s_w[:, kc * D:(kc + 1) * D], in_=wrr[:, kc, :]), dep=False, amount=16)
        tld2 = p.dma(lambda e: e.dma_start(out=s_b[:], in_=outb))
        tld2 = p.dma(lambda e: e.dma_start(out=s_idc[:], in_=ident))
        tmo = p.op("vector", lambda e: e.memset(ones_r[:], 1.0), dep=False)
        p.wait("tensor", p.tok("gpsimd"))
        p.wait("tensor", p.tok("sync"))
        p.wait("tensor", tmo)
        t_ev = None
        t_st = None
        tx = None
        for i in range(NT):
            p.wait("sync", tx)
            tl2 = p.dma(lambda e, i=i: e.dma_start(out=s_x2[:], in_=x2r[i]), dep=True)
            if t_st is not None:
                p.wait("sync", t_st)
            for half in range(2):
                pb = P0 if half == 0 else P1
                if t_ev is not None:
                    p.wait("tensor", t_ev)
                for kc in range(8):
                    p.op("tensor", lambda e, pb=pb, kc=kc, i=i, half=half: e.matmul(
                        pb[:], s_o[:, kc * TH + i * 128:kc * TH + (i + 1) * 128],
                        s_w[:, kc * D + half * 512:kc * D + (half + 1) * 512],
                        start=(kc == 0), stop=False))
                tm = p.op("tensor", lambda e, pb=pb, half=half: e.matmul(
                    pb[:], ones_r[0:1, :], s_b[0:1, half * 512:(half + 1) * 512],
                    start=False, stop=True), inc=True)
                p.wait("vector", tm)
                t_ev = p.op("vector", lambda e, pb=pb, half=half: e.tensor_copy(
                    s_o1[:, half * 512:(half + 1) * 512], pb[:]), dep=True)
            tb = plan_ln_scalars(p, s_o1[:], D, scr, sqd[:], 1e-5, dep_tok=t_ev)
            p.wait("scalar", tb)
            p.wait("scalar", t_st)
            ta = p.op("scalar", lambda e: e.activation(
                s_x3[:], s_o1[:], AF.Identity, bias=scr[:, 7:8], scale=scr[:, 6:7]), dep=True)
            p.wait("vector", ta)
            p.wait("vector", tl2)
            tx = p.op("vector", lambda e: e.tensor_add(s_x3[:], s_x3[:], s_x2[:]), dep=True)
            p.wait("sync", tx)
            t_st = p.dma(lambda e, i=i: e.dma_start(out=x3r[i], in_=s_x3[:]), dep=True)
            tb2 = plan_ln_scalars(p, s_x3[:], D, scr, sqd[:], 1e-5, dep_tok=tx)
            p.wait("scalar", tb2)
            tl3 = p.op("scalar", lambda e: e.activation(
                s_l3[:], s_x3[:], AF.Identity, bias=scr[:, 7:8], scale=scr[:, 6:7]), dep=True)
            p.wait("tensor", tl3)
            if i > 0:
                p.wait("tensor", t_l3dma)
            for kc in range(8):
                tt2 = p.op("tensor", lambda e, kc=kc: e.transpose(
                    PTc[:], s_l3[:, kc * 128:(kc + 1) * 128], s_idc[:]), inc=True)
                p.wait("scalar", tt2)
                tl3 = p.op("scalar", lambda e, kc=kc: e.copy(
                    l3stg[:, kc * 128:(kc + 1) * 128], PTc[:]), dep=True)
                p.wait("tensor", tl3)
            p.wait("sync", tl3)
            t_l3dma = p.dma(lambda e, i=i: e.dma_start(
                out=l3fr[:, :, i * 128:(i + 1) * 128],
                in_=l3stg[:].rearrange("p (kc t) -> p kc t", kc=8)), dep=False)
            tx = tl3
    return t_l3dma


def build_c3():
    nc = bass.Bass("TRN2", target_bir_lowering=False, debug=False)
    ofm = nc.dram_tensor("ofm", [D, TH], BF16, kind="ExternalInput").ap()
    orr = ofm.rearrange("(kc p) t -> p kc t", p=128)
    io = dict(
        outw=nc.dram_tensor("outw", [D, D], F32, kind="ExternalInput").ap(),
        outb=nc.dram_tensor("outb", [1, D], BF16, kind="ExternalInput").ap(),
        ident=nc.dram_tensor("ident", [128, 128], BF16, kind="ExternalInput").ap(),
        x2=nc.dram_tensor("x2", [TH, D], F32, kind="ExternalInput").ap(),
        x3=nc.dram_tensor("o_x3", [TH, D], F32, kind="ExternalOutput").ap(),
        ln3f=nc.dram_tensor("o_ln3f", [D, TH], BF16, kind="ExternalOutput").ap(),
        o_ld=lambda kc: orr[:, kc, :],
    )
    with ExitStack() as ctx:
        p = Plan()
        emit_c3(nc, p, ctx, io)
        p.wait("sync", p.tok("sync"))
        p.emit(nc, ctx)
    return nc


def stage_c3(inz, att_s, x2_shards):
    nc = _get("c3", build_c3)
    idb = np.eye(128, dtype=np.float32)
    in_maps = []
    for c in range(NCORES):
        b, half = c // 2, c % 2
        of = np.concatenate([np.asarray(att_s[2 * b]).reshape(512, T),
                             np.asarray(att_s[2 * b + 1]).reshape(512, T)], axis=0)
        in_maps.append(dict(
            ofm=_bf(of[:, half * TH:(half + 1) * TH]),
            outw=_f32(inz["L1_attn_out_w"]),
            outb=_bf(np.asarray(inz["L1_attn_out_b"]).reshape(1, -1)),
            ident=_bf(idb),
            x2=_f32(x2_shards[c])))
    res = run_bass_kernel_spmd(nc, in_maps, core_ids=list(range(NCORES)))
    return ([r["o_x3"] for r in res.results], [r["o_ln3f"] for r in res.results])



def emit_d(nc, p, ctx, io):
    rout_d = io["rout"]; wg_d = io["wg"]; wu_d = io["wu"]; wd_d = io["wd"]
    sg_d = io["sg"]; su_d = io["su"]; sd_d = io["sd"]; sel8_d = io["sel8"]
    identf = io["identf"]; x3_d = io["x3"]; o_x4 = io["x4"]; xf_ld = io["xf_ld"]
    if True:
        sb = lambda n, s, dt=BF16: ctx.enter_context(nc.sbuf_tensor(n, s, dt))
        f32t = lambda n, s: ctx.enter_context(nc.sbuf_tensor(n, s, F32))
        xf = sb("xfD", [128, 8 * TH])
        acc = f32t("accD", [128, NT * D])
        s_rout = sb("s_routD", [128, 8 * 8])
        s_sel = sb("s_selD", [8, 8 * 128])
        s_idf = f32t("s_idfD", [128, 128])
        cf8 = sb("cf8D", [8, NT * 128])
        prb = f32t("prbD", [128, 8]); m8 = f32t("m8D", [128, 8])
        scr = f32t("scrD", [128, 16])
        sqd = f32t("sqdD", [128, D])
        wgA = sb("wgAD", [128, 8 * 512]); wgB = sb("wgBD", [128, 8 * 512])
        wuA = sb("wuAD", [128, 8 * 512]); wuB = sb("wuBD", [128, 8 * 512])
        wdA = sb("wdAD", [128, 4 * D]); wdB = sb("wdBD", [128, 4 * D])
        wBp = [(wgA, wuA, wdA), (wgB, wuB, wdB)]
        hfm = sb("hfmD", [128, 8 * 512])
        gstg = sb("gstgD", [128, 512])
        crep = sb("crepD", [128, 512])
        ones_b = sb("ones_bD", [128, 128])
        s_x3 = f32t("s_x3D", [128, D])
        x4t = f32t("x4tD", [128, D])
        PG = ctx.enter_context(nc.psum_tensor("PGd", [128, 512], F32))
        PU = ctx.enter_context(nc.psum_tensor("PUd", [128, 512], F32))
        PD0 = ctx.enter_context(nc.psum_tensor("PD0d", [128, 512], F32))
        PD1 = ctx.enter_context(nc.psum_tensor("PD1d", [128, 512], F32))
        PC = ctx.enter_context(nc.psum_tensor("PCd", [128, 512], F32))
        x3r = x3_d.rearrange("(n p) d -> n p d", p=128)
        x4r = o_x4.rearrange("(n p) d -> n p d", p=128)

        for kc in range(8):
            t_x = p.dma(lambda e, kc=kc: e.dma_start(
                out=xf[:, kc * TH:(kc + 1) * TH], in_=xf_ld(kc)), dep=(kc == 0))
        t_r = p.op("gpsimd", lambda e: e.dma_start(
            out=s_rout[:].rearrange("p (kc c) -> p kc c", c=8),
            in_=rout_d.rearrange("(kc p) c -> p kc c", p=128)), dep=False, amount=16)
        t_s = p.dma(lambda e: e.dma_start(out=s_sel[:], in_=sel8_d))
        t_s = p.dma(lambda e: e.dma_start(out=s_idf[:], in_=identf))
        t_ones = p.op("vector", lambda e: e.memset(ones_b[:], 1.0), dep=False)

        p.wait("tensor", t_x)
        p.wait("tensor", t_r)
        p.wait("tensor", t_s)
        t_cf = None
        t_pcread = None
        for i in range(NT):
            p.wait("tensor", t_pcread)
            for kc in range(8):
                tm = p.op("tensor", lambda e, kc=kc, i=i: e.matmul(
                    PC[:, 0:8], xf[:, kc * TH + i * 128:kc * TH + (i + 1) * 128],
                    s_rout[:, kc * 8:(kc + 1) * 8],
                    start=(kc == 0), stop=(kc == 7)), inc=(kc == 7))
            p.wait("vector", tm)
            tv = p.op("vector", lambda e: e.tensor_copy(prb[:], PC[:, 0:8]), dep=True)
            t_pcread = tv
            tv = p.op("vector", lambda e: e.reduce_max(scr[:, 0:1], prb[:], axis=AX.X,
                                                       negate=True), dep=True)
            p.wait("scalar", tv)
            ts = p.op("scalar", lambda e: e.activation(
                prb[:], prb[:], AF.Exp, bias=scr[:, 0:1], scale=1.0,
                accum_out=scr[:, 1:2]), dep=True)
            p.wait("vector", ts)
            tv = p.op("vector", lambda e: e.reciprocal(scr[:, 2:3], scr[:, 1:2]), dep=True)
            tv = p.op("vector", lambda e: e.tensor_scalar(
                prb[:], prb[:], scr[:, 2:3], None, ALU.mult), dep=True)
            tv = p.op("vector", lambda e: e.max(out=m8[:], in_=prb[:]), dep=True)
            tcoef = p.op("vector", lambda e: e.scalar_tensor_tensor(
                prb[:], prb[:], m8[:, 1:2], prb[:], ALU.is_ge, ALU.mult), dep=True)
            p.wait("tensor", tcoef)
            tt = p.op("tensor", lambda e: e.transpose(PU[0:8, 0:128], prb[:], s_idf[:]),
                      inc=True)
            p.wait("scalar", tt)
            t_cf = p.op("scalar", lambda e, i=i: e.copy(
                cf8[0:8, i * 128:(i + 1) * 128], PU[0:8, 0:128]), dep=True)
            p.wait("tensor", t_cf)
        t_acc = None
        t_crep_read = t_pcread
        t_sil = None
        t_lastdown = None
        t_wl = [None, None]
        t_use = [None, None]

        def issue_loads(pp):
            if pp >= 10:
                return
            parw = pp % 2
            wg_sb, wu_sb, wd_sb = wBp[parw]
            p.wait("gpsimd", t_use[parw])
            if pp >= 8:
                h = pp - 8
                p.op("gpsimd", lambda e, h=h, a=wg_sb: e.dma_start(
                    out=a[:].rearrange("p (kc c) -> p kc c", c=512),
                    in_=sg_d[:, h * 512:(h + 1) * 512].rearrange(
                        "(kc p) c -> p kc c", p=128)), dep=True, amount=16)
                p.op("gpsimd", lambda e, h=h, a=wu_sb: e.dma_start(
                    out=a[:].rearrange("p (kc c) -> p kc c", c=512),
                    in_=su_d[:, h * 512:(h + 1) * 512].rearrange(
                        "(kc p) c -> p kc c", p=128)), dep=False, amount=16)
                tw = p.op("gpsimd", lambda e, h=h, a=wd_sb: e.dma_start(
                    out=a[:].rearrange("p (kc c) -> p kc c", c=D),
                    in_=sd_d[h * 512:(h + 1) * 512, :].rearrange(
                        "(kc p) c -> p kc c", p=128)), dep=False, amount=16)
            else:
                p.op("gpsimd", lambda e, pp=pp, a=wg_sb: e.dma_start(
                    out=a[:].rearrange("p (kc c) -> p kc c", c=512),
                    in_=wg_d[pp].rearrange("(kc p) c -> p kc c", p=128)),
                    dep=True, amount=16)
                p.op("gpsimd", lambda e, pp=pp, a=wu_sb: e.dma_start(
                    out=a[:].rearrange("p (kc c) -> p kc c", c=512),
                    in_=wu_d[pp].rearrange("(kc p) c -> p kc c", p=128)),
                    dep=False, amount=16)
                tw = p.op("gpsimd", lambda e, pp=pp, a=wd_sb: e.dma_start(
                    out=a[:].rearrange("p (kc c) -> p kc c", c=D),
                    in_=wd_d[pp].rearrange("(kc p) c -> p kc c", p=128)),
                    dep=False, amount=16)
            t_wl[parw] = tw

        issue_loads(0)
        for pp in range(10):
            shared = (pp >= 8)
            parw = pp % 2
            wg_sb, wu_sb, wd_sb = wBp[parw]
            issue_loads(pp + 1)
            p.wait("tensor", t_wl[parw])
            for tg in range(4):
                if not shared:
                    p.wait("tensor", t_cf)
                    p.wait("tensor", t_crep_read)
                    tc = p.op("tensor", lambda e, pp=pp, tg=tg: e.matmul(
                        PC[:], s_sel[0:8, pp * 128:(pp + 1) * 128],
                        cf8[0:8, tg * 512:(tg + 1) * 512], start=True, stop=True), inc=True)
                    p.wait("scalar", tc)
                    p.wait("scalar", t_lastdown)
                    tce = p.op("scalar", lambda e: e.copy(crep[:], PC[:]), dep=True)
                    t_crep_read = tce
                p.wait("vector", t_lastdown)
                t_h = None
                for hc in range(4):
                    p.wait("tensor", t_sil)
                    for kc in range(8):
                        p.op("tensor", lambda e, kc=kc, hc=hc, tg=tg, a=wg_sb: e.matmul(
                            PG[:], a[:, kc * 512 + hc * 128:kc * 512 + hc * 128 + 128],
                            xf[:, kc * TH + tg * 512:kc * TH + (tg + 1) * 512],
                            start=(kc == 0), stop=(kc == 7)), inc=(kc == 7))
                        tg2 = p.op("tensor", lambda e, kc=kc, hc=hc, tg=tg, a=wu_sb: e.matmul(
                            PU[:], a[:, kc * 512 + hc * 128:kc * 512 + hc * 128 + 128],
                            xf[:, kc * TH + tg * 512:kc * TH + (tg + 1) * 512],
                            start=(kc == 0), stop=(kc == 7)), inc=(kc == 7))
                    p.wait("scalar", tg2)
                    if t_h is not None:
                        p.wait("scalar", t_h)
                    tsil = p.op("scalar", lambda e: e.activation(
                        gstg[:], PG[:], AF.Silu), dep=True)
                    t_sil = tsil
                    p.wait("vector", tsil)
                    if not shared:
                        p.wait("vector", tce)
                        p.op("vector", lambda e: e.tensor_tensor(
                            gstg[:], gstg[:], PU[:], ALU.mult), dep=True)
                        t_h = p.op("vector", lambda e, hc=hc: e.tensor_tensor(
                            hfm[:, hc * 512:(hc + 1) * 512], gstg[:], crep[:],
                            ALU.mult), dep=True)
                    else:
                        t_h = p.op("vector", lambda e, hc=hc: e.tensor_tensor(
                            hfm[:, hc * 512:(hc + 1) * 512], gstg[:], PU[:],
                            ALU.mult), dep=True)
                    p.wait("tensor", t_h)
                for t4 in range(4):
                    it = tg * 4 + t4
                    for half in range(2):
                        pd = PD0 if half == 0 else PD1
                        if t_acc is not None:
                            p.wait("tensor", t_acc)
                        for kd in range(4):
                            tdm = p.op("tensor", lambda e, kd=kd, t4=t4, half=half, pd=pd, a=wd_sb: e.matmul(
                                pd[:], hfm[:, kd * 512 + t4 * 128:kd * 512 + t4 * 128 + 128],
                                a[:, kd * D + half * 512:kd * D + (half + 1) * 512],
                                start=(kd == 0), stop=(kd == 3)), inc=(kd == 3))
                        t_lastdown = tdm
                        p.wait("vector", tdm)
                        aslice = acc[:, it * D + half * 512: it * D + (half + 1) * 512]
                        if pp == 0:
                            t_acc = p.op("vector", lambda e, pd=pd, aslice=aslice: e.tensor_copy(
                                aslice, pd[:]), dep=True)
                        else:
                            t_acc = p.op("vector", lambda e, pd=pd, aslice=aslice: e.tensor_add(
                                aslice, aslice, pd[:]), dep=True)
            t_use[parw] = t_lastdown
        if "dbg_acc" in io:
            p.wait("sync", t_acc)
            p.dma(lambda e: e.dma_start(out=io["dbg_acc"], in_=acc[:]))
        t_st = None
        tx = None
        for i in range(NT):
            p.wait("sync", tx)
            tl = p.dma(lambda e, i=i: e.dma_start(out=s_x3[:], in_=x3r[i]), dep=True)
            asl = acc[:, i * D:(i + 1) * D]
            tb = plan_ln_scalars(p, asl, D, scr, sqd[:], 1e-5, dep_tok=t_acc)
            p.wait("scalar", tb)
            p.wait("scalar", t_st)
            ta = p.op("scalar", lambda e, asl=asl: e.activation(
                x4t[:], asl, AF.Identity, bias=scr[:, 7:8], scale=scr[:, 6:7]), dep=True)
            p.wait("vector", ta)
            p.wait("vector", tl)
            tx = p.op("vector", lambda e: e.tensor_add(x4t[:], x4t[:], s_x3[:]), dep=True)
            p.wait("sync", tx)
            if t_st is not None:
                p.wait("sync", t_st)
            t_st = p.dma(lambda e, i=i: e.dma_start(out=x4r[i], in_=x4t[:]), dep=True)
    return t_st


def build_d():
    nc = bass.Bass("TRN2", target_bir_lowering=False, debug=False)
    xf_d = nc.dram_tensor("xf", [D, TH], BF16, kind="ExternalInput").ap()
    xfr = xf_d.rearrange("(kc p) t -> p kc t", p=128)
    io = dict(
        rout=nc.dram_tensor("rout", [D, 8], F32, kind="ExternalInput").ap(),
        wg=nc.dram_tensor("wg", [8, D, 512], F32, kind="ExternalInput").ap(),
        wu=nc.dram_tensor("wu", [8, D, 512], F32, kind="ExternalInput").ap(),
        wd=nc.dram_tensor("wd", [8, 512, D], F32, kind="ExternalInput").ap(),
        sg=nc.dram_tensor("sg", [D, D], F32, kind="ExternalInput").ap(),
        su=nc.dram_tensor("su", [D, D], F32, kind="ExternalInput").ap(),
        sd=nc.dram_tensor("sd", [D, D], F32, kind="ExternalInput").ap(),
        sel8=nc.dram_tensor("sel8", [8, 8 * 128], BF16, kind="ExternalInput").ap(),
        identf=nc.dram_tensor("identf", [128, 128], F32, kind="ExternalInput").ap(),
        x3=nc.dram_tensor("x3", [TH, D], F32, kind="ExternalInput").ap(),
        x4=nc.dram_tensor("o_x4", [TH, D], F32, kind="ExternalOutput").ap(),
        xf_ld=lambda kc: xfr[:, kc, :],
    )
    with ExitStack() as ctx:
        p = Plan()
        emit_d(nc, p, ctx, io)
        p.wait("sync", p.tok("sync"))
        p.emit(nc, ctx)
    return nc


def stage_d(inz, ln3_shards, x3_shards):
    nc = _get("d", build_d)
    sel8 = np.zeros((8, 8 * 128), np.float32)
    for e in range(8):
        sel8[e, e * 128:(e + 1) * 128] = 1.0
    idf = np.eye(128, dtype=np.float32)
    in_maps = []
    for c in range(NCORES):
        in_maps.append(dict(
            xf=_bf(np.asarray(ln3_shards[c])),
            rout=_f32(inz["L1_ffn_router"]),
            wg=_f32(inz["L1_ffn_experts_gate"]), wu=_f32(inz["L1_ffn_experts_up"]),
            wd=_f32(inz["L1_ffn_experts_down"]),
            sg=_f32(inz["L1_ffn_shared_gate"]), su=_f32(inz["L1_ffn_shared_up"]),
            sd=_f32(inz["L1_ffn_shared_down"]),
            sel8=_bf(sel8), identf=_f32(idf), x3=_f32(x3_shards[c])))
    res = run_bass_kernel_spmd(nc, in_maps, core_ids=list(range(NCORES)))
    return [r["o_x4"] for r in res.results]



PAIRS = [[0, 1], [2, 3], [4, 5], [6, 7]]


class NCProxy:
    _n = 0

    def __init__(self, nc):
        self._nc = nc

    def sbuf_tensor(self, name, *a, **k):
        NCProxy._n += 1
        return self._nc.sbuf_tensor(f"{name}_u{NCProxy._n}", *a, **k)

    def psum_tensor(self, name, *a, **k):
        NCProxy._n += 1
        return self._nc.psum_tensor(f"{name}_u{NCProxy._n}", *a, **k)

    def __getattr__(self, k):
        return getattr(self._nc, k)


def _a2_consts(inz):
    iu, ju = np.triu_indices(FEAT)
    scl = np.where(iu == ju, 0.5, 1.0 / math.sqrt(2.0)).astype(np.float32)
    s_iu_m = np.zeros((64, NQUAD), np.float32)
    s_ju_m = np.zeros((64, NQUAD), np.float32)
    for m in range(NQUAD):
        s_iu_m[iu[m], m] = scl[m]
        s_ju_m[ju[m], m] = 1.0
    tri = (np.arange(128)[:, None] <= np.arange(128)[None, :]).astype(np.float32)
    ii = np.arange(128)[:, None]; jj = np.arange(128)[None, :]
    wm_prev = (ii >= jj + 64).astype(np.float32)
    wm_self = ((ii <= jj) & (ii >= jj - 64)).astype(np.float32)
    winm = np.concatenate([wm_prev, wm_self], axis=1)
    winm0 = np.concatenate([np.zeros_like(wm_prev), wm_self], axis=1)
    nw = np.asarray(inz["L0_attn_norm_w"]).reshape(8, 128).T
    wqf2 = np.concatenate([inz["L0_attn_Wqf"], inz["L0_attn_Wqf"]], axis=0)
    wkf2 = np.concatenate([inz["L0_attn_Wkf"], inz["L0_attn_Wkf"]], axis=0)
    return s_iu_m, s_ju_m, tri, winm, winm0, nw, wqf2, wkf2


def _c2_consts(inz, c):
    qkv_w = np.asarray(inz["L1_attn_qkv_w"]); qkv_b = np.asarray(inz["L1_attn_qkv_b"])
    base, swap = _c2_perm()
    cosf, sins = _rope_tables()
    kk = np.arange(128)[:, None]
    msk = np.zeros((128, 4 * 512), np.float32)
    for j in range(4):
        for s4 in range(4):
            qq = np.arange(128)[None, :]
            blk = (np.ones((128, 128), np.float32) if j < s4 else
                   ((kk <= qq).astype(np.float32) if j == s4 else np.zeros((128, 128), np.float32)))
            msk[:, j * 512 + s4 * 128:j * 512 + (s4 + 1) * 128] = blk
    ho = (c % 2) * 8
    qs, ks, vs, qbs, kbs, vbs, qss, kss, qsb, ksb = [], [], [], [], [], [], [], [], [], []
    for g in range(4):
        cols0 = (ho + 2 * g) * HD
        idx = cols0 + base; idxs = cols0 + swap
        qs.append(qkv_w[:, idx]); qss.append(qkv_w[:, idxs])
        ks.append(qkv_w[:, D + idx]); kss.append(qkv_w[:, D + idxs])
        vs.append(qkv_w[:, 2 * D + cols0:2 * D + cols0 + 128])
        qbs.append(qkv_b[idx]); qsb.append(qkv_b[idxs])
        kbs.append(qkv_b[D + idx]); ksb.append(qkv_b[D + idxs])
        vbs.append(qkv_b[2 * D + cols0:2 * D + cols0 + 128])
    return dict(
        c2_wq=_f32(np.concatenate(qs, 1)), c2_wqs=_f32(np.concatenate(qss, 1)),
        c2_wk=_f32(np.concatenate(ks, 1)), c2_wks=_f32(np.concatenate(kss, 1)),
        c2_wv=_f32(np.concatenate(vs, 1)),
        c2_bq=_bf(np.concatenate(qbs).reshape(1, -1)), c2_bqs=_bf(np.concatenate(qsb).reshape(1, -1)),
        c2_bk=_bf(np.concatenate(kbs).reshape(1, -1)), c2_bks=_bf(np.concatenate(ksb).reshape(1, -1)),
        c2_bv=_bf(np.concatenate(vbs).reshape(1, -1)),
        c2_cosf=_bf(cosf), c2_sins=_bf(sins), c2_msk=_bf(msk))


def build_fused():
    nc = bass.Bass("TRN2", target_bir_lowering=False, debug=False)
    dr = lambda n, s, dt, k: nc.dram_tensor(n, s, dt, kind=k).ap()
    x_in = dr("x", [TH, D], F32, "ExternalInput")
    ident = dr("ident", [128, 128], BF16, "ExternalInput")
    identf = dr("identf", [128, 128], F32, "ExternalInput")
    o_x4 = dr("o_x4", [TH, D], F32, "ExternalOutput")
    d_ln0 = dr("d_ln0", [TH, D], F32, "Internal")
    d_x1 = dr("d_x1", [TH, D], F32, "Internal")
    d_x2 = dr("d_x2", [TH, D], F32, "Internal")
    d_x3 = dr("d_x3", [TH, D], F32, "Internal")
    d_ln3f = dr("d_ln3f", [D, TH], BF16, "Internal")
    ag1_in = nc.dram_tensor("ag1_in", [D, TH], BF16, kind="Internal")
    ag1_out = nc.dram_tensor("ag1_out", [2 * D, TH], BF16, kind="Internal")
    ag2_in = nc.dram_tensor("ag2_in", [D, T], BF16, kind="Internal")
    ag2_out = nc.dram_tensor("ag2_out", [2 * D, T], BF16, kind="Internal")
    ag3_in = nc.dram_tensor("ag3_in", [D, TH], BF16, kind="Internal")
    ag3_out = nc.dram_tensor("ag3_out", [2 * D, TH], BF16, kind="Internal")
    ag4_in = nc.dram_tensor("ag4_in", [512, T], BF16, kind="Internal")
    ag4_out = nc.dram_tensor("ag4_out", [D, T], BF16, kind="Internal")

    ins = {}
    def mkin(pref, name, shp, dt):
        ins[pref + name] = dr(pref + name, shp, dt, "ExternalInput")
        return ins[pref + name]

    ncp = NCProxy(nc)
    with ExitStack() as ctx:
        p = Plan()
        pid = nc.partition_id(engines=[mybir.EngineType.SP])
        tok0 = (pid % 2) * TH

        io1 = dict(x=x_in, ident=ident, ln0=d_ln0, hf=ag1_in.ap())
        with ExitStack() as pc:
            t1 = emit_a1(ncp, p, pc, io1)
        p.wait("gpsimd", t1)
        cc1 = p.custom("gpsimd", lambda e: e.collective_compute(
            "AllGather", ALU.bypass, replica_groups=PAIRS,
            ins=[ag1_in.ap().opt()], outs=[ag1_out.ap().opt()]), "cc1")
        p.barrier()
        p.wait("sync", cc1)

        if FUSE_UPTO < 2:
            p.wait("sync", p.tok("sync"))
            p.emit(nc, ctx)
            return nc
        ag1o = ag1_out.ap()
        ag2i = ag2_in.ap()
        io2 = {}
        for nmm, shp, dt in (("wq", [D, 512], F32), ("wk", [D, 512], F32), ("wv", [D, 512], F32),
                             ("nw", [128, 8], F32), ("wqf2", [128, FEAT], BF16),
                             ("wkf2", [128, FEAT], BF16), ("s_iu", [64, NQUAD], BF16),
                             ("s_ju", [64, NQUAD], BF16), ("tri01", [128, 128], BF16),
                             ("winm", [128, 256], BF16), ("winm0", [128, 256], BF16)):
            io2[nmm] = mkin("a2_", nmm, shp, dt)
        io2["h_ld"] = lambda kc, r: ag1o[r * D + kc * 128: r * D + (kc + 1) * 128, :]
        io2["lin_dst"] = lambda hq: ag2i[hq * 64:(hq + 1) * 64, :]
        io2["win_dst"] = lambda hq: ag2i[512 + hq * 64:512 + (hq + 1) * 64, :]
        with ExitStack() as pc:
            t2 = emit_a2(ncp, p, pc, io2)
        p.wait("gpsimd", t2)
        cc2 = p.custom("gpsimd", lambda e: e.collective_compute(
            "AllGather", ALU.bypass, replica_groups=PAIRS,
            ins=[ag2_in.ap().opt()], outs=[ag2_out.ap().opt()]), "cc2")
        p.barrier()
        p.wait("sync", cc2)

        if FUSE_UPTO < 3:
            p.wait("sync", p.tok("sync"))
            p.emit(nc, ctx)
            return nc
        ag2o = ag2_out.ap()
        io3 = dict(
            wout=mkin("a3_", "wout", [2 * D, D], F32),
            ln0=d_ln0, x0=x_in, x1=d_x1,
            cat_ld=lambda kc: ag2o[kc * 128:(kc + 1) * 128, ds(tok0, TH)],
        )
        with ExitStack() as pc:
            emit_a3(ncp, p, pc, io3)
        p.barrier()

        if FUSE_UPTO < 4:
            p.wait("sync", p.tok("sync"))
            p.emit(nc, ctx)
            return nc
        iob = dict(
            x1=d_x1, x2=d_x2, ln2f=ag3_in.ap(),
            w1t=mkin("b_", "w1t", [D, 4096], F32),
            b1=mkin("b_", "b1", [1, 4096], F32),
            w2t=mkin("b_", "w2t", [4096, D], F32),
            b2=mkin("b_", "b2", [1, D], F32),
            ident=ident, identf=identf)
        if FUSED_DBG:
            iob["dbg_gs"] = dr("bd_gs", [1, 8], F32, "ExternalOutput")
            iob["dbg_wq"] = dr("bd_wq", [128, 4096], BF16, "ExternalOutput")
            iob["dbg_xq"] = dr("bd_xq", [128, D], BF16, "ExternalOutput")
            iob["dbg_h"] = dr("bd_h", [128, 4096], BF16, "ExternalOutput")
            iob["dbg_sc"] = dr("bd_sc", [128, 16], F32, "ExternalOutput")
        with ExitStack() as pc:
            t3 = emit_b(ncp, p, pc, iob)
        p.wait("gpsimd", t3)
        cc3 = p.custom("gpsimd", lambda e: e.collective_compute(
            "AllGather", ALU.bypass, replica_groups=PAIRS,
            ins=[ag3_in.ap().opt()], outs=[ag3_out.ap().opt()]), "cc3")
        p.barrier()
        p.wait("sync", cc3)

        if FUSE_UPTO < 5:
            if FUSED_DBG:
                dbg = {"g_ln0": d_ln0, "g_x1": d_x1, "g_x2": d_x2}
                for nmm, ap in dbg.items():
                    o = dr(nmm, [TH, D], F32, "ExternalOutput")
                    p.dma(lambda e, o=o, ap=ap: e.dma_start(out=o, in_=ap))
            p.wait("sync", p.tok("sync"))
            p.emit(nc, ctx)
            return nc
        ag3o = ag3_out.ap()
        ag4i = ag4_in.ap()
        ioc = {}
        for nmm, shp, dt in (("wq", [D, 512], F32), ("wqs", [D, 512], F32), ("wk", [D, 512], F32),
                             ("wks", [D, 512], F32), ("wv", [D, 512], F32),
                             ("bq", [1, 512], BF16), ("bqs", [1, 512], BF16), ("bk", [1, 512], BF16),
                             ("bks", [1, 512], BF16), ("bv", [1, 512], BF16),
                             ("cosf", [128, T], BF16), ("sins", [128, T], BF16),
                             ("msk", [128, 4 * 512], BF16)):
            ioc[nmm] = mkin("c2_", nmm, shp, dt)
        ioc["ln_ld"] = lambda kc, tt: ag3o[(tt // 4) * D + kc * 128:(tt // 4) * D + (kc + 1) * 128,
                                           (tt % 4) * 512:(tt % 4 + 1) * 512]
        ioc["o_dst"] = lambda hq: ag4i[hq * 64:(hq + 1) * 64, :]
        with ExitStack() as pc:
            t4 = emit_c2(ncp, p, pc, ioc)
        p.wait("gpsimd", t4)
        cc4 = p.custom("gpsimd", lambda e: e.collective_compute(
            "AllGather", ALU.bypass, replica_groups=PAIRS,
            ins=[ag4_in.ap().opt()], outs=[ag4_out.ap().opt()]), "cc4")
        p.barrier()
        p.wait("sync", cc4)

        if FUSE_UPTO < 6:
            p.wait("sync", p.tok("sync"))
            p.emit(nc, ctx)
            return nc
        ag4o = ag4_out.ap()
        ioc3 = dict(
            outw=mkin("c3_", "outw", [D, D], F32),
            outb=mkin("c3_", "outb", [1, D], BF16),
            ident=ident, x2=d_x2, x3=d_x3, ln3f=d_ln3f,
            o_ld=lambda kc: ag4o[kc * 128:(kc + 1) * 128, ds(tok0, TH)],
        )
        with ExitStack() as pc:
            emit_c3(ncp, p, pc, ioc3)
        p.barrier()

        if FUSE_UPTO < 7:
            p.wait("sync", p.tok("sync"))
            p.emit(nc, ctx)
            return nc
        d_ln3fr = d_ln3f.rearrange("(kc p) t -> p kc t", p=128)
        iod = dict(
            rout=mkin("d_", "rout", [D, 8], F32),
            wg=mkin("d_", "wg", [8, D, 512], F32),
            wu=mkin("d_", "wu", [8, D, 512], F32),
            wd=mkin("d_", "wd", [8, 512, D], F32),
            sg=mkin("d_", "sg", [D, D], F32),
            su=mkin("d_", "su", [D, D], F32),
            sd=mkin("d_", "sd", [D, D], F32),
            sel8=mkin("d_", "sel8", [8, 8 * 128], BF16),
            identf=identf, x3=d_x3, x4=o_x4,
            xf_ld=lambda kc: d_ln3fr[:, kc, :],
        )
        if FUSED_DBG:
            iod["dbg_acc"] = dr("gd_acc", [128, NT * D], F32, "ExternalOutput")
        with ExitStack() as pc:
            emit_d(ncp, p, pc, iod)
        if FUSED_DBG:
            o = dr("g_ln3f", [D, TH], BF16, "ExternalOutput")
            p.dma(lambda e, o=o: e.dma_start(out=o, in_=d_ln3f))
            dbg = {"g_ln0": d_ln0, "g_x1": d_x1, "g_x2": d_x2, "g_x3": d_x3}
            for nmm, ap in dbg.items():
                o = dr(nmm, [TH, D], F32, "ExternalOutput")
                p.dma(lambda e, o=o, ap=ap: e.dma_start(out=o, in_=ap))
            for nmm, t in (("g_ag1", ag1_out), ("g_ag2", ag2_out), ("g_ag3", ag3_out),
                           ("g_ag4", ag4_out)):
                shp = list(t.ap().shape)
                o = dr(nmm, shp, BF16, "ExternalOutput")
                p.dma(lambda e, o=o, t=t: e.dma_start(out=o, in_=t.ap()))
        p.wait("sync", p.tok("sync"))
        p.emit(nc, ctx)
    return nc



def build_only_b():
    nc = bass.Bass("TRN2", target_bir_lowering=False, debug=False)
    dr = lambda n, s, dt, k: nc.dram_tensor(n, s, dt, kind=k).ap()
    ncp = NCProxy(nc)
    import os
    use_proxy = os.environ.get("OB_PROXY", "1") == "1"
    close_pc = os.environ.get("OB_CLOSE", "1") == "1"
    with ExitStack() as ctx:
        p = Plan()
        iob = dict(
            x1=dr("xb1", [TH, D], F32, "ExternalInput"),
            x2=dr("o_x2", [TH, D], F32, "ExternalOutput"),
            ln2f=dr("o_ln2f", [D, TH], BF16, "ExternalOutput"),
            w1t=dr("b_w1t", [D, 4096], F32, "ExternalInput"),
            b1=dr("b_b1", [1, 4096], F32, "ExternalInput"),
            w2t=dr("b_w2t", [4096, D], F32, "ExternalInput"),
            b2=dr("b_b2", [1, D], F32, "ExternalInput"),
            ident=dr("ident", [128, 128], BF16, "ExternalInput"),
            identf=dr("identf", [128, 128], F32, "ExternalInput"),
            dbg_gs=dr("bd_gs", [1, 8], F32, "ExternalOutput"),
            dbg_wq=dr("bd_wq", [128, 4096], BF16, "ExternalOutput"),
            dbg_xq=dr("bd_xq", [128, D], BF16, "ExternalOutput"),
            dbg_h=dr("bd_h", [128, 4096], BF16, "ExternalOutput"),
            dbg_sc=dr("bd_sc", [128, 16], F32, "ExternalOutput"),
        )
        tgt = ncp if use_proxy else nc
        if close_pc:
            with ExitStack() as pc:
                emit_b(tgt, p, pc, iob)
        else:
            emit_b(tgt, p, ctx, iob)
        p.wait("sync", p.tok("sync"))
        p.emit(nc, ctx)
    return nc


def prep_fused(inz):
    s_iu_m, s_ju_m, tri, winm, winm0, nw, wqf2, wkf2 = _a2_consts(inz)
    idb = np.eye(128, dtype=np.float32)
    sel8 = np.zeros((8, 8 * 128), np.float32)
    for e in range(8):
        sel8[e, e * 128:(e + 1) * 128] = 1.0
    wq, wk, wv = inz["L0_attn_Wq"], inz["L0_attn_Wk"], inz["L0_attn_Wv"]
    wout = np.asarray(inz["L0_attn_Wout"])
    perm = np.concatenate([np.arange(0, 512), np.arange(1024, 1536),
                           np.arange(512, 1024), np.arange(1536, 2048)])
    wout_p = wout[perm]
    x0 = np.asarray(inz["L0_attn_x"])
    in_maps = []
    for c in range(NCORES):
        b, ho = c // 2, (c % 2) * 8
        cols = slice(ho * HD, (ho + 8) * HD)
        m = dict(
            x=_f32(x0[b, (c % 2) * TH:(c % 2 + 1) * TH]),
            ident=_bf(idb), identf=_f32(idb),
            a2_wq=_f32(np.asarray(wq)[:, cols]), a2_wk=_f32(np.asarray(wk)[:, cols]),
            a2_wv=_f32(np.asarray(wv)[:, cols]), a2_nw=_f32(nw),
            a2_wqf2=_bf(wqf2), a2_wkf2=_bf(wkf2), a2_s_iu=_bf(s_iu_m), a2_s_ju=_bf(s_ju_m),
            a2_tri01=_bf(tri), a2_winm=_bf(winm), a2_winm0=_bf(winm0),
            a3_wout=_f32(wout_p),
            b_w1t=_f32(np.asarray(inz["L0_ffn_w1"]).T),
            b_b1=_f32(np.asarray(inz["L0_ffn_b1"]).reshape(1, -1)),
            b_w2t=_f32(np.asarray(inz["L0_ffn_w2"]).T),
            b_b2=_f32(np.asarray(inz["L0_ffn_b2"]).reshape(1, -1)),
            c3_outw=_f32(inz["L1_attn_out_w"]),
            c3_outb=_bf(np.asarray(inz["L1_attn_out_b"]).reshape(1, -1)),
            d_rout=_f32(inz["L1_ffn_router"]),
            d_wg=_f32(inz["L1_ffn_experts_gate"]), d_wu=_f32(inz["L1_ffn_experts_up"]),
            d_wd=_f32(inz["L1_ffn_experts_down"]),
            d_sg=_f32(inz["L1_ffn_shared_gate"]), d_su=_f32(inz["L1_ffn_shared_up"]),
            d_sd=_f32(inz["L1_ffn_shared_down"]), d_sel8=_bf(sel8),
        )
        m.update(_c2_consts(inz, c))
        in_maps.append(m)
    return in_maps


FUSED = True
FUSED_DBG = False
FUSE_UPTO = 9


def kernel(**inputs):
    if FUSED:
        nc = _get("fused", build_fused)
        in_maps = prep_fused(inputs)
        res = run_bass_kernel_spmd(nc, in_maps, core_ids=list(range(NCORES)))
        return unshard_tokens([np.asarray(r["o_x4"], np.float32) for r in res.results])
    return kernel_unfused(**inputs)


def kernel_unfused(**inputs):
    x0 = np.asarray(inputs["L0_attn_x"])
    shards = shard_tokens(x0)
    ln_s, h_s = stage_a1(shards)
    lin_s, win_s = stage_a2(inputs, h_s)
    x1_s = stage_a3(inputs, lin_s, win_s, ln_s, shards)
    x2_s, ln2_s = stage_b(inputs, x1_s)
    att_s = stage_c2(inputs, ln2_s)
    x3_s, ln3_s = stage_c3(inputs, att_s, x2_s)
    x4_s = stage_d(inputs, ln3_s, x3_s)
    return unshard_tokens([np.asarray(s, np.float32) for s in x4_s])


if __name__ == "__main__":
    pass


# version 10 (final)
# speedup vs baseline: 1.0502x; 1.0502x over previous
import math
from contextlib import ExitStack

import numpy as np
import ml_dtypes

import concourse.bass as bass
import concourse.mybir as mybir
from concourse.bass_utils import run_bass_kernel_spmd

F32 = mybir.dt.float32
BF16 = mybir.dt.bfloat16
AF = mybir.ActivationFunctionType
ALU = mybir.AluOpType
AX = mybir.AxisListType
ds = bass.ds

NCORES = 8
B, T, D = 4, 4096, 1024
TH = T // 2
NT = TH // 128
H, HD = 16, 64
HPC = 8
FEAT = 16
NQUAD = 136

BF = ml_dtypes.bfloat16


def _bf(x):
    return np.ascontiguousarray(np.asarray(x).astype(BF))


def _f32(x):
    return np.ascontiguousarray(np.asarray(x).astype(np.float32))


ENGINES = ("sync", "gpsimd", "tensor", "scalar", "vector")


class Plan:
    def __init__(self):
        self.items = {e: [] for e in ENGINES}
        self.cnt = {e: 0 for e in ENGINES}

    def op(self, eng, fn, dep=None, inc=None, amount=None):
        if amount is None:
            amount = 16 if eng in ("sync",) else 1
        if dep is None:
            dep = eng not in ("tensor",)
        if inc is None:
            inc = eng not in ("tensor",)
        if not inc:
            self.items[eng].append(("op", fn, dep, False, 0))
            return (eng, self.cnt[eng])
        self.cnt[eng] += amount
        self.items[eng].append(("op", fn, dep, True, amount))
        return (eng, self.cnt[eng])

    def dma(self, fn, eng="sync", dep=False):
        return self.op(eng, fn, dep=dep, inc=True, amount=16)

    def wait(self, eng, token):
        if token is None:
            return
        teng, tval = token
        if tval <= 0:
            return
        self.items[eng].append(("wait", teng, tval))

    def tok(self, eng):
        return (eng, self.cnt[eng])

    def custom(self, eng, fn, sem):
        self.items[eng].append(("cop", fn, sem))
        return (sem, 1)

    def barrier(self):
        toks = {e: self.tok(e) for e in ENGINES}
        for e in ENGINES:
            for o in ENGINES:
                if o != e and toks[o][1] > 0:
                    self.wait(e, toks[o])

    def emit(self, nc, ctx):
        sems = {e: ctx.enter_context(nc.semaphore("S" + e)) for e in ENGINES}
        csems = {}
        for e in ENGINES:
            for it in self.items[e]:
                if it[0] == "cop" and it[2] not in csems:
                    csems[it[2]] = ctx.enter_context(nc.semaphore(it[2]))
        self.csems = csems
        block = ctx.enter_context(nc.Block())

        def mk_body(ename):
            items = self.items[ename]

            def body(eng):
                done = 0
                for it in items:
                    if it[0] == "wait":
                        _, teng, tval = it
                        eng.wait_ge(sems.get(teng) or csems[teng], tval)
                    elif it[0] == "cop":
                        _, fn, semn = it
                        inst = fn(eng)
                        inst.then_inc(csems[semn], 1)
                    else:
                        _, fn, dep, inc, amount = it
                        if dep and done > 0:
                            eng.wait_ge(sems[ename], done)
                        inst = fn(eng)
                        if inc:
                            inst.then_inc(sems[ename], amount)
                            done += amount
            return body

        for ename in ENGINES:
            if not self.items[ename]:
                continue
            getattr(block, ename)(mk_body(ename))


def emit_a1(nc, p, ctx, io):
    xt = io["x"].rearrange("(n p) d -> n p d", p=128)
    lnt = io["ln0"].rearrange("(n p) d -> n p d", p=128)
    hfr = io["hf"].rearrange("(kc p) t -> p kc t", p=128)
    ident = io["ident"]
    with ExitStack() as c2:
        sb = lambda n, s, dt=F32: c2.enter_context(nc.sbuf_tensor(n, s, dt))
        sx = sb("sxA1", [128, D])
        sqd = sb("sqdA1", [128, D])
        scr = sb("scrA1", [128, 16])
        sln = sb("slnA1", [128, D])
        sh = sb("shA1", [128, D], BF16)
        s_id = sb("s_idA1", [128, 128], BF16)
        hf = sb("hfA1", [128, 8 * TH], BF16)
        PT = c2.enter_context(nc.psum_tensor("PTa1", [128, 128], BF16))
        tid = p.dma(lambda e: e.dma_start(out=s_id[:], in_=ident))
        p.wait("tensor", tid)
        t_sh = None
        t_st = None
        for i in range(NT):
            p.wait("sync", t_sh)
            tl = p.dma(lambda e, i=i: e.dma_start(out=sx[:], in_=xt[i]), dep=True)
            tb = plan_ln_scalars(p, sx[:], D, scr, sqd[:], 1e-5, dep_tok=tl)
            p.op("vector", lambda e: e.tensor_tensor(scr[:, 9:10], scr[:, 4:5], scr[:, 5:6], ALU.mult))
            p.op("vector", lambda e: e.tensor_scalar_add(scr[:, 9:10], scr[:, 9:10], 1e-6))
            tv = p.op("vector", lambda e: e.reciprocal(scr[:, 9:10], scr[:, 9:10]))
            p.wait("scalar", tv)
            ts = p.op("scalar", lambda e: e.activation(scr[:, 9:10], scr[:, 9:10], AF.Sqrt))
            p.wait("vector", ts)
            p.op("vector", lambda e: e.tensor_tensor(scr[:, 10:11], scr[:, 6:7], scr[:, 9:10], ALU.mult))
            tg = p.op("vector", lambda e: e.scalar_tensor_tensor(
                scr[:, 11:12], scr[:, 2:3], -1.0, scr[:, 10:11], ALU.mult, ALU.mult))
            p.wait("scalar", tb)
            p.wait("scalar", t_st)
            ta = p.op("scalar", lambda e: e.activation(
                sln[:], sx[:], AF.Identity, bias=scr[:, 7:8], scale=scr[:, 6:7]), dep=True)
            p.wait("scalar", tg)
            t_sh = p.op("scalar", lambda e: e.activation(
                sh[:], sx[:], AF.Identity, bias=scr[:, 11:12], scale=scr[:, 10:11]), dep=True)
            p.wait("sync", ta)
            if t_st is not None:
                p.wait("sync", t_st)
            t_st = p.dma(lambda e, i=i: e.dma_start(out=lnt[i], in_=sln[:]), dep=True)
            p.wait("tensor", t_sh)
            for kc in range(8):
                tt = p.op("tensor", lambda e, kc=kc: e.transpose(
                    PT[:], sh[:, kc * 128:(kc + 1) * 128], s_id[:]), inc=True)
                p.wait("scalar", tt)
                tcp = p.op("scalar", lambda e, kc=kc, i=i: e.copy(
                    hf[:, kc * TH + i * 128:kc * TH + (i + 1) * 128], PT[:]), dep=True)
                p.wait("tensor", tcp)
            t_sh = tcp
        p.wait("sync", t_sh)
        t_h = None
        for kc in range(8):
            t_h = p.dma(lambda e, kc=kc: e.dma_start(
                out=hfr[:, kc, :], in_=hf[:, kc * TH:(kc + 1) * TH]), dep=False)
    return t_h


def build_a1():
    nc = bass.Bass("TRN2", target_bir_lowering=False, debug=False)
    io = dict(
        x=nc.dram_tensor("x", [TH, D], F32, kind="ExternalInput").ap(),
        ident=nc.dram_tensor("ident", [128, 128], BF16, kind="ExternalInput").ap(),
        ln0=nc.dram_tensor("o_ln", [TH, D], F32, kind="ExternalOutput").ap(),
        hf=nc.dram_tensor("o_hf", [D, TH], BF16, kind="ExternalOutput").ap(),
    )
    with ExitStack() as ctx:
        p = Plan()
        emit_a1(nc, p, ctx, io)
        p.wait("sync", p.tok("sync"))
        p.emit(nc, ctx)
    return nc


_CACHE = {}


def _get(name, builder):
    if name not in _CACHE:
        _CACHE[name] = builder()
    return _CACHE[name]


def stage_a1(x_shards):
    nc = _get("a1", build_a1)
    idb = np.eye(128, dtype=np.float32)
    res = run_bass_kernel_spmd(nc, [{"x": s, "ident": _bf(idb)} for s in x_shards],
                               core_ids=list(range(NCORES)))
    return ([r["o_ln"] for r in res.results], [r["o_hf"] for r in res.results])


def shard_tokens(x):
    return [np.ascontiguousarray(x[c // 2, (c % 2) * TH:(c % 2 + 1) * TH])
            for c in range(NCORES)]


def unshard_tokens(shards):
    out = np.empty((B, T, shards[0].shape[-1]), dtype=np.asarray(shards[0]).dtype)
    for c in range(NCORES):
        out[c // 2, (c % 2) * TH:(c % 2 + 1) * TH] = shards[c]
    return out



A2_NHEADS = 8
A2_PARTS = ("scan", "win")
A2_PHASES = ("fold", "proj", "cb")
A2_TT = 8
A2_QK = True
A2_V = True


def emit_a2(nc, p, ctx, io):
    wq_d = io["wq"]; wk_d = io["wk"]; wv_d = io["wv"]; nw_d = io["nw"]
    wqf_d = io["wqf2"]; wkf_d = io["wkf2"]; siu_d = io["s_iu"]; sju_d = io["s_ju"]
    tri_d = io["tri01"]; winm_d = io["winm"]; winm0_d = io["winm0"]
    h_ld = io["h_ld"]; lin_dst = io["lin_dst"]; win_dst = io["win_dst"]

    NC2 = T // 256
    if True:
        sb = lambda n, s, dt=BF16: ctx.enter_context(nc.sbuf_tensor(n, s, dt))
        ps = lambda n, s: ctx.enter_context(nc.psum_tensor(n, s, F32))
        s_wqf = sb("s_wqf", [128, FEAT]); s_wkf = sb("s_wkf", [128, FEAT])
        s_iu = sb("ss_iu", [64, NQUAD]); s_ju = sb("ss_ju", [64, NQUAD])
        s_tri = sb("s_tri", [128, 128]); s_wm = sb("s_wm", [128, 256]); s_wm0 = sb("s_wm0", [128, 256])
        qfm = sb("qfm", [128, 4 * T]); kfm = sb("kfm", [128, 4 * T])
        vsb = sb("vsb", [128, 32 * 520])
        vstg = sb("vstg", [128, 512])
        ones_b = sb("ones_b", [128, 128])
        onesf = ctx.enter_context(nc.sbuf_tensor("onesf", [128, 128], F32))
        md = ctx.enter_context(nc.sbuf_tensor("md", [128, 16], F32))
        cbuf = ctx.enter_context(nc.sbuf_tensor("cbuf", [1, 32], F32))
        negc = ctx.enter_context(nc.sbuf_tensor("negc", [128, 8], F32))
        rz = ctx.enter_context(nc.sbuf_tensor("rz", [128, 4], F32))

        ctx2 = ExitStack()
        sb2 = lambda n, s, dt=BF16: ctx2.enter_context(nc.sbuf_tensor(n, s, dt))
        s_h = sb2("s_h", [128, 8 * T])
        s_wq = sb2("s_wq", [128, 8 * 512]); s_wk = sb2("s_wk", [128, 8 * 512]); s_wv = sb2("s_wv", [128, 8 * 512])
        s_nw = ctx2.enter_context(nc.sbuf_tensor("s_nw", [128, 8], F32))

        PX = ps("PX", [128, 512]); PS = ps("PS", [128, 512])
        PG = ps("PG", [128, 512]); PH = ps("PH", [128, 512])
        PY = ps("PY", [128, 512]); PKV = ps("PKV", [128, 260])
        PB2 = ps("PB2", [128, 512])

        for kc in range(8):
            for r in range(2):
                t_h = p.dma(lambda e, kc=kc, r=r: e.dma_start(
                    out=s_h[:, kc * T + r * TH:kc * T + (r + 1) * TH], in_=h_ld(kc, r)))
        t_w = []
        for w_sb, w_d in ((s_wq, wq_d), (s_wk, wk_d), (s_wv, wv_d)):
            w_r = w_d.rearrange("(kc p) c -> p kc c", p=128)
            for kc in range(8):
                t_w.append(p.op("gpsimd", lambda e, a=w_sb, w_r=w_r, kc=kc: e.dma_start(
                    out=a[:, kc * 512:(kc + 1) * 512], in_=w_r[:, kc, :]), dep=False, amount=16))
        smalls = [(s_nw, nw_d), (s_wqf, wqf_d), (s_wkf, wkf_d), (s_iu, siu_d),
                  (s_ju, sju_d), (s_tri, tri_d), (s_wm, winm_d), (s_wm0, winm0_d)]
        t_small = None
        for a, b in smalls:
            t_small = p.dma(lambda e, a=a, b=b: e.dma_start(out=a[:], in_=b))
        t_ones = p.op("vector", lambda e: e.memset(ones_b[:], 1.0), dep=False)
        t_onesf = p.op("vector", lambda e: e.memset(onesf[:], 1.0), dep=False)
        t_vinit = p.op("vector", lambda e: e.memset(vsb[:], 1.0), dep=False)
        p.wait("vector", t_w[-1])
        p.wait("vector", t_small)
        t_fold = None
        for w_sb in ((s_wq, s_wk, s_wv) if "fold" in A2_PHASES else ()):
            for kc in range(8):
                t_fold = p.op("vector", lambda e, w=w_sb, kc=kc: e.tensor_scalar_mul(
                    w[:, kc * 512:(kc + 1) * 512], w[:, kc * 512:(kc + 1) * 512],
                    s_nw[:, kc:kc + 1]), dep=False)
        p.wait("tensor", t_h)
        p.wait("tensor", t_fold)
        p.wait("tensor", t_vinit)
        pq = [PX, PS, PG, PH]
        ev_tok = [None, None, None, None]
        pi = 0
        for tt in range(A2_TT if "proj" in A2_PHASES else 0):
            for hp in range(4 if A2_QK else 0):
                for which, wsb, dst in (("q", s_wq, qfm), ("k", s_wk, kfm)):
                    pb = pq[pi % 4]
                    if ev_tok[pi % 4] is not None:
                        p.wait("tensor", ev_tok[pi % 4])
                    for kc in range(8):
                        tmm = p.op("tensor", lambda e, pb=pb, wsb=wsb, kc=kc, hp=hp, tt=tt: e.matmul(
                            pb[:], wsb[:, kc * 512 + hp * 128:kc * 512 + (hp + 1) * 128],
                            s_h[:, kc * T + tt * 512:kc * T + (tt + 1) * 512],
                            start=(kc == 0), stop=(kc == 7)), inc=(kc == 7))
                    p.wait("scalar", tmm)
                    ev_tok[pi % 4] = p.op("scalar", lambda e, pb=pb, dst=dst, hp=hp, tt=tt: e.copy(
                        dst[:, hp * T + tt * 512:hp * T + (tt + 1) * 512], pb[:]), dep=False)
                    pi += 1
            for t4 in range(4 if A2_V else 0):
                ck = tt * 4 + t4
                pb = pq[pi % 4]
                if ev_tok[pi % 4] is not None:
                    p.wait("tensor", ev_tok[pi % 4])
                for kc in range(8):
                    tmm = p.op("tensor", lambda e, pb=pb, kc=kc, ck=ck: e.matmul(
                        pb[:], s_h[:, kc * T + ck * 128:kc * T + (ck + 1) * 128],
                        s_wv[:, kc * 512:(kc + 1) * 512],
                        start=(kc == 0), stop=(kc == 7)), inc=(kc == 7))
                p.wait("scalar", tmm)
                tstg = p.op("scalar", lambda e, pb=pb: e.copy(vstg[:], pb[:]), dep=True)
                p.wait("vector", tstg)
                vview = vsb[:].rearrange("p (c h ee) -> p c h ee", c=32, h=8)
                tvs = p.op("vector", lambda e, ck=ck, vv=vview: e.tensor_copy(
                    vv[:, ck, :, 0:64], vstg[:].rearrange("p (h ee) -> p h ee", h=8)), dep=False)
                ev_tok[pi % 4] = tstg
                p.wait("scalar", tvs)
                pi += 1
        t_proj_pe = p.tok("tensor")
        t_proj_sc = p.tok("scalar")
        t_proj_ve = p.tok("vector")
        ctx2.close()
        qf_sb = sb("qf_sb", [16, 2 * T]); kf_sb = sb("kf_sb", [16, 2 * T])
        phik = sb("phik", [128, 32 * NQUAD]); kp_sb = sb("kp_sb", [128, 32 * 16])
        Asb = sb("Asb", [128, 384])
        Q2qa = sb("Q2qa", [128, 256]); Q2qb = sb("Q2qb", [8, 256])
        Gsb = sb("Gsb", [128, 256])
        KVsb = sb("KVsb", [128, 260])
        ptw = sb("ptw", [128, 256])
        ptw2 = sb("ptw2A", [128, 256])
        lin_sbs = [sb("lin_sb0", [64, T]), sb("lin_sb1", [64, T])]
        win_sbs = [sb("win_sb0", [64, T]), sb("win_sb1", [64, T])]
        rzf = ctx.enter_context(nc.sbuf_tensor("rzfA", [128, 512], F32))
        bsbA = sb("bsbA", [65, 512])
        p.wait("vector", t_proj_sc)
        t_md = None
        for cg in range(4 if "cb" in A2_PHASES else 0):
            t_md = p.op("vector", lambda e, cg=cg: e.reduce_max(
                md[:, cg:cg + 1], qfm[:, cg * T:(cg + 1) * T], axis=AX.X,
                apply_absolute_value=True), dep=False)
            t_md = p.op("vector", lambda e, cg=cg: e.reduce_max(
                md[:, 4 + cg:5 + cg], kfm[:, cg * T:(cg + 1) * T], axis=AX.X,
                apply_absolute_value=True), dep=False)
        p.wait("scalar", t_md)
        if "cb" not in A2_PHASES:
            p.wait("sync", p.tok("sync"))
            p.emit(nc, ctx)
            return nc
        t_md2 = p.op("scalar", lambda e: e.activation(md[:, 8:16], md[:, 0:8], AF.Square))
        p.wait("tensor", t_md2)
        p.wait("tensor", t_onesf)
        for h in range(8):
            cg, r = h // 2, 64 * (h % 2)
            p.op("tensor", lambda e, cg=cg, r=r, h=h: e.matmul(
                PY[0:1, h:h + 1], md[r:r + 64, 8 + cg:9 + cg], onesf[r:r + 64, 0:1],
                start=True, stop=True))
            t_cmm = p.op("tensor", lambda e, cg=cg, r=r, h=h: e.matmul(
                PY[0:1, 8 + h:9 + h], md[r:r + 64, 12 + cg:13 + cg], onesf[r:r + 64, 0:1],
                start=True, stop=True), inc=(h == 7))
        p.wait("vector", t_cmm)
        p.op("vector", lambda e: e.tensor_copy(cbuf[0:1, 0:16], PY[0:1, 0:16]))
        p.wait("scalar", p.tok("vector"))
        t_sq = p.op("scalar", lambda e: e.activation(cbuf[0:1, 16:32], cbuf[0:1, 0:16], AF.Sqrt))
        p.wait("vector", t_sq)
        p.op("vector", lambda e: e.tensor_tensor(
            cbuf[0:1, 0:8], cbuf[0:1, 16:24], cbuf[0:1, 24:32], ALU.mult))
        tnb = p.op("vector", lambda e: e.tensor_scalar(
            cbuf[0:1, 8:16], cbuf[0:1, 0:8], -0.125, 60.0, ALU.mult, ALU.add))
        p.wait("tensor", tnb)
        t_nc = p.op("tensor", lambda e: e.matmul(PY[:, 16:24], onesf[0:1, 0:128],
                                          cbuf[0:1, 8:16], start=True, stop=True), inc=True)
        p.wait("vector", t_nc)
        t_negc = p.op("vector", lambda e: e.tensor_copy(negc[:], PY[:, 16:24]))
        t_out_prev = None
        t_y_done = None
        t_lin_dma = None
        t_win_dma = None

        def emit_head(hq, t_y_done, t_lin_dma, t_win_dma):
            lin_sb = lin_sbs[hq % 2]; win_sb = win_sbs[hq % 2]
            hp, par = hq // 2, hq % 2
            r = 64 * par
            qcol = hp * T
            fcol = par * T
            vview = vsb[:].rearrange("p (c h ee) -> p c h ee", c=32, h=8)

            def vrow(s, hq=hq, vview=vview):
                return vview[:, s, hq, :]

            t_qf = None
            for t8 in range(8):
                p.op("tensor", lambda e, t8=t8: e.matmul(
                    PX[0:16, 0:512], s_wqf[r:r + 64, 0:FEAT],
                    qfm[r:r + 64, qcol + t8 * 512: qcol + (t8 + 1) * 512],
                    start=True, stop=True), dep=False)
                tq = p.op("tensor", lambda e, t8=t8: e.matmul(
                    PS[0:16, 0:512], s_wkf[r:r + 64, 0:FEAT],
                    kfm[r:r + 64, qcol + t8 * 512: qcol + (t8 + 1) * 512],
                    start=True, stop=True), dep=False, inc=True)
                p.wait("scalar", tq)
                t_qf = p.op("scalar", lambda e, t8=t8: e.copy(
                    qf_sb[0:16, fcol + t8 * 512: fcol + (t8 + 1) * 512], PX[0:16, 0:512]), dep=False)
                t_qf = p.op("scalar", lambda e, t8=t8: e.copy(
                    kf_sb[0:16, fcol + t8 * 512: fcol + (t8 + 1) * 512], PS[0:16, 0:512]), dep=False)
                p.wait("tensor", t_qf)
            t_phik = None
            for ck in range(32):
                kfs = kf_sb[0:16, fcol + ck * 128: fcol + (ck + 1) * 128]
                p.op("tensor", lambda e, kfs=kfs: e.matmul(
                    PG[:, 0:NQUAD], kfs, s_iu[0:16, :], start=True, stop=True), dep=False)
                p.op("tensor", lambda e, kfs=kfs: e.matmul(
                    PH[:, 0:NQUAD], kfs, s_ju[0:16, :], start=True, stop=True), dep=False)
                tmm = p.op("tensor", lambda e, ck=ck: e.matmul(
                    PG[:, 256:272], kfm[r:r + 64, qcol + ck * 128: qcol + (ck + 1) * 128],
                    s_wkf[r:r + 64, 0:FEAT], start=True, stop=True), dep=False, inc=True)
                p.wait("scalar", tmm)
                tg = p.op("scalar", lambda e: e.copy(Gsb[:, 0:NQUAD], PG[:, 0:NQUAD]), dep=True)
                tg2 = p.op("scalar", lambda e, ck=ck: e.copy(
                    kp_sb[:, ck * 16:(ck + 1) * 16], PG[:, 256:272]), dep=False)
                p.wait("vector", tg)
                t_phik = p.op("vector", lambda e, ck=ck: e.tensor_tensor(
                    phik[:, ck * NQUAD:(ck + 1) * NQUAD], Gsb[:, 0:NQUAD],
                    PH[:, 0:NQUAD], ALU.mult), dep=True)
                p.wait("tensor", t_phik)
                p.wait("tensor", tg2)
            t_kv_evac = p.op("vector", lambda e: e.memset(KVsb[:], 0.0), dep=True)
            t_y_done = None
            prev_tg2 = None
            prev_tq2 = None
            for c2 in range(NC2 if "scan" in A2_PARTS else 0):
                s0, s1 = 2 * c2, 2 * c2 + 1
                p.wait("tensor", prev_tg2)
                p.wait("tensor", prev_tq2)
                for (cols, ks, qs) in ((0, s0, s0), (128, s0, s1), (256, s1, s1)):
                    p.op("tensor", lambda e, cols=cols, ks=ks, qs=qs: e.matmul(
                        PS[:, cols:cols + 128],
                        kf_sb[0:16, fcol + ks * 128: fcol + (ks + 1) * 128],
                        qf_sb[0:16, fcol + qs * 128: fcol + (qs + 1) * 128],
                        start=True, stop=True), dep=False)
                qfs = qf_sb[0:16, fcol + c2 * 256: fcol + (c2 + 1) * 256]
                p.op("tensor", lambda e, qfs=qfs: e.matmul(
                    PG[:, 0:256], s_iu[0:16, 0:128], qfs, start=True, stop=True), dep=False)
                p.op("tensor", lambda e, qfs=qfs: e.matmul(
                    PG[0:8, 256:512], s_iu[0:16, 128:136], qfs, start=True, stop=True), dep=False)
                p.op("tensor", lambda e, qfs=qfs: e.matmul(
                    PH[:, 0:256], s_ju[0:16, 0:128], qfs, start=True, stop=True), dep=False)
                tb = p.op("tensor", lambda e, qfs=qfs: e.matmul(
                    PH[0:8, 256:512], s_ju[0:16, 128:136], qfs, start=True, stop=True),
                    dep=False, inc=True)
                p.wait("scalar", tb)
                ta1 = p.op("scalar", lambda e: e.activation(
                    Asb[:], PS[:, 0:384], AF.Square, bias=1.0, scale=0.5), dep=True)
                tg1 = p.op("scalar", lambda e: e.copy(Gsb[:, 0:256], PG[:, 0:256]), dep=False)
                tg2 = p.op("scalar", lambda e: e.copy(Q2qb[0:8, 0:256], PG[0:8, 256:512]), dep=False)
                p.wait("vector", ta1)
                tm1 = p.op("vector", lambda e: e.tensor_tensor(
                    Asb[:, 0:128], Asb[:, 0:128], s_tri[:], ALU.mult), dep=True)
                tm2 = p.op("vector", lambda e: e.tensor_tensor(
                    Asb[:, 256:384], Asb[:, 256:384], s_tri[:], ALU.mult), dep=False)
                p.wait("vector", tg1)
                tq1 = p.op("vector", lambda e: e.tensor_tensor(
                    Q2qa[:], Gsb[:, 0:256], PH[:, 0:256], ALU.mult), dep=True)
                p.wait("vector", tg2)
                tq2 = p.op("vector", lambda e: e.tensor_tensor(
                    Q2qb[0:8, 0:256], Q2qb[0:8, 0:256], PH[0:8, 256:512], ALU.mult), dep=True)
                prev_tg2 = tg2
                prev_tq2 = tq2
                for si, s in enumerate((s0, s1)):
                    p.wait("tensor", tm1 if si == 0 else tm2)
                    p.wait("tensor", tq1)
                    p.wait("tensor", tq2)
                    if si == 0:
                        p.wait("tensor", t_kv_evac)
                    if si == 0 and t_y_done is not None:
                        p.wait("tensor", t_y_done)
                    cs = si * 128
                    first = True
                    if si == 1:
                        p.op("tensor", lambda e, s0=s0, cs=cs: e.matmul(
                            PY[0:65, cs:cs + 128], vrow(s0), Asb[:, 128:256],
                            start=True, stop=False), dep=False)
                        first = False
                    diag = 0 if si == 0 else 256
                    p.op("tensor", lambda e, diag=diag, s=s, first=first, cs=cs: e.matmul(
                        PY[0:65, cs:cs + 128], vrow(s), Asb[:, diag:diag + 128],
                        start=first, stop=False), dep=False)
                    p.op("tensor", lambda e, si=si, cs=cs: e.matmul(
                        PY[0:65, cs:cs + 128], KVsb[:, 0:65], Q2qa[:, si * 128:(si + 1) * 128],
                        start=False, stop=False), dep=False)
                    p.op("tensor", lambda e, si=si, cs=cs: e.matmul(
                        PY[0:65, cs:cs + 128], KVsb[0:8, 65:130], Q2qb[0:8, si * 128:(si + 1) * 128],
                        start=False, stop=False), dep=False)
                    p.op("tensor", lambda e, s=s, cs=cs: e.matmul(
                        PY[0:65, cs:cs + 128], KVsb[0:16, 130:195],
                        qf_sb[0:16, fcol + s * 128: fcol + (s + 1) * 128],
                        start=False, stop=False), dep=False)
                    tylast = p.op("tensor", lambda e, cs=cs: e.matmul(
                        PY[0:65, cs:cs + 128], KVsb[0:1, 195:260], ones_b[0:1, 0:128],
                        start=False, stop=True), dep=False, inc=True)
                p.wait("vector", tylast)
                tz = p.op("vector", lambda e: e.tensor_scalar_add(
                    rzf[64:65, 0:256], PY[64:65, 0:256], 1e-6), dep=True)
                tz2 = p.op("vector", lambda e: e.reciprocal(
                    rzf[64:65, 0:256], rzf[64:65, 0:256]), dep=True)
                p.wait("tensor", tz2)
                tbm = p.op("tensor", lambda e: e.matmul(
                    PB2[0:65, 0:256], onesf[64:65, 0:65], rzf[64:65, 0:256],
                    start=True, stop=True), dep=False, inc=True)
                p.wait("scalar", tbm)
                tbe = p.op("scalar", lambda e: e.copy(bsbA[0:65, 0:256], PB2[0:65, 0:256]), dep=True)
                p.wait("vector", tbe)
                t_y_done = p.op("vector", lambda e, c2=c2: e.tensor_tensor(
                    lin_sb[0:64, c2 * 256:(c2 + 1) * 256], bsbA[0:64, 0:256],
                    PY[0:64, 0:256], ALU.mult), dep=True)
                p.wait("tensor", t_y_done)
                for si, s in enumerate((s0, s1)):
                    stf = (c2 == 0 and si == 0)
                    last = (c2 == NC2 - 1 and si == 1)
                    p.op("tensor", lambda e, s=s, stf=stf: e.matmul(
                        PKV[:, 0:65], phik[:, s * NQUAD: s * NQUAD + 128], vrow(s),
                        start=stf, stop=False), dep=False)
                    p.op("tensor", lambda e, s=s: e.matmul(
                        PKV[0:8, 65:130], phik[:, s * NQUAD + 128: s * NQUAD + 136], vrow(s),
                        start=False, stop=False), dep=False)
                    p.op("tensor", lambda e, s=s: e.matmul(
                        PKV[0:16, 130:195], kp_sb[:, s * 16:(s + 1) * 16], vrow(s),
                        start=False, stop=False), dep=False)
                    tkv = p.op("tensor", lambda e, s=s, last=last: e.matmul(
                        PKV[0:1, 195:260], ones_b[:, 0:1], vrow(s),
                        start=False, stop=last), dep=False, inc=(si == 1))
                p.wait("scalar", tkv)
                t_kv_evac = p.op("scalar", lambda e: e.copy(KVsb[:], PKV[:]), dep=True)
            t_w_done = t_y_done
            ps_freeW = [None, None]
            av_tokW = [None, None]
            pt_tokW = [None, None]

            def emit_Sw(qb):
                par = qb % 2
                psb = PS if par == 0 else PX
                ptb = ptw if par == 0 else ptw2
                kprev = max(qb - 1, 0)
                p.wait("tensor", ps_freeW[par])
                p.op("tensor", lambda e, qb=qb, kprev=kprev, psb=psb: e.matmul(
                    psb[:, 0:128], kfm[r:r + 64, qcol + kprev * 128: qcol + (kprev + 1) * 128],
                    qfm[r:r + 64, qcol + qb * 128: qcol + (qb + 1) * 128],
                    start=True, stop=True), dep=False)
                ts2 = p.op("tensor", lambda e, qb=qb, psb=psb: e.matmul(
                    psb[:, 128:256], kfm[r:r + 64, qcol + qb * 128: qcol + (qb + 1) * 128],
                    qfm[r:r + 64, qcol + qb * 128: qcol + (qb + 1) * 128],
                    start=True, stop=True), dep=False, inc=True)
                p.wait("scalar", ts2)
                p.wait("scalar", av_tokW[par])
                te = p.op("scalar", lambda e, hq=hq, psb=psb, ptb=ptb: e.activation(
                    ptb[:], psb[:, 0:256], AF.Exp, bias=negc[:, hq:hq + 1], scale=0.125), dep=True)
                ps_freeW[par] = te
                p.wait("vector", te)
                tmsk = p.op("vector", lambda e, qb=qb, ptb=ptb: e.tensor_tensor(
                    ptb[:], ptb[:], s_wm0[:] if qb == 0 else s_wm[:], ALU.mult), dep=True)
                pt_tokW[par] = tmsk

            def emit_AVw(qb):
                nonlocal t_w_done
                par = qb % 2
                ptb = ptw if par == 0 else ptw2
                kprev = max(qb - 1, 0)
                p.wait("tensor", pt_tokW[par])
                cs = (qb % 4) * 128
                if qb % 4 == 0 and t_w_done is not None:
                    p.wait("tensor", t_w_done)
                p.op("tensor", lambda e, kprev=kprev, cs=cs, ptb=ptb: e.matmul(
                    PY[0:65, cs:cs + 128], vrow(kprev), ptb[:, 0:128],
                    start=True, stop=False), dep=False)
                tav = p.op("tensor", lambda e, qb=qb, cs=cs, ptb=ptb: e.matmul(
                    PY[0:65, cs:cs + 128], vrow(qb), ptb[:, 128:256],
                    start=False, stop=True), dep=False, inc=True)
                av_tokW[par] = tav
                if qb % 4 == 3:
                    qg0 = (qb // 4) * 512
                    p.wait("vector", tav)
                    tz2 = p.op("vector", lambda e: e.reciprocal(
                        rzf[64:65, 0:512], PY[64:65, 0:512]), dep=True)
                    p.wait("tensor", tz2)
                    tbm = p.op("tensor", lambda e: e.matmul(
                        PB2[0:65, 0:512], onesf[64:65, 0:65], rzf[64:65, 0:512],
                        start=True, stop=True), dep=False, inc=True)
                    p.wait("scalar", tbm)
                    tbe = p.op("scalar", lambda e: e.copy(bsbA[0:65, 0:512], PB2[0:65, 0:512]), dep=True)
                    p.wait("vector", tbe)
                    t_w_done = p.op("vector", lambda e, qg0=qg0: e.tensor_tensor(
                        win_sb[0:64, qg0:qg0 + 512], bsbA[0:64, 0:512],
                        PY[0:64, 0:512], ALU.mult), dep=True)

            if "win" in A2_PARTS:
                emit_Sw(0)
                for qb in range(1, 32):
                    emit_Sw(qb)
                    emit_AVw(qb - 1)
                emit_AVw(31)
                p.wait("tensor", ps_freeW[0])
                p.wait("tensor", ps_freeW[1])
            t_y_done = t_w_done
            p.wait("sync", t_y_done)
            p.wait("sync", t_w_done)
            t_lin_dma = p.dma(lambda e, hq=hq: e.dma_start(
                out=lin_dst(hq), in_=lin_sb[0:64, :]))
            t_win_dma = p.dma(lambda e, hq=hq: e.dma_start(
                out=win_dst(hq), in_=win_sb[0:64, :]))
            p.wait("scalar", t_lin_dma)
            p.wait("scalar", t_win_dma)
            p.wait("vector", t_win_dma)
            return t_y_done, t_lin_dma, t_win_dma

        for hq in range(A2_NHEADS):
            t_y_done, t_lin_dma, t_win_dma = emit_head(hq, t_y_done, t_lin_dma, t_win_dma)
    return p.tok("sync")


def build_a2():
    nc = bass.Bass("TRN2", target_bir_lowering=False, debug=False)
    hfm = nc.dram_tensor("hfm", [D, T], BF16, kind="ExternalInput").ap()
    hfm_r = hfm.rearrange("(kc p) t -> p kc t", p=128)
    o_lin = nc.dram_tensor("o_lin", [HPC, HD, T], BF16, kind="ExternalOutput").ap()
    o_win = nc.dram_tensor("o_win", [HPC, HD, T], BF16, kind="ExternalOutput").ap()
    io = {}
    for nmm, shp, dt in (("wq", [D, 512], F32), ("wk", [D, 512], F32), ("wv", [D, 512], F32),
                         ("nw", [128, 8], F32), ("wqf2", [128, FEAT], BF16),
                         ("wkf2", [128, FEAT], BF16), ("s_iu", [64, NQUAD], BF16),
                         ("s_ju", [64, NQUAD], BF16), ("tri01", [128, 128], BF16),
                         ("winm", [128, 256], BF16), ("winm0", [128, 256], BF16)):
        io[nmm] = nc.dram_tensor(nmm, shp, dt, kind="ExternalInput").ap()
    io["h_ld"] = lambda kc, r: hfm_r[:, kc, r * TH:(r + 1) * TH]
    io["lin_dst"] = lambda hq: o_lin[hq]
    io["win_dst"] = lambda hq: o_win[hq]
    with ExitStack() as ctx:
        p = Plan()
        emit_a2(nc, p, ctx, io)
        p.wait("sync", p.tok("sync"))
        p.emit(nc, ctx)
    return nc


def _a2_inputs():
    pass


def stage_a2(inz, h_shards):
    nc = _get("a2", build_a2)
    iu, ju = np.triu_indices(FEAT)
    scl = np.where(iu == ju, 0.5, 1.0 / math.sqrt(2.0)).astype(np.float32)
    s_iu_m = np.zeros((64, NQUAD), np.float32)
    s_ju_m = np.zeros((64, NQUAD), np.float32)
    for m in range(NQUAD):
        s_iu_m[iu[m], m] = scl[m]
        s_ju_m[ju[m], m] = 1.0
    tri = (np.arange(128)[:, None] <= np.arange(128)[None, :]).astype(np.float32)
    ii = np.arange(128)[:, None]
    jj = np.arange(128)[None, :]
    wm_prev = (ii >= jj + 64).astype(np.float32)
    wm_self = ((ii <= jj) & (ii >= jj - 64)).astype(np.float32)
    winm = np.concatenate([wm_prev, wm_self], axis=1)
    winm0 = np.concatenate([np.zeros_like(wm_prev), wm_self], axis=1)
    wq, wk, wv = inz["L0_attn_Wq"], inz["L0_attn_Wk"], inz["L0_attn_Wv"]
    nw = np.asarray(inz["L0_attn_norm_w"]).reshape(8, 128).T
    wqf2 = np.concatenate([inz["L0_attn_Wqf"], inz["L0_attn_Wqf"]], axis=0)
    wkf2 = np.concatenate([inz["L0_attn_Wkf"], inz["L0_attn_Wkf"]], axis=0)
    in_maps = []
    hfull = [np.concatenate([np.asarray(h_shards[2 * b]), np.asarray(h_shards[2 * b + 1])], axis=1)
             for b in range(B)]
    for c in range(NCORES):
        b, ho = c // 2, (c % 2) * 8
        cols = slice(ho * HD, (ho + 8) * HD)
        in_maps.append(dict(
            hfm=_bf(hfull[b]), wq=_f32(wq[:, cols]), wk=_f32(wk[:, cols]),
            wv=_f32(wv[:, cols]), nw=_f32(nw), wqf2=_bf(wqf2), wkf2=_bf(wkf2),
            s_iu=_bf(s_iu_m), s_ju=_bf(s_ju_m), tri01=_bf(tri),
            winm=_bf(winm), winm0=_bf(winm0)))
    res = run_bass_kernel_spmd(nc, in_maps, core_ids=list(range(NCORES)))
    return ([r["o_lin"] for r in res.results], [r["o_win"] for r in res.results])



def plan_ln_scalars(p, xtile, Dd, scr, sqd, eps, dep_tok=None):
    if dep_tok is not None:
        p.wait("vector", dep_tok)
        p.wait("scalar", dep_tok)
    t1 = p.op("vector", lambda e: e.reduce_sum(scr[:, 0:1], xtile, axis=AX.X), dep=True)
    t2 = p.op("scalar", lambda e: e.activation(sqd, xtile, AF.Square,
                                               accum_out=scr[:, 1:2]), dep=True)
    p.wait("vector", t2)
    p.op("vector", lambda e: e.tensor_scalar_mul(scr[:, 2:3], scr[:, 0:1], 1.0 / Dd))
    p.op("vector", lambda e: e.tensor_scalar_mul(scr[:, 3:4], scr[:, 1:2], 1.0 / Dd))
    p.op("vector", lambda e: e.tensor_tensor(scr[:, 4:5], scr[:, 2:3], scr[:, 2:3], ALU.mult))
    p.op("vector", lambda e: e.tensor_sub(scr[:, 4:5], scr[:, 3:4], scr[:, 4:5]))
    p.op("vector", lambda e: e.tensor_scalar_add(scr[:, 5:6], scr[:, 4:5], eps))
    tv = p.op("vector", lambda e: e.reciprocal(scr[:, 5:6], scr[:, 5:6]))
    p.wait("scalar", tv)
    ts = p.op("scalar", lambda e: e.activation(scr[:, 6:7], scr[:, 5:6], AF.Sqrt))
    p.wait("vector", ts)
    tb = p.op("vector", lambda e: e.scalar_tensor_tensor(
        scr[:, 7:8], scr[:, 2:3], -1.0, scr[:, 6:7], ALU.mult, ALU.mult))
    return tb


def emit_a3(nc, p, ctx, io):
    wout = io["wout"]; ln0 = io["ln0"]; x0 = io["x0"]; o_x1 = io["x1"]
    cat_ld = io["cat_ld"]
    if True:
        sb = lambda n, s, dt=BF16: ctx.enter_context(nc.sbuf_tensor(n, s, dt))
        s_cat = sb("s_cat", [128, 16 * TH])
        s_w = sb("s_w", [128, 16 * D])
        s_ln = ctx.enter_context(nc.sbuf_tensor("s_ln", [128, D], F32))
        s_x0 = ctx.enter_context(nc.sbuf_tensor("s_x0", [128, D], F32))
        s_a0 = ctx.enter_context(nc.sbuf_tensor("s_a0", [128, D], F32))
        s_x1 = ctx.enter_context(nc.sbuf_tensor("s_x1", [128, D], F32))
        sqd = ctx.enter_context(nc.sbuf_tensor("sqd", [128, D], F32))
        scr = ctx.enter_context(nc.sbuf_tensor("scr", [128, 8], F32))
        P0 = ctx.enter_context(nc.psum_tensor("P0", [128, 512], F32))
        P1 = ctx.enter_context(nc.psum_tensor("P1", [128, 512], F32))
        woutr = wout.rearrange("(kc p) d -> p kc d", p=128)
        lnr = ln0.rearrange("(n p) d -> n p d", p=128)
        x0r = x0.rearrange("(n p) d -> n p d", p=128)
        x1r = o_x1.rearrange("(n p) d -> n p d", p=128)

        tld = None
        for kc in range(16):
            tld = p.dma(lambda e, kc=kc: e.dma_start(
                out=s_cat[:, kc * TH:(kc + 1) * TH], in_=cat_ld(kc)))
            tld = p.op("gpsimd", lambda e, kc=kc: e.dma_start(
                out=s_w[:, kc * D:(kc + 1) * D], in_=woutr[:, kc, :]),
                dep=False, amount=16)
        t_w_ld = p.tok("gpsimd")
        t_c_ld = p.tok("sync")
        p.wait("tensor", t_w_ld)
        p.wait("tensor", t_c_ld)
        t_ev = None
        t_st = None
        tx = None
        for i in range(NT):
            p.wait("sync", tx)
            tl1 = p.dma(lambda e, i=i: e.dma_start(out=s_ln[:], in_=lnr[i]), dep=True)
            tl2 = p.dma(lambda e, i=i: e.dma_start(out=s_x0[:], in_=x0r[i]), dep=True)
            if t_st is not None:
                p.wait("sync", t_st)
            for half in range(2):
                pb = P0 if half == 0 else P1
                if t_ev is not None:
                    p.wait("tensor", t_ev)
                for kc in range(16):
                    tm = p.op("tensor", lambda e, pb=pb, kc=kc, i=i, half=half: e.matmul(
                        pb[:], s_cat[:, kc * TH + i * 128:kc * TH + (i + 1) * 128],
                        s_w[:, kc * D + half * 512:kc * D + (half + 1) * 512],
                        start=(kc == 0), stop=(kc == 15)), inc=(kc == 15))
                p.wait("vector", tm)
                p.wait("vector", tl1)
                t_ev = p.op("vector", lambda e, pb=pb, half=half: e.tensor_add(
                    s_a0[:, half * 512:(half + 1) * 512],
                    s_ln[:, half * 512:(half + 1) * 512], pb[:]), dep=True)
            tb = plan_ln_scalars(p, s_a0[:], D, scr, sqd[:], 1e-5, dep_tok=t_ev)
            p.wait("scalar", tb)
            p.wait("scalar", t_st)
            ta = p.op("scalar", lambda e: e.activation(
                s_x1[:], s_a0[:], AF.Identity, bias=scr[:, 7:8], scale=scr[:, 6:7]), dep=True)
            p.wait("vector", ta)
            p.wait("vector", tl2)
            tx = p.op("vector", lambda e: e.tensor_add(s_x1[:], s_x1[:], s_x0[:]), dep=True)
            p.wait("sync", tx)
            t_st = p.dma(lambda e, i=i: e.dma_start(out=x1r[i], in_=s_x1[:]), dep=True)
    return t_st


def build_a3():
    nc = bass.Bass("TRN2", target_bir_lowering=False, debug=False)
    catf = nc.dram_tensor("catf", [2 * D, TH], BF16, kind="ExternalInput").ap()
    catr = catf.rearrange("(kc p) t -> p kc t", p=128)
    io = dict(
        wout=nc.dram_tensor("wout", [2 * D, D], F32, kind="ExternalInput").ap(),
        ln0=nc.dram_tensor("ln0", [TH, D], F32, kind="ExternalInput").ap(),
        x0=nc.dram_tensor("x0", [TH, D], F32, kind="ExternalInput").ap(),
        x1=nc.dram_tensor("o_x1", [TH, D], F32, kind="ExternalOutput").ap(),
        cat_ld=lambda kc: catr[:, kc, :],
    )
    with ExitStack() as ctx:
        p = Plan()
        emit_a3(nc, p, ctx, io)
        p.wait("sync", p.tok("sync"))
        p.emit(nc, ctx)
    return nc


def stage_a3(inz, lin_s, win_s, ln_s, x_shards):
    nc = _get("a3", build_a3)
    in_maps = []
    for c in range(NCORES):
        b, half = c // 2, c % 2
        linf = np.concatenate([np.asarray(lin_s[2 * b]), np.asarray(lin_s[2 * b + 1])],
                              axis=0).reshape(D, T)
        winf = np.concatenate([np.asarray(win_s[2 * b]), np.asarray(win_s[2 * b + 1])],
                              axis=0).reshape(D, T)
        catf = np.concatenate([linf, winf], axis=0)[:, half * TH:(half + 1) * TH]
        in_maps.append(dict(catf=_bf(catf), wout=_f32(inz["L0_attn_Wout"]),
                            ln0=_f32(ln_s[c]), x0=_f32(x_shards[c])))
    res = run_bass_kernel_spmd(nc, in_maps, core_ids=list(range(NCORES)))
    return [r["o_x1"] for r in res.results]


def emit_b(nc, p, ctx, io):
    x1 = io["x1"]; w1t = io["w1t"]; b1 = io["b1"]; w2t = io["w2t"]; b2 = io["b2"]
    ident = io["ident"]; identf = io["identf"]; o_x2 = io["x2"]; ln2f_d = io["ln2f"]
    HN = 4096
    CMAGIC = 1.5 * (2 ** 23)
    if True:
        sb = lambda n, s, dt=BF16: ctx.enter_context(nc.sbuf_tensor(n, s, dt))
        f32t = lambda n, s: ctx.enter_context(nc.sbuf_tensor(n, s, F32))
        wq1 = sb("wq1", [128, 8 * HN])
        wq2 = sb("wq2", [128, 32 * D])
        s_id = sb("s_id", [128, 128]); s_idf = f32t("s_idf", [128, 128])
        b1s = sb("b1s", [1, HN]); b2s = sb("b2s", [1, D])
        s_x1 = f32t("s_x1", [128, D])
        scr = f32t("scrB", [128, 16])
        gs = f32t("gsB", [1, 8])
        onesf = f32t("onesfB", [128, 1])
        swcol = f32t("swcolB", [128, 2])
        srow = sb("srowB", [1, 128])
        xq = sb("xqB", [128, D]); xqf = sb("xqfB", [128, D])
        f0 = f32t("f0B", [128, D]); x2t = f32t("x2tB", [128, D])
        PT = ctx.enter_context(nc.psum_tensor("PTb", [128, 128], BF16))
        PM0 = ctx.enter_context(nc.psum_tensor("PM0", [128, 512], F32))
        PM1 = ctx.enter_context(nc.psum_tensor("PM1", [128, 512], F32))
        PR = ctx.enter_context(nc.psum_tensor("PRb", [128, 128], F32))
        x1r = x1.rearrange("(n p) d -> n p d", p=128)
        x2r = o_x2.rearrange("(n p) d -> n p d", p=128)
        ln2fr = ln2f_d.rearrange("(kc p) t -> p kc t", p=128)
        w1r = w1t.rearrange("(kc p) h -> p kc h", p=128)
        w2r = w2t.rearrange("(kc p) d -> p kc d", p=128)

        t0 = p.dma(lambda e: e.dma_start(out=s_id[:], in_=ident))
        t0 = p.dma(lambda e: e.dma_start(out=s_idf[:], in_=identf))
        p.op("vector", lambda e: e.memset(onesf[:], 1.0), dep=False)

        onesrow = f32t("onesrowB", [1, 128])
        ctx2 = ExitStack()
        wstg = ctx2.enter_context(nc.sbuf_tensor("wstgB", [128, 4096], F32))
        s_b1 = ctx2.enter_context(nc.sbuf_tensor("s_b1", [1, HN], F32))
        s_b2 = ctx2.enter_context(nc.sbuf_tensor("s_b2", [1, D], F32))

        t0 = p.dma(lambda e: e.dma_start(out=s_b1[:], in_=b1))
        t0 = p.dma(lambda e: e.dma_start(out=s_b2[:], in_=b2))

        def quant(wr, nkc, cols, gcol, wq_sb, bsrc, bdst):
            p.op("vector", lambda e: e.memset(scr[:, 8:9], 0.0), dep=True)
            tq = p.tok("vector")
            for kc in range(nkc):
                p.wait("sync", tq)
                tld = p.dma(lambda e, kc=kc: e.dma_start(
                    out=wstg[:, 0:cols], in_=wr[:, kc, :]), dep=True)
                p.wait("vector", tld)
                p.op("vector", lambda e: e.tensor_reduce(
                    scr[:, 9:10], wstg[:, 0:cols], AX.X, ALU.add,
                    apply_absolute_value=True), dep=True)
                tq = p.op("vector", lambda e: e.tensor_add(
                    scr[:, 8:9], scr[:, 8:9], scr[:, 9:10]), dep=True)
            p.wait("tensor", tq)
            tps = p.op("tensor", lambda e: e.matmul(
                PR[0:1, 0:1], scr[:, 8:9], onesf[:], start=True, stop=True), inc=True)
            p.wait("vector", tps)
            p.op("vector", lambda e: e.tensor_copy(gs[0:1, gcol:gcol + 1], PR[0:1, 0:1]), dep=True)
            nn = float(nkc * 128 * cols)
            p.op("vector", lambda e: e.tensor_scalar_mul(
                gs[0:1, gcol:gcol + 1], gs[0:1, gcol:gcol + 1], 1.0 / nn), dep=True)
            p.op("vector", lambda e: e.tensor_scalar_max(
                gs[0:1, gcol:gcol + 1], gs[0:1, gcol:gcol + 1], 1e-5), dep=True)
            trc = p.op("vector", lambda e: e.reciprocal(
                gs[0:1, 2 + gcol:3 + gcol], gs[0:1, gcol:gcol + 1]), dep=True)
            tbs = p.op("vector", lambda e: e.tensor_copy(bdst[0:1, :], bsrc[0:1, :]), dep=True)
            return tbs

        p.op("vector", lambda e: e.memset(onesrow[:], 1.0), dep=False)

        def bcast_col(src11, dstcol, tdep):
            p.wait("tensor", tdep)
            t = p.op("tensor", lambda e: e.matmul(
                PR[:, 2:3], onesrow[0:1, :], src11, start=True, stop=True), inc=True)
            p.wait("vector", t)
            return p.op("vector", lambda e: e.tensor_copy(dstcol, PR[:, 2:3]), dep=True)

        def quant2(wr, nkc, cols, gcol, wq_sb):
            tq = p.tok("vector")
            for kc in range(nkc):
                p.wait("sync", tq)
                tld = p.dma(lambda e, kc=kc: e.dma_start(
                    out=wstg[:, 0:cols], in_=wr[:, kc, :]), dep=True)
                p.wait("vector", tld)
                p.op("vector", lambda e: e.tensor_scalar(
                    wstg[:, 0:cols], wstg[:, 0:cols], swcol[:, gcol:gcol + 1],
                    CMAGIC, ALU.mult, ALU.add), dep=True)
                p.op("vector", lambda e: e.tensor_scalar(
                    wstg[:, 0:cols], wstg[:, 0:cols], CMAGIC, 1.0,
                    ALU.subtract, ALU.min), dep=True)
                tq = p.op("vector", lambda e, kc=kc: e.tensor_scalar_max(
                    wq_sb[:, kc * cols:(kc + 1) * cols], wstg[:, 0:cols], -1.0), dep=True)
            return tq

        tq1t = quant(w1r, 8, HN, 0, wq1, s_b1, b1s)
        bcast_col(gs[0:1, 2:3], swcol[:, 0:1], tq1t)
        tq2t = quant(w2r, 32, D, 1, wq2, s_b2, b2s)
        bcast_col(gs[0:1, 3:4], swcol[:, 1:2], tq2t)
        quant2(w1r, 8, HN, 0, wq1)
        tqw = quant2(w2r, 32, D, 1, wq2)
        if "dbg_gs" in io:
            p.wait("sync", tqw)
            p.dma(lambda e: e.dma_start(out=io["dbg_gs"], in_=gs[:]))
        ctx2.close()
        sqd = f32t("sqdB", [128, 2048])
        hbuf = sb("hbufB", [128, HN])
        xq2 = sb("xq2B", [128, HN]); xq2f = sb("xq2fB", [128, HN])
        ln2f = sb("ln2fB", [128, 1024])
        s_x1b = f32t("s_x1bB", [128, D])
        scrB2 = f32t("scrB2B", [128, 16]); scrM = f32t("scrMB", [128, 16])
        sqdp0 = f32t("sqdp0B", [128, D]); sqdp1 = f32t("sqdp1B", [128, D])
        xqb = sb("xqbB", [128, D])
        PT2 = ctx.enter_context(nc.psum_tensor("PT2b", [128, 128], BF16))
        s_x1P = [s_x1, s_x1b]; scrP = [scr, scrB2]
        sqdP = [sqdp0, sqdp1]; xqP = [xq, xqb]
        p.wait("tensor", tqw)
        p.wait("scalar", tqw)
        p.wait("vector", tqw)
        p.wait("sync", tqw)

        def act_quant_scalars(src_tile, Dd, has_mean, scr):
            if has_mean:
                p.op("vector", lambda e: e.tensor_tensor(scr[:, 9:10], scr[:, 4:5], scr[:, 5:6], ALU.mult))
                p.op("vector", lambda e: e.tensor_scalar_add(scr[:, 9:10], scr[:, 9:10], 1e-6))
                tv = p.op("vector", lambda e: e.reciprocal(scr[:, 9:10], scr[:, 9:10]))
                p.wait("scalar", tv)
                ts = p.op("scalar", lambda e: e.activation(scr[:, 9:10], scr[:, 9:10], AF.Sqrt))
                p.wait("vector", ts)
                p.op("vector", lambda e: e.tensor_tensor(scr[:, 10:11], scr[:, 6:7], scr[:, 9:10], ALU.mult))
                t1 = p.op("vector", lambda e: e.reduce_max(scr[:, 11:12], src_tile, axis=AX.X), dep=True)
                t1 = p.op("vector", lambda e: e.tensor_reduce(scr[:, 9:10], src_tile, AX.X, ALU.min), dep=True)
                p.op("vector", lambda e: e.tensor_sub(scr[:, 11:12], scr[:, 11:12], scr[:, 2:3]))
                p.op("vector", lambda e: e.tensor_sub(scr[:, 9:10], scr[:, 2:3], scr[:, 9:10]))
                p.op("vector", lambda e: e.tensor_max(scr[:, 11:12], scr[:, 11:12], scr[:, 9:10]))
            else:
                p.op("vector", lambda e, Dd=Dd: e.tensor_scalar_mul(scr[:, 9:10], scr[:, 1:2], 1.0 / Dd))
                p.op("vector", lambda e: e.tensor_scalar_add(scr[:, 9:10], scr[:, 9:10], 1e-6))
                tv = p.op("vector", lambda e: e.reciprocal(scr[:, 9:10], scr[:, 9:10]))
                p.wait("scalar", tv)
                ts = p.op("scalar", lambda e: e.activation(scr[:, 10:11], scr[:, 9:10], AF.Sqrt))
                p.wait("vector", ts)
                t1 = p.op("vector", lambda e: e.reduce_max(scr[:, 11:12], src_tile, axis=AX.X,
                                                           apply_absolute_value=True), dep=True)
            p.op("vector", lambda e: e.tensor_tensor(scr[:, 11:12], scr[:, 11:12], scr[:, 10:11], ALU.mult))
            p.op("vector", lambda e: e.tensor_scalar_max(scr[:, 11:12], scr[:, 11:12], 1e-5))
            p.op("vector", lambda e: e.reciprocal(scr[:, 12:13], scr[:, 11:12]))
            p.op("vector", lambda e: e.tensor_scalar_mul(scr[:, 12:13], scr[:, 12:13], 127.0))
            p.op("vector", lambda e: e.tensor_tensor(scr[:, 13:14], scr[:, 10:11], scr[:, 12:13], ALU.mult))
            if has_mean:
                tg = p.op("vector", lambda e: e.scalar_tensor_tensor(
                    scr[:, 14:15], scr[:, 2:3], -1.0, scr[:, 13:14], ALU.mult, ALU.mult))
            else:
                tg = p.op("vector", lambda e: e.memset(scr[:, 14:15], 0.0), dep=True)
            return tg

        def scale_rows_np(scr, gcol):
            p.op("vector", lambda e, scr=scr, gcol=gcol: e.tensor_tensor(
                scr[:, 15:16], scr[:, 12:13], swcol[:, gcol:gcol + 1], ALU.mult))
            tr = p.op("vector", lambda e, scr=scr: e.reciprocal(scr[:, 8:9], scr[:, 15:16]))
            return tr

        t_srt = [None]

        def scale_rows_pe(scr, tr):
            p.wait("tensor", tr)
            p.wait("tensor", t_srt[0])
            tt = p.op("tensor", lambda e, scr=scr: e.transpose(
                PR[0:1, 0:128], scr[:, 15:16], s_idf[:]), inc=True)
            p.wait("scalar", tt)
            tsr = p.op("scalar", lambda e: e.copy(srow[0:1, 0:128], PR[0:1, 0:128]), dep=True)
            t_srt[0] = tsr
            return tt, tsr

        tok_pt = [None, None]

        def transpose_to_fm(src_bf, dst_bf, nkq, tdep):
            p.wait("tensor", tdep)
            tt = None; tcp = None
            for kq in range(nkq):
                pb = PT if kq % 2 == 0 else PT2
                if tok_pt[kq % 2] is not None:
                    p.wait("tensor", tok_pt[kq % 2])
                tt = p.op("tensor", lambda e, kq=kq, pb=pb, src_bf=src_bf: e.transpose(
                    pb[:], src_bf[:, kq * 128:(kq + 1) * 128], s_id[:]), inc=True)
                p.wait("scalar", tt)
                tcp = p.op("scalar", lambda e, kq=kq, pb=pb, dst_bf=dst_bf: e.copy(
                    dst_bf[:, kq * 128:(kq + 1) * 128], pb[:]), dep=True)
                tok_pt[kq % 2] = tcp
            return tt, tcp

        t_st = [p.tok("sync")]
        tx2_tok = [None, None]
        gelu_tok = [None, None]
        xpt_tok = [None, None]
        tsub_par = [None, None]
        scale_tt_tok = [None, None]
        f0ev_prev = [None, None]
        tsub2_prev = [None]
        l2t_prev = [None]
        t_l2dma = [None]

        def pre_np(i):
            par = i % 2
            scr = scrP[par]; src = s_x1P[par][:]
            p.wait("sync", tx2_tok[par])
            tl = p.dma(lambda e, i=i, par=par: e.dma_start(
                out=s_x1P[par][:], in_=x1r[i]), dep=True)
            if scale_tt_tok[par] is not None:
                p.wait("vector", scale_tt_tok[par])
            gp = lambda fn: p.op("vector", fn, dep=True)
            p.wait("vector", tl)
            tvr = p.op("vector", lambda e, scr=scr, src=src: e.tensor_reduce(
                scr[:, 11:12], src, AX.X, ALU.max), dep=True)
            tvr = p.op("vector", lambda e, scr=scr, src=src: e.tensor_reduce(
                scr[:, 7:8], src, AX.X, ALU.min), dep=True)
            p.wait("scalar", tl)
            if gelu_tok[par] is not None:
                p.wait("scalar", gelu_tok[par])
            if tsub_par[par] is not None:
                p.wait("scalar", tsub_par[par])
            tS = p.op("scalar", lambda e, scr=scr, src=src, par=par: e.activation(
                sqdP[par][:], src, AF.Identity, accum_out=scr[:, 0:1]), dep=True)
            t2 = p.op("scalar", lambda e, scr=scr, src=src, par=par: e.activation(
                sqdP[par][:], src, AF.Square, accum_out=scr[:, 1:2]), dep=True)
            p.wait("vector", t2)
            gp(lambda e, scr=scr: e.tensor_scalar_mul(scr[:, 2:3], scr[:, 0:1], 1.0 / D))
            gp(lambda e, scr=scr: e.tensor_scalar_mul(scr[:, 3:4], scr[:, 1:2], 1.0 / D))
            gp(lambda e, scr=scr: e.tensor_tensor(scr[:, 4:5], scr[:, 2:3], scr[:, 2:3], ALU.mult))
            gp(lambda e, scr=scr: e.tensor_sub(scr[:, 4:5], scr[:, 3:4], scr[:, 4:5]))
            tga = gp(lambda e, scr=scr: e.tensor_scalar_add(scr[:, 5:6], scr[:, 4:5], 1e-5))
            p.wait("vector", tga)
            tvx = p.op("vector", lambda e, scr=scr: e.reciprocal(
                scr[:, 5:6], scr[:, 5:6]), dep=True)
            p.wait("scalar", tvx)
            tsa = p.op("scalar", lambda e, scr=scr: e.activation(
                scr[:, 6:7], scr[:, 5:6], AF.Sqrt), dep=True)
            p.wait("vector", tsa)
            gp(lambda e, scr=scr: e.tensor_tensor(scr[:, 9:10], scr[:, 4:5], scr[:, 5:6], ALU.mult))
            tga = gp(lambda e, scr=scr: e.tensor_scalar_add(scr[:, 9:10], scr[:, 9:10], 1e-6))
            p.wait("vector", tga)
            tvx = p.op("vector", lambda e, scr=scr: e.reciprocal(
                scr[:, 9:10], scr[:, 9:10]), dep=True)
            p.wait("scalar", tvx)
            tsa = p.op("scalar", lambda e, scr=scr: e.activation(
                scr[:, 9:10], scr[:, 9:10], AF.Sqrt), dep=True)
            p.wait("vector", tsa)
            gp(lambda e, scr=scr: e.tensor_tensor(scr[:, 10:11], scr[:, 6:7], scr[:, 9:10], ALU.mult))
            
            gp(lambda e, scr=scr: e.tensor_sub(scr[:, 11:12], scr[:, 11:12], scr[:, 2:3]))
            gp(lambda e, scr=scr: e.tensor_sub(scr[:, 7:8], scr[:, 2:3], scr[:, 7:8]))
            gp(lambda e, scr=scr: e.tensor_max(scr[:, 11:12], scr[:, 11:12], scr[:, 7:8]))
            tga = gp(lambda e, scr=scr: e.tensor_scalar(
                scr[:, 11:12], scr[:, 11:12], scr[:, 10:11], 1e-5, ALU.mult, ALU.max))
            p.wait("vector", tga)
            tvx = p.op("vector", lambda e, scr=scr: e.reciprocal(
                scr[:, 12:13], scr[:, 11:12]), dep=True)
            
            gp(lambda e, scr=scr: e.tensor_scalar(
                scr[:, 13:14], scr[:, 12:13], scr[:, 10:11], 127.0, ALU.mult, ALU.mult))
            gp(lambda e, scr=scr: e.scalar_tensor_tensor(
                scr[:, 14:15], scr[:, 2:3], -1.0, scr[:, 13:14], ALU.mult, ALU.mult))
            tg15 = gp(lambda e, scr=scr: e.tensor_scalar(
                scr[:, 15:16], scr[:, 12:13], swcol[:, 0:1], 127.0, ALU.mult, ALU.mult))
            p.wait("vector", tg15)
            tr8 = p.op("vector", lambda e, scr=scr: e.reciprocal(
                scr[:, 8:9], scr[:, 15:16]), dep=True)
            p.wait("scalar", tg15)
            ta = p.op("scalar", lambda e, par=par, scr=scr: e.activation(
                sqdP[par][:], s_x1P[par][:], AF.Identity,
                bias=scr[:, 14:15], scale=scr[:, 13:14]), dep=True)
            p.wait("vector", ta)
            if xpt_tok[par] is not None:
                p.wait("vector", xpt_tok[par])
            tsub = p.op("vector", lambda e, par=par: e.tensor_scalar(
                xqP[par][:], sqdP[par][:], CMAGIC, -CMAGIC, ALU.add, ALU.add), dep=True)
            tsub_par[par] = tsub
            return tsub, tr8

        def body(i, tsub, tr):
            par = i % 2
            tt0, tsr = scale_rows_pe(scrP[par], tr)
            scale_tt_tok[par] = tt0
            ttx, tcp = transpose_to_fm(xqP[par], xqf, 8, tsub)
            xpt_tok[par] = ttx
            p.wait("tensor", tsr)
            p.wait("tensor", f0ev_prev[0])
            p.wait("tensor", f0ev_prev[1])
            t_ev = None
            for g8 in range(8):
                pb = PM0 if g8 % 2 == 0 else PM1
                if g8 >= 2:
                    p.wait("tensor", t_ev2 if g8 % 2 == 0 else t_ev)
                for kc in range(8):
                    p.op("tensor", lambda e, pb=pb, kc=kc, g8=g8: e.matmul(
                        pb[:], xqf[:, kc * 128:(kc + 1) * 128],
                        wq1[:, kc * HN + g8 * 512:kc * HN + (g8 + 1) * 512],
                        start=(kc == 0), stop=False))
                tm = p.op("tensor", lambda e, pb=pb, g8=g8: e.matmul(
                    pb[:], srow[0:1, 0:128], b1s[0:1, g8 * 512:(g8 + 1) * 512],
                    start=False, stop=True), inc=True)
                p.wait("scalar", tm)
                tev = p.op("scalar", lambda e, pb=pb, g8=g8, par=par: e.activation(
                    hbuf[:, g8 * 512:(g8 + 1) * 512], pb[:], AF.Gelu,
                    scale=scrP[par][:, 8:9]), dep=True)
                if g8 % 2 == 0:
                    t_ev2 = tev
                else:
                    t_ev = tev
            t_ev = tev
            gelu_tok[par] = tev
            p.wait("scalar", t_ev)
            p.wait("scalar", tsub2_prev[0])
            t2s = p.op("scalar", lambda e: e.activation(
                sqd[:], hbuf[:, 0:2048], AF.Square, accum_out=scrM[:, 1:2]), dep=True)
            t2sb = p.op("scalar", lambda e: e.activation(
                sqd[:], hbuf[:, 2048:HN], AF.Square, accum_out=scrM[:, 3:4]), dep=True)
            p.wait("vector", t2sb)
            p.op("vector", lambda e: e.tensor_add(
                scrM[:, 1:2], scrM[:, 1:2], scrM[:, 3:4]), dep=True)
            tg2 = act_quant_scalars(hbuf[:], HN, False, scrM)
            p.wait("scalar", tg2)
            ta2 = p.op("scalar", lambda e: e.activation(
                sqd[:], hbuf[:, 0:2048], AF.Identity, bias=scrM[:, 14:15],
                scale=scrM[:, 13:14]), dep=True)
            p.wait("vector", ta2)
            p.wait("vector", l2t_prev[0])
            tsub2 = p.op("vector", lambda e: e.tensor_scalar(
                xq2[:, 0:2048], sqd[:], CMAGIC, -CMAGIC, ALU.add, ALU.add), dep=True)
            p.wait("scalar", tsub2)
            ta2b = p.op("scalar", lambda e: e.activation(
                sqd[:], hbuf[:, 2048:HN], AF.Identity, bias=scrM[:, 14:15],
                scale=scrM[:, 13:14]), dep=True)
            p.wait("vector", ta2b)
            tsub2 = p.op("vector", lambda e: e.tensor_scalar(
                xq2[:, 2048:HN], sqd[:], CMAGIC, -CMAGIC, ALU.add, ALU.add), dep=True)
            tsub2_prev[0] = tsub2
            tr2 = scale_rows_np(scrM, 1)
            _, tsr2 = scale_rows_pe(scrM, tr2)
            _, tcp2 = transpose_to_fm(xq2, xq2f, 32, tsub2)
            p.wait("tensor", tsr2)
            f0evs = [None, None]
            for half in range(2):
                pb = PM0 if half == 0 else PM1
                for kc in range(32):
                    p.op("tensor", lambda e, pb=pb, kc=kc, half=half: e.matmul(
                        pb[:], xq2f[:, kc * 128:(kc + 1) * 128],
                        wq2[:, kc * D + half * 512:kc * D + (half + 1) * 512],
                        start=(kc == 0), stop=False))
                tm = p.op("tensor", lambda e, pb=pb, half=half: e.matmul(
                    pb[:], srow[0:1, 0:128], b2s[0:1, half * 512:(half + 1) * 512],
                    start=False, stop=True), inc=True)
                p.wait("scalar", tm)
                t_ev = p.op("scalar", lambda e, pb=pb, half=half: e.activation(
                    f0[:, half * 512:(half + 1) * 512], pb[:], AF.Identity,
                    scale=scrM[:, 8:9]), dep=True)
                f0evs[half] = t_ev
            f0ev_prev[0] = f0evs[0]
            f0ev_prev[1] = f0evs[1]
            if i == 0 and "dbg_gs" in io:
                p.wait("sync", t_ev)
                p.dma(lambda e: e.dma_start(out=io["dbg_wq"], in_=wq1[:, 0:4096]))
                p.dma(lambda e: e.dma_start(out=io["dbg_xq"], in_=xqP[0][:]))
                p.dma(lambda e: e.dma_start(out=io["dbg_h"], in_=hbuf[:]))
                p.dma(lambda e: e.dma_start(out=io["dbg_sc"], in_=scrM[:]))
            tb2 = plan_ln_scalars(p, f0[:], D, scrM, sqd[:, 0:D], 1e-5, dep_tok=t_ev)
            p.wait("scalar", tb2)
            p.wait("scalar", t_st[0])
            ta3 = p.op("scalar", lambda e: e.activation(
                x2t[:], f0[:], AF.Identity, bias=scrM[:, 7:8], scale=scrM[:, 6:7]), dep=True)
            p.wait("vector", ta3)
            tx2 = p.op("vector", lambda e, par=par: e.tensor_add(
                x2t[:], x2t[:], s_x1P[par][:]), dep=True)
            tx2_tok[par] = tx2
            p.wait("sync", tx2)
            p.wait("sync", t_st[0])
            t_st[0] = p.dma(lambda e, i=i: e.dma_start(out=x2r[i], in_=x2t[:]), dep=True)
            tb3 = plan_ln_scalars(p, x2t[:], D, scrM, sqd[:, 0:D], 1e-5, dep_tok=tx2)
            p.wait("scalar", tb3)
            tl2 = p.op("scalar", lambda e: e.activation(
                xq2[:, 0:D], x2t[:], AF.Identity, bias=scrM[:, 7:8],
                scale=scrM[:, 6:7]), dep=True)
            p.wait("scalar", t_l2dma[0])
            ttl, tcpl = transpose_to_fm(xq2, ln2f, 8, tl2)
            l2t_prev[0] = ttl
            p.wait("sync", tcpl)
            t_l2dma[0] = p.dma(lambda e, i=i: e.dma_start(
                out=ln2fr[:, :, i * 128:(i + 1) * 128],
                in_=ln2f[:].rearrange("p (kc t) -> p kc t", kc=8)), dep=False)

        pre = pre_np(0)
        for i in range(NT):
            nxt = pre_np(i + 1) if i + 1 < NT else None
            body(i, *pre)
            pre = nxt
    return t_l2dma[0]


def build_b():
    nc = bass.Bass("TRN2", target_bir_lowering=False, debug=False)
    io = dict(
        x1=nc.dram_tensor("x1", [TH, D], F32, kind="ExternalInput").ap(),
        w1t=nc.dram_tensor("w1t", [D, 4096], F32, kind="ExternalInput").ap(),
        b1=nc.dram_tensor("b1", [1, 4096], F32, kind="ExternalInput").ap(),
        w2t=nc.dram_tensor("w2t", [4096, D], F32, kind="ExternalInput").ap(),
        b2=nc.dram_tensor("b2", [1, D], F32, kind="ExternalInput").ap(),
        ident=nc.dram_tensor("ident", [128, 128], BF16, kind="ExternalInput").ap(),
        identf=nc.dram_tensor("identf", [128, 128], F32, kind="ExternalInput").ap(),
        x2=nc.dram_tensor("o_x2", [TH, D], F32, kind="ExternalOutput").ap(),
        ln2f=nc.dram_tensor("o_ln2f", [D, TH], BF16, kind="ExternalOutput").ap(),
    )
    with ExitStack() as ctx:
        p = Plan()
        emit_b(nc, p, ctx, io)
        p.wait("sync", p.tok("sync"))
        p.emit(nc, ctx)
    return nc


def stage_b(inz, x1_shards):
    nc = _get("b", build_b)
    idb = np.eye(128, dtype=np.float32)
    in_maps = []
    for c in range(NCORES):
        in_maps.append(dict(
            x1=_f32(x1_shards[c]), w1t=_f32(np.asarray(inz["L0_ffn_w1"]).T),
            b1=_f32(np.asarray(inz["L0_ffn_b1"]).reshape(1, -1)),
            w2t=_f32(np.asarray(inz["L0_ffn_w2"]).T),
            b2=_f32(np.asarray(inz["L0_ffn_b2"]).reshape(1, -1)),
            ident=_bf(idb), identf=_f32(idb)))
    res = run_bass_kernel_spmd(nc, in_maps, core_ids=list(range(NCORES)))
    return ([r["o_x2"] for r in res.results], [r["o_ln2f"] for r in res.results])



def emit_c2(nc, p, ctx, io):
    wq_d = io["wq"]; wqs_d = io["wqs"]; wk_d = io["wk"]; wks_d = io["wks"]
    wv_d = io["wv"]; bq_d = io["bq"]; bqs_d = io["bqs"]; bk_d = io["bk"]
    bks_d = io["bks"]; bv_d = io["bv"]; cosf_d = io["cosf"]; sins_d = io["sins"]
    msk_d = io["msk"]; ln_ld = io["ln_ld"]; o_dst = io["o_dst"]
    if True:
        sb = lambda n, s, dt=BF16: ctx.enter_context(nc.sbuf_tensor(n, s, dt))
        f32t = lambda n, s: ctx.enter_context(nc.sbuf_tensor(n, s, F32))
        qfm = sb("qfmC", [128, 4 * T]); kfm = sb("kfmC", [128, 4 * T])
        vsb = sb("vsbC", [128, 32 * 520])
        s_cos = sb("s_cosC", [128, T]); s_sin = sb("s_sinC", [128, T])
        s_msk = sb("s_mskC", [128, 4 * 512])
        ptw = sb("ptwC", [128, 512])
        ptw2 = sb("ptw2C", [128, 512])
        bsb = sb("bsbC", [128, 512])
        rsc1 = sb("rsc1C", [128, 512]); rsc2 = sb("rsc2C", [128, 512])
        ones_b = sb("ones_bC", [128, 128])
        onesf = f32t("onesfC", [128, 128])
        md = f32t("mdC", [128, 16])
        cbuf = f32t("cbufC", [1, 32])
        negc = f32t("negcC", [128, 8])
        rzb = f32t("rzbC", [128, 512])
        osb = sb("osbC", [64, T])
        ones_row_b = sb("ones_row_bC", [1, 512])
        PX = ctx.enter_context(nc.psum_tensor("PXc", [128, 512], F32))
        PS = ctx.enter_context(nc.psum_tensor("PSc", [128, 512], F32))
        PO = ctx.enter_context(nc.psum_tensor("POc", [128, 512], F32))
        PB = ctx.enter_context(nc.psum_tensor("PBc", [128, 512], F32))
        PY = ctx.enter_context(nc.psum_tensor("PYc", [128, 72], F32))

        ctx2 = ExitStack()
        sb2 = lambda n, s, dt: ctx2.enter_context(nc.sbuf_tensor(n, s, dt))
        s_ht = [sb2("s_ht0C", [128, 8 * 512], BF16), sb2("s_ht1C", [128, 8 * 512], BF16)]
        s_wq = sb2("s_wqC", [128, 8 * 512], BF16)
        s_wqs = sb2("s_wqsC", [128, 8 * 512], BF16)
        s_wk = sb2("s_wkC", [128, 8 * 512], BF16)
        s_wks = sb2("s_wksC", [128, 8 * 512], BF16)
        s_wv = sb2("s_wvC", [128, 8 * 512], BF16)
        s_bias = sb2("s_biasC", [1, 5 * 512], BF16)

        t_h = None
        t_w = None
        for w_sb, w_d in ((s_wq, wq_d), (s_wqs, wqs_d), (s_wk, wk_d),
                          (s_wks, wks_d), (s_wv, wv_d)):
            w_r = w_d.rearrange("(kc p) c -> p kc c", p=128)
            for kc in range(8):
                t_w = p.op("gpsimd", lambda e, a=w_sb, w_r=w_r, kc=kc: e.dma_start(
                    out=a[:, kc * 512:(kc + 1) * 512], in_=w_r[:, kc, :]),
                    dep=False, amount=16)
        for j, b_d in enumerate((bq_d, bqs_d, bk_d, bks_d, bv_d)):
            t_h = p.dma(lambda e, j=j, b_d=b_d: e.dma_start(
                out=s_bias[0:1, j * 512:(j + 1) * 512], in_=b_d))
        t_h = p.dma(lambda e: e.dma_start(out=s_cos[:], in_=cosf_d))
        t_h = p.dma(lambda e: e.dma_start(out=s_sin[:], in_=sins_d))
        t_h = p.dma(lambda e: e.dma_start(out=s_msk[:], in_=msk_d))
        t_ones = p.op("vector", lambda e: e.memset(ones_b[:], 1.0), dep=False)
        p.op("vector", lambda e: e.memset(ones_row_b[:], 1.0), dep=False)
        t_onesf = p.op("vector", lambda e: e.memset(onesf[:], 1.0), dep=False)
        t_vinit = p.op("vector", lambda e: e.memset(vsb[:], 1.0), dep=False)

        p.wait("tensor", t_h)
        p.wait("tensor", t_w)
        p.wait("tensor", t_ones)
        p.wait("tensor", t_vinit)

        def proj(w_sb, bj, dst, tt, cg, t_evprev):
            sh = s_ht[tt % 2]
            if t_evprev is not None:
                p.wait("tensor", t_evprev)
            pb = PX if (tt + cg) % 2 == 0 else PS
            for kc in range(8):
                p.op("tensor", lambda e, pb=pb, w_sb=w_sb, kc=kc, cg=cg, sh=sh: e.matmul(
                    pb[:], w_sb[:, kc * 512 + cg * 128:kc * 512 + (cg + 1) * 128],
                    sh[:, kc * 512:(kc + 1) * 512],
                    start=(kc == 0), stop=False))
            tm = p.op("tensor", lambda e, pb=pb, bj=bj, cg=cg: e.matmul(
                pb[:], s_bias[0:1, bj * 512 + cg * 128:bj * 512 + (cg + 1) * 128],
                ones_b[0:1, 0:128].rearrange("a b -> a b") if False else ones_row_b[0:1, 0:512],
                start=False, stop=True), inc=True)
            p.wait("scalar", tm)
            tev = p.op("scalar", lambda e, pb=pb, dst=dst, cg=cg, tt=tt: e.copy(
                dst[:, cg * T + tt * 512:cg * T + (tt + 1) * 512], pb[:]), dep=False)
            return tev

        t_ev = None
        t_htld = [None, None]
        for tt in range(8):
            sh = s_ht[tt % 2]
            if tt >= 2:
                p.wait("sync", t_ev)
            for kc in range(8):
                t_htld[tt % 2] = p.dma(lambda e, kc=kc, sh=sh, tt=tt: e.dma_start(
                    out=sh[:, kc * 512:(kc + 1) * 512],
                    in_=ln_ld(kc, tt)), dep=False)
            p.wait("tensor", t_htld[tt % 2])
            for cg in range(4):
                t_ev = proj(s_wq, 0, qfm, tt, cg, t_ev)
                t_ev = proj(s_wk, 2, kfm, tt, cg, t_ev)
                for wsw, bj, dst in ((s_wqs, 1, qfm), (s_wks, 3, kfm)):
                    p.wait("tensor", t_ev)
                    pb = PX if (tt + cg) % 2 == 0 else PS
                    for kc in range(8):
                        p.op("tensor", lambda e, pb=pb, wsw=wsw, kc=kc, cg=cg, sh=sh: e.matmul(
                            pb[:], wsw[:, kc * 512 + cg * 128:kc * 512 + (cg + 1) * 128],
                            sh[:, kc * 512:(kc + 1) * 512],
                            start=(kc == 0), stop=False))
                    tm = p.op("tensor", lambda e, pb=pb, bj=bj, cg=cg: e.matmul(
                        pb[:], s_bias[0:1, bj * 512 + cg * 128:bj * 512 + (cg + 1) * 128],
                        ones_row_b[0:1, 0:512], start=False, stop=True), inc=True)
                    p.wait("scalar", tm)
                    tsw = p.op("scalar", lambda e, pb=pb: e.copy(rsc1[:], pb[:]), dep=True)
                    p.wait("vector", tsw)
                    p.op("vector", lambda e, dst=dst, cg=cg, tt=tt: e.tensor_tensor(
                        rsc2[:], dst[:, cg * T + tt * 512:cg * T + (tt + 1) * 512],
                        s_cos[:, tt * 512:(tt + 1) * 512], ALU.mult), dep=True)
                    p.op("vector", lambda e, tt=tt: e.tensor_tensor(
                        rsc1[:], rsc1[:], s_sin[:, tt * 512:(tt + 1) * 512], ALU.mult), dep=True)
                    t_ev = p.op("vector", lambda e, dst=dst, cg=cg, tt=tt: e.tensor_add(
                        dst[:, cg * T + tt * 512:cg * T + (tt + 1) * 512],
                        rsc2[:], rsc1[:]), dep=True)
            vview = vsb[:].rearrange("p (c h ee) -> p c h ee", c=32, h=8)
            for t4 in range(4):
                ck = tt * 4 + t4
                p.wait("tensor", t_ev)
                pb = PX if t4 % 2 == 0 else PS
                for kc in range(8):
                    p.op("tensor", lambda e, pb=pb, kc=kc, t4=t4, sh=sh: e.matmul(
                        pb[:], sh[:, kc * 512 + t4 * 128:kc * 512 + t4 * 128 + 128],
                        s_wv[:, kc * 512:(kc + 1) * 512],
                        start=(kc == 0), stop=False))
                tm = p.op("tensor", lambda e, pb=pb: e.matmul(
                    pb[:], ones_row_b[0:1, 0:128], s_bias[0:1, 4 * 512:4 * 512 + 512],
                    start=False, stop=True), inc=True)
                p.wait("scalar", tm)
                tstg = p.op("scalar", lambda e, pb=pb: e.copy(rsc1[:], pb[:]), dep=True)
                p.wait("vector", tstg)
                t_ev = p.op("vector", lambda e, ck=ck, vv=vview: e.tensor_copy(
                    vv[:, ck, :, 0:64], rsc1[:].rearrange("p (h ee) -> p h ee", h=8)), dep=True)
        ctx2.close()
        sqa = sb("sqaC", [64, 512]); sqb = sb("sqbC", [64, 512])
        mrow = f32t("mrowC", [1, 256])
        mred = f32t("mredC", [1, 32])

        p.wait("scalar", t_ev)
        p.wait("tensor", t_onesf)
        t_mm2 = [None, None]
        t_rd = [None, None]
        i = 0
        for side, src in ((0, qfm), (1, kfm)):
            for hq in range(8):
                hp, par = hq // 2, hq % 2
                r = 64 * par
                for qg in range(8):
                    parb = i % 2
                    sq = sqa if parb == 0 else sqb
                    pbb = PB if parb == 0 else PO
                    p.wait("scalar", t_mm2[parb])
                    tsq = p.op("scalar", lambda e, sq=sq, src=src, r=r, hp=hp, qg=qg: e.activation(
                        sq[0:64, 0:512],
                        src[r:r + 64, hp * T + qg * 512:hp * T + (qg + 1) * 512],
                        AF.Square), dep=True)
                    p.wait("tensor", tsq)
                    p.wait("tensor", t_rd[parb])
                    tmm = p.op("tensor", lambda e, sq=sq, pbb=pbb: e.matmul(
                        pbb[0:1, 0:512], ones_b[0:64, 0:1], sq[0:64, 0:512],
                        start=True, stop=True), inc=True)
                    t_mm2[parb] = tmm
                    p.wait("vector", tmm)
                    trd = p.op("vector", lambda e, pbb=pbb, side=side, hq=hq, qg=qg: e.reduce_max(
                        mrow[0:1, side * 128 + hq * 8 + qg:side * 128 + hq * 8 + qg + 1],
                        pbb[0:1, 0:512], axis=AX.X), dep=True)
                    t_rd[parb] = trd
                    i += 1
        for side in range(2):
            for hq in range(8):
                p.op("vector", lambda e, side=side, hq=hq: e.reduce_max(
                    mred[0:1, side * 8 + hq:side * 8 + hq + 1],
                    mrow[0:1, side * 128 + hq * 8:side * 128 + hq * 8 + 8],
                    axis=AX.X), dep=True)
        tm2 = p.op("vector", lambda e: e.tensor_tensor(
            mred[0:1, 16:24], mred[0:1, 0:8], mred[0:1, 8:16], ALU.mult), dep=True)
        p.wait("scalar", tm2)
        tsr = p.op("scalar", lambda e: e.activation(
            mred[0:1, 16:24], mred[0:1, 16:24], AF.Sqrt), dep=True)
        p.wait("vector", tsr)
        tnb = p.op("vector", lambda e: e.tensor_scalar(
            mred[0:1, 24:32], mred[0:1, 16:24], -0.125, 60.0, ALU.mult, ALU.add), dep=True)
        tnb = p.op("vector", lambda e: e.tensor_scalar(
            mred[0:1, 24:32], mred[0:1, 24:32], 0.0, None, ALU.min), dep=True)
        p.wait("tensor", tnb)
        t_nc = p.op("tensor", lambda e: e.matmul(PY[:, 16:24], onesf[0:1, 0:128],
                                                 mred[0:1, 24:32], start=True, stop=True), inc=True)
        p.wait("vector", t_nc)
        t_negc = p.op("vector", lambda e: e.tensor_copy(negc[:], PY[:, 16:24]), dep=True)
        p.wait("scalar", t_negc)
        p.wait("tensor", t_rd[0])
        p.wait("tensor", t_rd[1])

        t_scal0 = p.tok("scalar")
        pt_tok = [None, None]
        ps_free = [t_scal0, t_scal0]
        av_tok = [None, None]

        def emit_head_c2(hq, t_o_dma):
            hp, par = hq // 2, hq % 2
            r = 64 * par
            qcol = hp * T
            vview = vsb[:].rearrange("p (c h ee) -> p c h ee", c=32, h=8)

            def vrow(s):
                return vview[:, s, hq, :]

            t_po_done = None
            t_bsb = None
            for qg in range(8):
                nkb = 4 * qg + 4

                def emit_S(kb):
                    par = kb % 2
                    psb = PS if par == 0 else PX
                    p.wait("tensor", ps_free[par])
                    ts = p.op("tensor", lambda e, kb=kb, qg=qg, psb=psb: e.matmul(
                        psb[:], kfm[r:r + 64, qcol + kb * 128:qcol + (kb + 1) * 128],
                        qfm[r:r + 64, qcol + qg * 512:qcol + (qg + 1) * 512],
                        start=True, stop=True), inc=True)
                    p.wait("scalar", ts)
                    p.wait("scalar", av_tok[par])
                    ptwb = ptw if par == 0 else ptw2
                    te = p.op("scalar", lambda e, psb=psb, ptwb=ptwb, hq=hq: e.activation(
                        ptwb[:], psb[:], AF.Exp, bias=negc[:, hq:hq + 1], scale=0.125), dep=True)
                    ps_free[par] = te
                    tp = te
                    j = kb - 4 * qg
                    if j >= 0:
                        p.wait("vector", te)
                        tp = p.op("vector", lambda e, j=j, ptwb=ptwb: e.tensor_tensor(
                            ptwb[:], ptwb[:], s_msk[:, j * 512:(j + 1) * 512], ALU.mult), dep=True)
                    pt_tok[par] = tp

                def emit_AV(kb):
                    par = kb % 2
                    ptwb = ptw if par == 0 else ptw2
                    p.wait("tensor", pt_tok[par])
                    if kb == 0 and t_po_done is not None:
                        p.wait("tensor", t_po_done)
                    tavx = p.op("tensor", lambda e, kb=kb, ptwb=ptwb: e.matmul(
                        PO[0:65, 0:512], vrow(kb), ptwb[:],
                        start=(kb == 0), stop=(kb == nkb - 1)), inc=True)
                    av_tok[par] = tavx
                    return tavx

                emit_S(0)
                for kb in range(1, nkb):
                    emit_S(kb)
                    tav = emit_AV(kb - 1)
                tav = emit_AV(nkb - 1)
                p.wait("vector", tav)
                trz = p.op("vector", lambda e: e.reciprocal(
                    rzb[64:65, 0:512], PO[64:65, 0:512]), dep=True)
                p.wait("tensor", trz)
                p.wait("tensor", t_bsb)
                tbm = p.op("tensor", lambda e: e.matmul(
                    PB[0:65, 0:512], onesf[64:65, 0:65], rzb[64:65, 0:512],
                    start=True, stop=True), inc=True)
                p.wait("scalar", tbm)
                tbe = p.op("scalar", lambda e: e.copy(bsb[0:65, 0:512], PB[0:65, 0:512]), dep=True)
                t_bsb = tbe
                p.wait("vector", tbe)
                t_po_done = p.op("vector", lambda e, qg=qg: e.tensor_tensor(
                    osb[0:64, qg * 512:(qg + 1) * 512], bsb[0:64, 0:512],
                    PO[0:64, 0:512], ALU.mult), dep=True)
            p.wait("sync", t_po_done)
            t_o_dma = p.dma(lambda e, hq=hq: e.dma_start(out=o_dst(hq), in_=osb[0:64, :]))
            p.wait("vector", t_o_dma)
            p.wait("tensor", t_o_dma)
            return t_o_dma

        t_o_dma = None
        for hq in range(HPC):
            t_o_dma = emit_head_c2(hq, t_o_dma)
    return t_o_dma


def build_c2():
    nc = bass.Bass("TRN2", target_bir_lowering=False, debug=False)
    lnf = nc.dram_tensor("lnf", [D, T], BF16, kind="ExternalInput").ap()
    lnr = lnf.rearrange("(kc p) t -> p kc t", p=128)
    o_att = nc.dram_tensor("o_att", [HPC, HD, T], BF16, kind="ExternalOutput").ap()
    io = {}
    for nmm, shp, dt in (("wq", [D, 512], F32), ("wqs", [D, 512], F32), ("wk", [D, 512], F32),
                         ("wks", [D, 512], F32), ("wv", [D, 512], F32),
                         ("bq", [1, 512], BF16), ("bqs", [1, 512], BF16), ("bk", [1, 512], BF16),
                         ("bks", [1, 512], BF16), ("bv", [1, 512], BF16),
                         ("cosf", [128, T], BF16), ("sins", [128, T], BF16),
                         ("msk", [128, 4 * 512], BF16)):
        io[nmm] = nc.dram_tensor(nmm, shp, dt, kind="ExternalInput").ap()
    io["ln_ld"] = lambda kc, tt: lnr[:, kc, tt * 512:(tt + 1) * 512]
    io["o_dst"] = lambda hq: o_att[hq]
    with ExitStack() as ctx:
        p = Plan()
        emit_c2(nc, p, ctx, io)
        p.wait("sync", p.tok("sync"))
        p.emit(nc, ctx)
    return nc


def _rope_tables():
    inv_freq = 1.0 / 10000.0 ** (np.arange(0, HD, 2, dtype=np.float64) / HD)
    t = np.arange(T, dtype=np.float64)
    fr = t[:, None] * inv_freq[None, :]
    rope = np.concatenate([np.sin(fr), np.cos(fr)], axis=1)
    sin = rope[:, 0::2].T
    cos = rope[:, 1::2].T
    cosf = np.concatenate([cos, cos, cos, cos], axis=0)
    sins = np.concatenate([-sin, sin, -sin, sin], axis=0)
    return cosf.astype(np.float32), sins.astype(np.float32)


def _c2_perm():
    ev = np.arange(0, 64, 2); od = np.arange(1, 64, 2)
    base = np.concatenate([ev, od, 64 + ev, 64 + od])
    swap = np.concatenate([od, ev, 64 + od, 64 + ev])
    return base, swap


def stage_c2(inz, ln2_shards):
    nc = _get("c2", build_c2)
    qkv_w = np.asarray(inz["L1_attn_qkv_w"])
    qkv_b = np.asarray(inz["L1_attn_qkv_b"])
    base, swap = _c2_perm()
    cosf, sins = _rope_tables()
    kk = np.arange(128)[:, None]
    msk = np.zeros((128, 4 * 512), np.float32)
    for j in range(4):
        for s4 in range(4):
            qq = np.arange(128)[None, :]
            if j < s4:
                blk = np.ones((128, 128), np.float32)
            elif j == s4:
                blk = (kk <= qq).astype(np.float32)
            else:
                blk = np.zeros((128, 128), np.float32)
            msk[:, j * 512 + s4 * 128:j * 512 + (s4 + 1) * 128] = blk
    in_maps = []
    lnfull = [np.concatenate([np.asarray(ln2_shards[2 * b]), np.asarray(ln2_shards[2 * b + 1])],
                             axis=0) for b in range(B)]
    for c in range(NCORES):
        b, ho = c // 2, (c % 2) * 8
        qs, ks, vs = [], [], []
        qbs, kbs, vbs = [], [], []
        qss, kss = [], []
        qsb, ksb = [], []
        for g in range(4):
            cols0 = (ho + 2 * g) * HD
            idx = cols0 + base
            idxs = cols0 + swap
            qs.append(qkv_w[:, idx]); qss.append(qkv_w[:, idxs])
            ks.append(qkv_w[:, D + idx]); kss.append(qkv_w[:, D + idxs])
            vs.append(qkv_w[:, 2 * D + cols0:2 * D + cols0 + 128])
            qbs.append(qkv_b[idx]); qsb.append(qkv_b[idxs])
            kbs.append(qkv_b[D + idx]); ksb.append(qkv_b[D + idxs])
            vbs.append(qkv_b[2 * D + cols0:2 * D + cols0 + 128])
        in_maps.append(dict(
            lnf=_bf(lnfull[b].astype(np.float32).T),
            wq=_f32(np.concatenate(qs, 1)), wqs=_f32(np.concatenate(qss, 1)),
            wk=_f32(np.concatenate(ks, 1)), wks=_f32(np.concatenate(kss, 1)),
            wv=_f32(np.concatenate(vs, 1)),
            bq=_bf(np.concatenate(qbs).reshape(1, -1)), bqs=_bf(np.concatenate(qsb).reshape(1, -1)),
            bk=_bf(np.concatenate(kbs).reshape(1, -1)), bks=_bf(np.concatenate(ksb).reshape(1, -1)),
            bv=_bf(np.concatenate(vbs).reshape(1, -1)),
            cosf=_bf(cosf), sins=_bf(sins), msk=_bf(msk)))
    res = run_bass_kernel_spmd(nc, in_maps, core_ids=list(range(NCORES)))
    return [r["o_att"] for r in res.results]


def emit_c3(nc, p, ctx, io):
    outw = io["outw"]; outb = io["outb"]; x2 = io["x2"]; o_x3 = io["x3"]
    ln3f_d = io["ln3f"]; o_ld = io["o_ld"]; ident = io["ident"]
    if True:
        s_o = ctx.enter_context(nc.sbuf_tensor("s_oC3", [128, 8 * TH], BF16))
        s_w = ctx.enter_context(nc.sbuf_tensor("s_wC3", [128, 8 * D], BF16))
        s_b = ctx.enter_context(nc.sbuf_tensor("s_bC3", [1, D], BF16))
        ones_r = ctx.enter_context(nc.sbuf_tensor("ones_rC3", [1, 128], BF16))
        s_x2 = ctx.enter_context(nc.sbuf_tensor("s_x2C3", [128, D], F32))
        s_o1 = ctx.enter_context(nc.sbuf_tensor("s_o1C3", [128, D], F32))
        s_x3 = ctx.enter_context(nc.sbuf_tensor("s_x3C3", [128, D], F32))
        s_l3 = ctx.enter_context(nc.sbuf_tensor("s_l3C3", [128, D], BF16))
        sqd = ctx.enter_context(nc.sbuf_tensor("sqdC3", [128, D], F32))
        scr = ctx.enter_context(nc.sbuf_tensor("scrC3", [128, 8], F32))
        P0 = ctx.enter_context(nc.psum_tensor("P0c3", [128, 512], F32))
        P1 = ctx.enter_context(nc.psum_tensor("P1c3", [128, 512], F32))
        wrr = outw.rearrange("(kc p) d -> p kc d", p=128)
        x2r = x2.rearrange("(n p) d -> n p d", p=128)
        x3r = o_x3.rearrange("(n p) d -> n p d", p=128)
        l3fr = ln3f_d.rearrange("(kc p) t -> p kc t", p=128)
        s_idc = ctx.enter_context(nc.sbuf_tensor("s_idC3", [128, 128], BF16))
        l3stg = ctx.enter_context(nc.sbuf_tensor("l3stgC3", [128, 1024], BF16))
        PTc = ctx.enter_context(nc.psum_tensor("PTc3", [128, 128], BF16))

        for kc in range(8):
            tld = p.dma(lambda e, kc=kc: e.dma_start(
                out=s_o[:, kc * TH:(kc + 1) * TH], in_=o_ld(kc)))
            tld = p.op("gpsimd", lambda e, kc=kc: e.dma_start(
                out=s_w[:, kc * D:(kc + 1) * D], in_=wrr[:, kc, :]), dep=False, amount=16)
        tld2 = p.dma(lambda e: e.dma_start(out=s_b[:], in_=outb))
        tld2 = p.dma(lambda e: e.dma_start(out=s_idc[:], in_=ident))
        tmo = p.op("vector", lambda e: e.memset(ones_r[:], 1.0), dep=False)
        p.wait("tensor", p.tok("gpsimd"))
        p.wait("tensor", p.tok("sync"))
        p.wait("tensor", tmo)
        t_ev = None
        t_st = None
        tx = None
        for i in range(NT):
            p.wait("sync", tx)
            tl2 = p.dma(lambda e, i=i: e.dma_start(out=s_x2[:], in_=x2r[i]), dep=True)
            if t_st is not None:
                p.wait("sync", t_st)
            for half in range(2):
                pb = P0 if half == 0 else P1
                if t_ev is not None:
                    p.wait("tensor", t_ev)
                for kc in range(8):
                    p.op("tensor", lambda e, pb=pb, kc=kc, i=i, half=half: e.matmul(
                        pb[:], s_o[:, kc * TH + i * 128:kc * TH + (i + 1) * 128],
                        s_w[:, kc * D + half * 512:kc * D + (half + 1) * 512],
                        start=(kc == 0), stop=False))
                tm = p.op("tensor", lambda e, pb=pb, half=half: e.matmul(
                    pb[:], ones_r[0:1, :], s_b[0:1, half * 512:(half + 1) * 512],
                    start=False, stop=True), inc=True)
                p.wait("vector", tm)
                t_ev = p.op("vector", lambda e, pb=pb, half=half: e.tensor_copy(
                    s_o1[:, half * 512:(half + 1) * 512], pb[:]), dep=True)
            tb = plan_ln_scalars(p, s_o1[:], D, scr, sqd[:], 1e-5, dep_tok=t_ev)
            p.wait("scalar", tb)
            p.wait("scalar", t_st)
            ta = p.op("scalar", lambda e: e.activation(
                s_x3[:], s_o1[:], AF.Identity, bias=scr[:, 7:8], scale=scr[:, 6:7]), dep=True)
            p.wait("vector", ta)
            p.wait("vector", tl2)
            tx = p.op("vector", lambda e: e.tensor_add(s_x3[:], s_x3[:], s_x2[:]), dep=True)
            p.wait("sync", tx)
            t_st = p.dma(lambda e, i=i: e.dma_start(out=x3r[i], in_=s_x3[:]), dep=True)
            tb2 = plan_ln_scalars(p, s_x3[:], D, scr, sqd[:], 1e-5, dep_tok=tx)
            p.wait("scalar", tb2)
            tl3 = p.op("scalar", lambda e: e.activation(
                s_l3[:], s_x3[:], AF.Identity, bias=scr[:, 7:8], scale=scr[:, 6:7]), dep=True)
            p.wait("tensor", tl3)
            if i > 0:
                p.wait("tensor", t_l3dma)
            for kc in range(8):
                tt2 = p.op("tensor", lambda e, kc=kc: e.transpose(
                    PTc[:], s_l3[:, kc * 128:(kc + 1) * 128], s_idc[:]), inc=True)
                p.wait("scalar", tt2)
                tl3 = p.op("scalar", lambda e, kc=kc: e.copy(
                    l3stg[:, kc * 128:(kc + 1) * 128], PTc[:]), dep=True)
                p.wait("tensor", tl3)
            p.wait("sync", tl3)
            t_l3dma = p.dma(lambda e, i=i: e.dma_start(
                out=l3fr[:, :, i * 128:(i + 1) * 128],
                in_=l3stg[:].rearrange("p (kc t) -> p kc t", kc=8)), dep=False)
            tx = tl3
    return t_l3dma


def build_c3():
    nc = bass.Bass("TRN2", target_bir_lowering=False, debug=False)
    ofm = nc.dram_tensor("ofm", [D, TH], BF16, kind="ExternalInput").ap()
    orr = ofm.rearrange("(kc p) t -> p kc t", p=128)
    io = dict(
        outw=nc.dram_tensor("outw", [D, D], F32, kind="ExternalInput").ap(),
        outb=nc.dram_tensor("outb", [1, D], BF16, kind="ExternalInput").ap(),
        ident=nc.dram_tensor("ident", [128, 128], BF16, kind="ExternalInput").ap(),
        x2=nc.dram_tensor("x2", [TH, D], F32, kind="ExternalInput").ap(),
        x3=nc.dram_tensor("o_x3", [TH, D], F32, kind="ExternalOutput").ap(),
        ln3f=nc.dram_tensor("o_ln3f", [D, TH], BF16, kind="ExternalOutput").ap(),
        o_ld=lambda kc: orr[:, kc, :],
    )
    with ExitStack() as ctx:
        p = Plan()
        emit_c3(nc, p, ctx, io)
        p.wait("sync", p.tok("sync"))
        p.emit(nc, ctx)
    return nc


def stage_c3(inz, att_s, x2_shards):
    nc = _get("c3", build_c3)
    idb = np.eye(128, dtype=np.float32)
    in_maps = []
    for c in range(NCORES):
        b, half = c // 2, c % 2
        of = np.concatenate([np.asarray(att_s[2 * b]).reshape(512, T),
                             np.asarray(att_s[2 * b + 1]).reshape(512, T)], axis=0)
        in_maps.append(dict(
            ofm=_bf(of[:, half * TH:(half + 1) * TH]),
            outw=_f32(inz["L1_attn_out_w"]),
            outb=_bf(np.asarray(inz["L1_attn_out_b"]).reshape(1, -1)),
            ident=_bf(idb),
            x2=_f32(x2_shards[c])))
    res = run_bass_kernel_spmd(nc, in_maps, core_ids=list(range(NCORES)))
    return ([r["o_x3"] for r in res.results], [r["o_ln3f"] for r in res.results])



def emit_d(nc, p, ctx, io):
    rout_d = io["rout"]; wg_d = io["wg"]; wu_d = io["wu"]; wd_d = io["wd"]
    sg_d = io["sg"]; su_d = io["su"]; sd_d = io["sd"]; sel8_d = io["sel8"]
    identf = io["identf"]; x3_d = io["x3"]; o_x4 = io["x4"]; xf_ld = io["xf_ld"]
    if True:
        sb = lambda n, s, dt=BF16: ctx.enter_context(nc.sbuf_tensor(n, s, dt))
        f32t = lambda n, s: ctx.enter_context(nc.sbuf_tensor(n, s, F32))
        xf = sb("xfD", [128, 8 * TH])
        acc = f32t("accD", [128, NT * D])
        s_rout = sb("s_routD", [128, 8 * 8])
        s_sel = sb("s_selD", [8, 8 * 128])
        s_idf = f32t("s_idfD", [128, 128])
        cf8 = sb("cf8D", [8, NT * 128])
        prb = f32t("prbD", [128, 8]); m8 = f32t("m8D", [128, 8])
        scr = f32t("scrD", [128, 16])
        sqd = f32t("sqdD", [128, D])
        wgA = sb("wgAD", [128, 8 * 512]); wgB = sb("wgBD", [128, 8 * 512])
        wuA = sb("wuAD", [128, 8 * 512]); wuB = sb("wuBD", [128, 8 * 512])
        wdA = sb("wdAD", [128, 4 * D]); wdB = sb("wdBD", [128, 4 * D])
        wBp = [(wgA, wuA, wdA), (wgB, wuB, wdB)]
        hfm = sb("hfmD", [128, 8 * 512])
        gstg = sb("gstgD", [128, 512])
        crep = sb("crepD", [128, 512])
        ones_b = sb("ones_bD", [128, 128])
        s_x3 = f32t("s_x3D", [128, D])
        x4t = f32t("x4tD", [128, D])
        PG = ctx.enter_context(nc.psum_tensor("PGd", [128, 512], F32))
        PU = ctx.enter_context(nc.psum_tensor("PUd", [128, 512], F32))
        PD0 = ctx.enter_context(nc.psum_tensor("PD0d", [128, 512], F32))
        PD1 = ctx.enter_context(nc.psum_tensor("PD1d", [128, 512], F32))
        PC = ctx.enter_context(nc.psum_tensor("PCd", [128, 512], F32))
        x3r = x3_d.rearrange("(n p) d -> n p d", p=128)
        x4r = o_x4.rearrange("(n p) d -> n p d", p=128)

        for kc in range(8):
            t_x = p.dma(lambda e, kc=kc: e.dma_start(
                out=xf[:, kc * TH:(kc + 1) * TH], in_=xf_ld(kc)), dep=(kc == 0))
        t_r = p.op("gpsimd", lambda e: e.dma_start(
            out=s_rout[:].rearrange("p (kc c) -> p kc c", c=8),
            in_=rout_d.rearrange("(kc p) c -> p kc c", p=128)), dep=False, amount=16)
        t_s = p.dma(lambda e: e.dma_start(out=s_sel[:], in_=sel8_d))
        t_s = p.dma(lambda e: e.dma_start(out=s_idf[:], in_=identf))
        t_ones = p.op("vector", lambda e: e.memset(ones_b[:], 1.0), dep=False)

        p.wait("tensor", t_x)
        p.wait("tensor", t_r)
        p.wait("tensor", t_s)
        t_cf = None
        t_pcread = None
        for i in range(NT):
            p.wait("tensor", t_pcread)
            for kc in range(8):
                tm = p.op("tensor", lambda e, kc=kc, i=i: e.matmul(
                    PC[:, 0:8], xf[:, kc * TH + i * 128:kc * TH + (i + 1) * 128],
                    s_rout[:, kc * 8:(kc + 1) * 8],
                    start=(kc == 0), stop=(kc == 7)), inc=(kc == 7))
            p.wait("vector", tm)
            tv = p.op("vector", lambda e: e.tensor_copy(prb[:], PC[:, 0:8]), dep=True)
            t_pcread = tv
            tv = p.op("vector", lambda e: e.reduce_max(scr[:, 0:1], prb[:], axis=AX.X,
                                                       negate=True), dep=True)
            p.wait("scalar", tv)
            ts = p.op("scalar", lambda e: e.activation(
                prb[:], prb[:], AF.Exp, bias=scr[:, 0:1], scale=1.0,
                accum_out=scr[:, 1:2]), dep=True)
            p.wait("vector", ts)
            tv = p.op("vector", lambda e: e.reciprocal(scr[:, 2:3], scr[:, 1:2]), dep=True)
            tv = p.op("vector", lambda e: e.tensor_scalar(
                prb[:], prb[:], scr[:, 2:3], None, ALU.mult), dep=True)
            tv = p.op("vector", lambda e: e.max(out=m8[:], in_=prb[:]), dep=True)
            tcoef = p.op("vector", lambda e: e.scalar_tensor_tensor(
                prb[:], prb[:], m8[:, 1:2], prb[:], ALU.is_ge, ALU.mult), dep=True)
            p.wait("tensor", tcoef)
            tt = p.op("tensor", lambda e: e.transpose(PU[0:8, 0:128], prb[:], s_idf[:]),
                      inc=True)
            p.wait("scalar", tt)
            t_cf = p.op("scalar", lambda e, i=i: e.copy(
                cf8[0:8, i * 128:(i + 1) * 128], PU[0:8, 0:128]), dep=True)
            p.wait("tensor", t_cf)
        t_acc = None
        t_crep_read = t_pcread
        t_sil = None
        t_lastdown = None
        t_wl = [None, None]
        t_use = [None, None]

        def issue_loads(pp):
            if pp >= 10:
                return
            parw = pp % 2
            wg_sb, wu_sb, wd_sb = wBp[parw]
            p.wait("gpsimd", t_use[parw])
            if pp >= 8:
                h = pp - 8
                p.op("gpsimd", lambda e, h=h, a=wg_sb: e.dma_start(
                    out=a[:].rearrange("p (kc c) -> p kc c", c=512),
                    in_=sg_d[:, h * 512:(h + 1) * 512].rearrange(
                        "(kc p) c -> p kc c", p=128)), dep=True, amount=16)
                p.op("gpsimd", lambda e, h=h, a=wu_sb: e.dma_start(
                    out=a[:].rearrange("p (kc c) -> p kc c", c=512),
                    in_=su_d[:, h * 512:(h + 1) * 512].rearrange(
                        "(kc p) c -> p kc c", p=128)), dep=False, amount=16)
                tw = p.op("gpsimd", lambda e, h=h, a=wd_sb: e.dma_start(
                    out=a[:].rearrange("p (kc c) -> p kc c", c=D),
                    in_=sd_d[h * 512:(h + 1) * 512, :].rearrange(
                        "(kc p) c -> p kc c", p=128)), dep=False, amount=16)
            else:
                p.op("gpsimd", lambda e, pp=pp, a=wg_sb: e.dma_start(
                    out=a[:].rearrange("p (kc c) -> p kc c", c=512),
                    in_=wg_d[pp].rearrange("(kc p) c -> p kc c", p=128)),
                    dep=True, amount=16)
                p.op("gpsimd", lambda e, pp=pp, a=wu_sb: e.dma_start(
                    out=a[:].rearrange("p (kc c) -> p kc c", c=512),
                    in_=wu_d[pp].rearrange("(kc p) c -> p kc c", p=128)),
                    dep=False, amount=16)
                tw = p.op("gpsimd", lambda e, pp=pp, a=wd_sb: e.dma_start(
                    out=a[:].rearrange("p (kc c) -> p kc c", c=D),
                    in_=wd_d[pp].rearrange("(kc p) c -> p kc c", p=128)),
                    dep=False, amount=16)
            t_wl[parw] = tw

        issue_loads(0)
        for pp in range(10):
            shared = (pp >= 8)
            parw = pp % 2
            wg_sb, wu_sb, wd_sb = wBp[parw]
            issue_loads(pp + 1)
            p.wait("tensor", t_wl[parw])
            for tg in range(4):
                if not shared:
                    p.wait("tensor", t_cf)
                    p.wait("tensor", t_crep_read)
                    tc = p.op("tensor", lambda e, pp=pp, tg=tg: e.matmul(
                        PC[:], s_sel[0:8, pp * 128:(pp + 1) * 128],
                        cf8[0:8, tg * 512:(tg + 1) * 512], start=True, stop=True), inc=True)
                    p.wait("scalar", tc)
                    p.wait("scalar", t_lastdown)
                    tce = p.op("scalar", lambda e: e.copy(crep[:], PC[:]), dep=True)
                    t_crep_read = tce
                p.wait("vector", t_lastdown)
                t_h = None
                for hc in range(4):
                    p.wait("tensor", t_sil)
                    for kc in range(8):
                        p.op("tensor", lambda e, kc=kc, hc=hc, tg=tg, a=wg_sb: e.matmul(
                            PG[:], a[:, kc * 512 + hc * 128:kc * 512 + hc * 128 + 128],
                            xf[:, kc * TH + tg * 512:kc * TH + (tg + 1) * 512],
                            start=(kc == 0), stop=(kc == 7)), inc=(kc == 7))
                        tg2 = p.op("tensor", lambda e, kc=kc, hc=hc, tg=tg, a=wu_sb: e.matmul(
                            PU[:], a[:, kc * 512 + hc * 128:kc * 512 + hc * 128 + 128],
                            xf[:, kc * TH + tg * 512:kc * TH + (tg + 1) * 512],
                            start=(kc == 0), stop=(kc == 7)), inc=(kc == 7))
                    p.wait("scalar", tg2)
                    if t_h is not None:
                        p.wait("scalar", t_h)
                    tsil = p.op("scalar", lambda e: e.activation(
                        gstg[:], PG[:], AF.Silu), dep=True)
                    t_sil = tsil
                    p.wait("vector", tsil)
                    if not shared:
                        p.wait("vector", tce)
                        p.op("vector", lambda e: e.tensor_tensor(
                            gstg[:], gstg[:], PU[:], ALU.mult), dep=True)
                        t_h = p.op("vector", lambda e, hc=hc: e.tensor_tensor(
                            hfm[:, hc * 512:(hc + 1) * 512], gstg[:], crep[:],
                            ALU.mult), dep=True)
                    else:
                        t_h = p.op("vector", lambda e, hc=hc: e.tensor_tensor(
                            hfm[:, hc * 512:(hc + 1) * 512], gstg[:], PU[:],
                            ALU.mult), dep=True)
                    p.wait("tensor", t_h)
                for t4 in range(4):
                    it = tg * 4 + t4
                    for half in range(2):
                        pd = PD0 if half == 0 else PD1
                        if t_acc is not None:
                            p.wait("tensor", t_acc)
                        for kd in range(4):
                            tdm = p.op("tensor", lambda e, kd=kd, t4=t4, half=half, pd=pd, a=wd_sb: e.matmul(
                                pd[:], hfm[:, kd * 512 + t4 * 128:kd * 512 + t4 * 128 + 128],
                                a[:, kd * D + half * 512:kd * D + (half + 1) * 512],
                                start=(kd == 0), stop=(kd == 3)), inc=(kd == 3))
                        t_lastdown = tdm
                        p.wait("vector", tdm)
                        aslice = acc[:, it * D + half * 512: it * D + (half + 1) * 512]
                        if pp == 0:
                            t_acc = p.op("vector", lambda e, pd=pd, aslice=aslice: e.tensor_copy(
                                aslice, pd[:]), dep=True)
                        else:
                            t_acc = p.op("vector", lambda e, pd=pd, aslice=aslice: e.tensor_add(
                                aslice, aslice, pd[:]), dep=True)
            t_use[parw] = t_lastdown
        if "dbg_acc" in io:
            p.wait("sync", t_acc)
            p.dma(lambda e: e.dma_start(out=io["dbg_acc"], in_=acc[:]))
        t_st = None
        tx = None
        for i in range(NT):
            p.wait("sync", tx)
            tl = p.dma(lambda e, i=i: e.dma_start(out=s_x3[:], in_=x3r[i]), dep=True)
            asl = acc[:, i * D:(i + 1) * D]
            tb = plan_ln_scalars(p, asl, D, scr, sqd[:], 1e-5, dep_tok=t_acc)
            p.wait("scalar", tb)
            p.wait("scalar", t_st)
            ta = p.op("scalar", lambda e, asl=asl: e.activation(
                x4t[:], asl, AF.Identity, bias=scr[:, 7:8], scale=scr[:, 6:7]), dep=True)
            p.wait("vector", ta)
            p.wait("vector", tl)
            tx = p.op("vector", lambda e: e.tensor_add(x4t[:], x4t[:], s_x3[:]), dep=True)
            p.wait("sync", tx)
            if t_st is not None:
                p.wait("sync", t_st)
            t_st = p.dma(lambda e, i=i: e.dma_start(out=x4r[i], in_=x4t[:]), dep=True)
    return t_st


def build_d():
    nc = bass.Bass("TRN2", target_bir_lowering=False, debug=False)
    xf_d = nc.dram_tensor("xf", [D, TH], BF16, kind="ExternalInput").ap()
    xfr = xf_d.rearrange("(kc p) t -> p kc t", p=128)
    io = dict(
        rout=nc.dram_tensor("rout", [D, 8], F32, kind="ExternalInput").ap(),
        wg=nc.dram_tensor("wg", [8, D, 512], F32, kind="ExternalInput").ap(),
        wu=nc.dram_tensor("wu", [8, D, 512], F32, kind="ExternalInput").ap(),
        wd=nc.dram_tensor("wd", [8, 512, D], F32, kind="ExternalInput").ap(),
        sg=nc.dram_tensor("sg", [D, D], F32, kind="ExternalInput").ap(),
        su=nc.dram_tensor("su", [D, D], F32, kind="ExternalInput").ap(),
        sd=nc.dram_tensor("sd", [D, D], F32, kind="ExternalInput").ap(),
        sel8=nc.dram_tensor("sel8", [8, 8 * 128], BF16, kind="ExternalInput").ap(),
        identf=nc.dram_tensor("identf", [128, 128], F32, kind="ExternalInput").ap(),
        x3=nc.dram_tensor("x3", [TH, D], F32, kind="ExternalInput").ap(),
        x4=nc.dram_tensor("o_x4", [TH, D], F32, kind="ExternalOutput").ap(),
        xf_ld=lambda kc: xfr[:, kc, :],
    )
    with ExitStack() as ctx:
        p = Plan()
        emit_d(nc, p, ctx, io)
        p.wait("sync", p.tok("sync"))
        p.emit(nc, ctx)
    return nc


def stage_d(inz, ln3_shards, x3_shards):
    nc = _get("d", build_d)
    sel8 = np.zeros((8, 8 * 128), np.float32)
    for e in range(8):
        sel8[e, e * 128:(e + 1) * 128] = 1.0
    idf = np.eye(128, dtype=np.float32)
    in_maps = []
    for c in range(NCORES):
        in_maps.append(dict(
            xf=_bf(np.asarray(ln3_shards[c])),
            rout=_f32(inz["L1_ffn_router"]),
            wg=_f32(inz["L1_ffn_experts_gate"]), wu=_f32(inz["L1_ffn_experts_up"]),
            wd=_f32(inz["L1_ffn_experts_down"]),
            sg=_f32(inz["L1_ffn_shared_gate"]), su=_f32(inz["L1_ffn_shared_up"]),
            sd=_f32(inz["L1_ffn_shared_down"]),
            sel8=_bf(sel8), identf=_f32(idf), x3=_f32(x3_shards[c])))
    res = run_bass_kernel_spmd(nc, in_maps, core_ids=list(range(NCORES)))
    return [r["o_x4"] for r in res.results]



PAIRS = [[0, 1], [2, 3], [4, 5], [6, 7]]


class NCProxy:
    _n = 0

    def __init__(self, nc):
        self._nc = nc

    def sbuf_tensor(self, name, *a, **k):
        NCProxy._n += 1
        return self._nc.sbuf_tensor(f"{name}_u{NCProxy._n}", *a, **k)

    def psum_tensor(self, name, *a, **k):
        NCProxy._n += 1
        return self._nc.psum_tensor(f"{name}_u{NCProxy._n}", *a, **k)

    def __getattr__(self, k):
        return getattr(self._nc, k)


def _a2_consts(inz):
    iu, ju = np.triu_indices(FEAT)
    scl = np.where(iu == ju, 0.5, 1.0 / math.sqrt(2.0)).astype(np.float32)
    s_iu_m = np.zeros((64, NQUAD), np.float32)
    s_ju_m = np.zeros((64, NQUAD), np.float32)
    for m in range(NQUAD):
        s_iu_m[iu[m], m] = scl[m]
        s_ju_m[ju[m], m] = 1.0
    tri = (np.arange(128)[:, None] <= np.arange(128)[None, :]).astype(np.float32)
    ii = np.arange(128)[:, None]; jj = np.arange(128)[None, :]
    wm_prev = (ii >= jj + 64).astype(np.float32)
    wm_self = ((ii <= jj) & (ii >= jj - 64)).astype(np.float32)
    winm = np.concatenate([wm_prev, wm_self], axis=1)
    winm0 = np.concatenate([np.zeros_like(wm_prev), wm_self], axis=1)
    nw = np.asarray(inz["L0_attn_norm_w"]).reshape(8, 128).T
    wqf2 = np.concatenate([inz["L0_attn_Wqf"], inz["L0_attn_Wqf"]], axis=0)
    wkf2 = np.concatenate([inz["L0_attn_Wkf"], inz["L0_attn_Wkf"]], axis=0)
    return s_iu_m, s_ju_m, tri, winm, winm0, nw, wqf2, wkf2


def _c2_consts(inz, c):
    qkv_w = np.asarray(inz["L1_attn_qkv_w"]); qkv_b = np.asarray(inz["L1_attn_qkv_b"])
    base, swap = _c2_perm()
    cosf, sins = _rope_tables()
    kk = np.arange(128)[:, None]
    msk = np.zeros((128, 4 * 512), np.float32)
    for j in range(4):
        for s4 in range(4):
            qq = np.arange(128)[None, :]
            blk = (np.ones((128, 128), np.float32) if j < s4 else
                   ((kk <= qq).astype(np.float32) if j == s4 else np.zeros((128, 128), np.float32)))
            msk[:, j * 512 + s4 * 128:j * 512 + (s4 + 1) * 128] = blk
    ho = (c % 2) * 8
    qs, ks, vs, qbs, kbs, vbs, qss, kss, qsb, ksb = [], [], [], [], [], [], [], [], [], []
    for g in range(4):
        cols0 = (ho + 2 * g) * HD
        idx = cols0 + base; idxs = cols0 + swap
        qs.append(qkv_w[:, idx]); qss.append(qkv_w[:, idxs])
        ks.append(qkv_w[:, D + idx]); kss.append(qkv_w[:, D + idxs])
        vs.append(qkv_w[:, 2 * D + cols0:2 * D + cols0 + 128])
        qbs.append(qkv_b[idx]); qsb.append(qkv_b[idxs])
        kbs.append(qkv_b[D + idx]); ksb.append(qkv_b[D + idxs])
        vbs.append(qkv_b[2 * D + cols0:2 * D + cols0 + 128])
    return dict(
        c2_wq=_f32(np.concatenate(qs, 1)), c2_wqs=_f32(np.concatenate(qss, 1)),
        c2_wk=_f32(np.concatenate(ks, 1)), c2_wks=_f32(np.concatenate(kss, 1)),
        c2_wv=_f32(np.concatenate(vs, 1)),
        c2_bq=_bf(np.concatenate(qbs).reshape(1, -1)), c2_bqs=_bf(np.concatenate(qsb).reshape(1, -1)),
        c2_bk=_bf(np.concatenate(kbs).reshape(1, -1)), c2_bks=_bf(np.concatenate(ksb).reshape(1, -1)),
        c2_bv=_bf(np.concatenate(vbs).reshape(1, -1)),
        c2_cosf=_bf(cosf), c2_sins=_bf(sins), c2_msk=_bf(msk))


def build_fused():
    nc = bass.Bass("TRN2", target_bir_lowering=False, debug=False)
    dr = lambda n, s, dt, k: nc.dram_tensor(n, s, dt, kind=k).ap()
    x_in = dr("x", [TH, D], F32, "ExternalInput")
    ident = dr("ident", [128, 128], BF16, "ExternalInput")
    identf = dr("identf", [128, 128], F32, "ExternalInput")
    o_x4 = dr("o_x4", [TH, D], F32, "ExternalOutput")
    d_ln0 = dr("d_ln0", [TH, D], F32, "Internal")
    d_x1 = dr("d_x1", [TH, D], F32, "Internal")
    d_x2 = dr("d_x2", [TH, D], F32, "Internal")
    d_x3 = dr("d_x3", [TH, D], F32, "Internal")
    d_ln3f = dr("d_ln3f", [D, TH], BF16, "Internal")
    ag1_in = nc.dram_tensor("ag1_in", [D, TH], BF16, kind="Internal")
    ag1_out = nc.dram_tensor("ag1_out", [2 * D, TH], BF16, kind="Internal")
    ag2_in = nc.dram_tensor("ag2_in", [D, T], BF16, kind="Internal")
    ag2_out = nc.dram_tensor("ag2_out", [2 * D, T], BF16, kind="Internal")
    ag3_in = nc.dram_tensor("ag3_in", [D, TH], BF16, kind="Internal")
    ag3_out = nc.dram_tensor("ag3_out", [2 * D, TH], BF16, kind="Internal")
    ag4_in = nc.dram_tensor("ag4_in", [512, T], BF16, kind="Internal")
    ag4_out = nc.dram_tensor("ag4_out", [D, T], BF16, kind="Internal")

    ins = {}
    def mkin(pref, name, shp, dt):
        ins[pref + name] = dr(pref + name, shp, dt, "ExternalInput")
        return ins[pref + name]

    ncp = NCProxy(nc)
    with ExitStack() as ctx:
        p = Plan()
        pid = nc.partition_id(engines=[mybir.EngineType.SP])
        tok0 = (pid % 2) * TH

        io1 = dict(x=x_in, ident=ident, ln0=d_ln0, hf=ag1_in.ap())
        with ExitStack() as pc:
            t1 = emit_a1(ncp, p, pc, io1)
        p.wait("gpsimd", t1)
        cc1 = p.custom("gpsimd", lambda e: e.collective_compute(
            "AllGather", ALU.bypass, replica_groups=PAIRS,
            ins=[ag1_in.ap().opt()], outs=[ag1_out.ap().opt()]), "cc1")
        p.barrier()
        p.wait("sync", cc1)

        if FUSE_UPTO < 2:
            p.wait("sync", p.tok("sync"))
            p.emit(nc, ctx)
            return nc
        ag1o = ag1_out.ap()
        ag2i = ag2_in.ap()
        io2 = {}
        for nmm, shp, dt in (("wq", [D, 512], F32), ("wk", [D, 512], F32), ("wv", [D, 512], F32),
                             ("nw", [128, 8], F32), ("wqf2", [128, FEAT], BF16),
                             ("wkf2", [128, FEAT], BF16), ("s_iu", [64, NQUAD], BF16),
                             ("s_ju", [64, NQUAD], BF16), ("tri01", [128, 128], BF16),
                             ("winm", [128, 256], BF16), ("winm0", [128, 256], BF16)):
            io2[nmm] = mkin("a2_", nmm, shp, dt)
        io2["h_ld"] = lambda kc, r: ag1o[r * D + kc * 128: r * D + (kc + 1) * 128, :]
        io2["lin_dst"] = lambda hq: ag2i[hq * 64:(hq + 1) * 64, :]
        io2["win_dst"] = lambda hq: ag2i[512 + hq * 64:512 + (hq + 1) * 64, :]
        with ExitStack() as pc:
            t2 = emit_a2(ncp, p, pc, io2)
        p.wait("gpsimd", t2)
        cc2 = p.custom("gpsimd", lambda e: e.collective_compute(
            "AllGather", ALU.bypass, replica_groups=PAIRS,
            ins=[ag2_in.ap().opt()], outs=[ag2_out.ap().opt()]), "cc2")
        p.barrier()
        p.wait("sync", cc2)

        if FUSE_UPTO < 3:
            p.wait("sync", p.tok("sync"))
            p.emit(nc, ctx)
            return nc
        ag2o = ag2_out.ap()
        io3 = dict(
            wout=mkin("a3_", "wout", [2 * D, D], F32),
            ln0=d_ln0, x0=x_in, x1=d_x1,
            cat_ld=lambda kc: ag2o[kc * 128:(kc + 1) * 128, ds(tok0, TH)],
        )
        with ExitStack() as pc:
            emit_a3(ncp, p, pc, io3)
        p.barrier()

        if FUSE_UPTO < 4:
            p.wait("sync", p.tok("sync"))
            p.emit(nc, ctx)
            return nc
        iob = dict(
            x1=d_x1, x2=d_x2, ln2f=ag3_in.ap(),
            w1t=mkin("b_", "w1t", [D, 4096], F32),
            b1=mkin("b_", "b1", [1, 4096], F32),
            w2t=mkin("b_", "w2t", [4096, D], F32),
            b2=mkin("b_", "b2", [1, D], F32),
            ident=ident, identf=identf)
        if FUSED_DBG:
            iob["dbg_gs"] = dr("bd_gs", [1, 8], F32, "ExternalOutput")
            iob["dbg_wq"] = dr("bd_wq", [128, 4096], BF16, "ExternalOutput")
            iob["dbg_xq"] = dr("bd_xq", [128, D], BF16, "ExternalOutput")
            iob["dbg_h"] = dr("bd_h", [128, 4096], BF16, "ExternalOutput")
            iob["dbg_sc"] = dr("bd_sc", [128, 16], F32, "ExternalOutput")
        with ExitStack() as pc:
            t3 = emit_b(ncp, p, pc, iob)
        p.wait("gpsimd", t3)
        cc3 = p.custom("gpsimd", lambda e: e.collective_compute(
            "AllGather", ALU.bypass, replica_groups=PAIRS,
            ins=[ag3_in.ap().opt()], outs=[ag3_out.ap().opt()]), "cc3")
        p.barrier()
        p.wait("sync", cc3)

        if FUSE_UPTO < 5:
            if FUSED_DBG:
                dbg = {"g_ln0": d_ln0, "g_x1": d_x1, "g_x2": d_x2}
                for nmm, ap in dbg.items():
                    o = dr(nmm, [TH, D], F32, "ExternalOutput")
                    p.dma(lambda e, o=o, ap=ap: e.dma_start(out=o, in_=ap))
            p.wait("sync", p.tok("sync"))
            p.emit(nc, ctx)
            return nc
        ag3o = ag3_out.ap()
        ag4i = ag4_in.ap()
        ioc = {}
        for nmm, shp, dt in (("wq", [D, 512], F32), ("wqs", [D, 512], F32), ("wk", [D, 512], F32),
                             ("wks", [D, 512], F32), ("wv", [D, 512], F32),
                             ("bq", [1, 512], BF16), ("bqs", [1, 512], BF16), ("bk", [1, 512], BF16),
                             ("bks", [1, 512], BF16), ("bv", [1, 512], BF16),
                             ("cosf", [128, T], BF16), ("sins", [128, T], BF16),
                             ("msk", [128, 4 * 512], BF16)):
            ioc[nmm] = mkin("c2_", nmm, shp, dt)
        ioc["ln_ld"] = lambda kc, tt: ag3o[(tt // 4) * D + kc * 128:(tt // 4) * D + (kc + 1) * 128,
                                           (tt % 4) * 512:(tt % 4 + 1) * 512]
        ioc["o_dst"] = lambda hq: ag4i[hq * 64:(hq + 1) * 64, :]
        with ExitStack() as pc:
            t4 = emit_c2(ncp, p, pc, ioc)
        p.wait("gpsimd", t4)
        cc4 = p.custom("gpsimd", lambda e: e.collective_compute(
            "AllGather", ALU.bypass, replica_groups=PAIRS,
            ins=[ag4_in.ap().opt()], outs=[ag4_out.ap().opt()]), "cc4")
        p.barrier()
        p.wait("sync", cc4)

        if FUSE_UPTO < 6:
            p.wait("sync", p.tok("sync"))
            p.emit(nc, ctx)
            return nc
        ag4o = ag4_out.ap()
        ioc3 = dict(
            outw=mkin("c3_", "outw", [D, D], F32),
            outb=mkin("c3_", "outb", [1, D], BF16),
            ident=ident, x2=d_x2, x3=d_x3, ln3f=d_ln3f,
            o_ld=lambda kc: ag4o[kc * 128:(kc + 1) * 128, ds(tok0, TH)],
        )
        with ExitStack() as pc:
            emit_c3(ncp, p, pc, ioc3)
        p.barrier()

        if FUSE_UPTO < 7:
            p.wait("sync", p.tok("sync"))
            p.emit(nc, ctx)
            return nc
        d_ln3fr = d_ln3f.rearrange("(kc p) t -> p kc t", p=128)
        iod = dict(
            rout=mkin("d_", "rout", [D, 8], F32),
            wg=mkin("d_", "wg", [8, D, 512], F32),
            wu=mkin("d_", "wu", [8, D, 512], F32),
            wd=mkin("d_", "wd", [8, 512, D], F32),
            sg=mkin("d_", "sg", [D, D], F32),
            su=mkin("d_", "su", [D, D], F32),
            sd=mkin("d_", "sd", [D, D], F32),
            sel8=mkin("d_", "sel8", [8, 8 * 128], BF16),
            identf=identf, x3=d_x3, x4=o_x4,
            xf_ld=lambda kc: d_ln3fr[:, kc, :],
        )
        if FUSED_DBG:
            iod["dbg_acc"] = dr("gd_acc", [128, NT * D], F32, "ExternalOutput")
        with ExitStack() as pc:
            emit_d(ncp, p, pc, iod)
        if FUSED_DBG:
            o = dr("g_ln3f", [D, TH], BF16, "ExternalOutput")
            p.dma(lambda e, o=o: e.dma_start(out=o, in_=d_ln3f))
            dbg = {"g_ln0": d_ln0, "g_x1": d_x1, "g_x2": d_x2, "g_x3": d_x3}
            for nmm, ap in dbg.items():
                o = dr(nmm, [TH, D], F32, "ExternalOutput")
                p.dma(lambda e, o=o, ap=ap: e.dma_start(out=o, in_=ap))
            for nmm, t in (("g_ag1", ag1_out), ("g_ag2", ag2_out), ("g_ag3", ag3_out),
                           ("g_ag4", ag4_out)):
                shp = list(t.ap().shape)
                o = dr(nmm, shp, BF16, "ExternalOutput")
                p.dma(lambda e, o=o, t=t: e.dma_start(out=o, in_=t.ap()))
        p.wait("sync", p.tok("sync"))
        p.emit(nc, ctx)
    return nc



def build_only_b():
    nc = bass.Bass("TRN2", target_bir_lowering=False, debug=False)
    dr = lambda n, s, dt, k: nc.dram_tensor(n, s, dt, kind=k).ap()
    ncp = NCProxy(nc)
    import os
    use_proxy = os.environ.get("OB_PROXY", "1") == "1"
    close_pc = os.environ.get("OB_CLOSE", "1") == "1"
    with ExitStack() as ctx:
        p = Plan()
        iob = dict(
            x1=dr("xb1", [TH, D], F32, "ExternalInput"),
            x2=dr("o_x2", [TH, D], F32, "ExternalOutput"),
            ln2f=dr("o_ln2f", [D, TH], BF16, "ExternalOutput"),
            w1t=dr("b_w1t", [D, 4096], F32, "ExternalInput"),
            b1=dr("b_b1", [1, 4096], F32, "ExternalInput"),
            w2t=dr("b_w2t", [4096, D], F32, "ExternalInput"),
            b2=dr("b_b2", [1, D], F32, "ExternalInput"),
            ident=dr("ident", [128, 128], BF16, "ExternalInput"),
            identf=dr("identf", [128, 128], F32, "ExternalInput"),
            dbg_gs=dr("bd_gs", [1, 8], F32, "ExternalOutput"),
            dbg_wq=dr("bd_wq", [128, 4096], BF16, "ExternalOutput"),
            dbg_xq=dr("bd_xq", [128, D], BF16, "ExternalOutput"),
            dbg_h=dr("bd_h", [128, 4096], BF16, "ExternalOutput"),
            dbg_sc=dr("bd_sc", [128, 16], F32, "ExternalOutput"),
        )
        tgt = ncp if use_proxy else nc
        if close_pc:
            with ExitStack() as pc:
                emit_b(tgt, p, pc, iob)
        else:
            emit_b(tgt, p, ctx, iob)
        p.wait("sync", p.tok("sync"))
        p.emit(nc, ctx)
    return nc


def prep_fused(inz):
    s_iu_m, s_ju_m, tri, winm, winm0, nw, wqf2, wkf2 = _a2_consts(inz)
    idb = np.eye(128, dtype=np.float32)
    sel8 = np.zeros((8, 8 * 128), np.float32)
    for e in range(8):
        sel8[e, e * 128:(e + 1) * 128] = 1.0
    wq, wk, wv = inz["L0_attn_Wq"], inz["L0_attn_Wk"], inz["L0_attn_Wv"]
    wout = np.asarray(inz["L0_attn_Wout"])
    perm = np.concatenate([np.arange(0, 512), np.arange(1024, 1536),
                           np.arange(512, 1024), np.arange(1536, 2048)])
    wout_p = wout[perm]
    x0 = np.asarray(inz["L0_attn_x"])
    in_maps = []
    for c in range(NCORES):
        b, ho = c // 2, (c % 2) * 8
        cols = slice(ho * HD, (ho + 8) * HD)
        m = dict(
            x=_f32(x0[b, (c % 2) * TH:(c % 2 + 1) * TH]),
            ident=_bf(idb), identf=_f32(idb),
            a2_wq=_f32(np.asarray(wq)[:, cols]), a2_wk=_f32(np.asarray(wk)[:, cols]),
            a2_wv=_f32(np.asarray(wv)[:, cols]), a2_nw=_f32(nw),
            a2_wqf2=_bf(wqf2), a2_wkf2=_bf(wkf2), a2_s_iu=_bf(s_iu_m), a2_s_ju=_bf(s_ju_m),
            a2_tri01=_bf(tri), a2_winm=_bf(winm), a2_winm0=_bf(winm0),
            a3_wout=_f32(wout_p),
            b_w1t=_f32(np.asarray(inz["L0_ffn_w1"]).T),
            b_b1=_f32(np.asarray(inz["L0_ffn_b1"]).reshape(1, -1)),
            b_w2t=_f32(np.asarray(inz["L0_ffn_w2"]).T),
            b_b2=_f32(np.asarray(inz["L0_ffn_b2"]).reshape(1, -1)),
            c3_outw=_f32(inz["L1_attn_out_w"]),
            c3_outb=_bf(np.asarray(inz["L1_attn_out_b"]).reshape(1, -1)),
            d_rout=_f32(inz["L1_ffn_router"]),
            d_wg=_f32(inz["L1_ffn_experts_gate"]), d_wu=_f32(inz["L1_ffn_experts_up"]),
            d_wd=_f32(inz["L1_ffn_experts_down"]),
            d_sg=_f32(inz["L1_ffn_shared_gate"]), d_su=_f32(inz["L1_ffn_shared_up"]),
            d_sd=_f32(inz["L1_ffn_shared_down"]), d_sel8=_bf(sel8),
        )
        m.update(_c2_consts(inz, c))
        in_maps.append(m)
    return in_maps


FUSED = True
FUSED_DBG = False
FUSE_UPTO = 9


def kernel(**inputs):
    if FUSED:
        nc = _get("fused", build_fused)
        in_maps = prep_fused(inputs)
        res = run_bass_kernel_spmd(nc, in_maps, core_ids=list(range(NCORES)))
        return unshard_tokens([np.asarray(r["o_x4"], np.float32) for r in res.results])
    return kernel_unfused(**inputs)


def kernel_unfused(**inputs):
    x0 = np.asarray(inputs["L0_attn_x"])
    shards = shard_tokens(x0)
    ln_s, h_s = stage_a1(shards)
    lin_s, win_s = stage_a2(inputs, h_s)
    x1_s = stage_a3(inputs, lin_s, win_s, ln_s, shards)
    x2_s, ln2_s = stage_b(inputs, x1_s)
    att_s = stage_c2(inputs, ln2_s)
    x3_s, ln3_s = stage_c3(inputs, att_s, x2_s)
    x4_s = stage_d(inputs, ln3_s, x3_s)
    return unshard_tokens([np.asarray(s, np.float32) for s in x4_s])


if __name__ == "__main__":
    pass
